# version 03
# baseline (speedup 1.0000x reference)
import numpy as np
import ml_dtypes
from contextlib import ExitStack

import concourse.bass as bass
import concourse.mybir as mybir
from concourse.bass_utils import run_bass_kernel_spmd

F32 = mybir.dt.float32
BF16 = mybir.dt.bfloat16
U8 = mybir.dt.uint8
AF = mybir.ActivationFunctionType
ALU = mybir.AluOpType
AX = mybir.AxisListType

L, D = 2048, 1024
NH, HD, BLK, NB = 16, 64, 32, 64
NTC = 16
SCALE = 0.125
BIS_IT = 52
NEXP, DH = 16, 2048
BF = ml_dtypes.bfloat16
ENGINES = ["sync", "gpsimd", "tensor", "scalar", "vector"]

_TAPS = []


class Sem:
    __slots__ = ("h", "n")

    def __init__(self, nc, name):
        self.h = nc.alloc_semaphore(name=name)
        self.n = 0


class Prog:

    def __init__(self, nc):
        self.nc = nc
        self.esem = {}
        self.reset()

    def reset(self):
        self.q = {e: [] for e in ENGINES}

    def wait(self, eng, sem, n):
        if n and n > 0:
            self.q[eng].append(lambda e: e.wait_ge(sem.h, n))

    def run(self, eng, make, sem=None, k=1):
        es = self.esem.get(eng)
        if es is not None:
            self.wait(eng, es, es.n)
            if sem is None:
                sem, k = es, 1
        if sem is None:
            self.q[eng].append(lambda e: make(e))
            return None
        sem.n += k
        kk = k

        def f(e):
            make(e).then_inc(sem.h, kk)
        self.q[eng].append(f)
        return sem.n

    def emit(self, name):
        with self.nc.Block(name) as block:
            for eng in ENGINES:
                fns = self.q[eng]
                if not fns:
                    continue

                def body(e, fns=fns):
                    for f in fns:
                        f(e)
                getattr(block, eng)(body)
        self.reset()


class Ctx:
    pass


def _consts():
    c = {}
    c["identf"] = np.eye(128, dtype=np.float32)
    c["identb"] = np.eye(128, dtype=BF)
    c["onesb"] = np.ones((128, 128), dtype=BF)
    e = np.zeros((64, 2048), np.float32)
    for nb in range(64):
        e[nb, 32 * nb:32 * nb + 32] = 10000.0
    c["ebig"] = np.concatenate([e, e], 0).astype(BF)
    band = (np.abs(np.arange(NB)[:, None] - np.arange(NB)[None, :]) <= 1)
    c["band5k"] = (band * 5000.0).astype(np.float32)
    ab = np.zeros((2048, 64), np.float32)
    for t in range(2048):
        ab[t, t // 32] = 1.0 / 32.0
    c["abm"] = ab.astype(BF)
    i8 = np.zeros((128, 128), np.float32)
    for p in range(128):
        i8[p, (p // 8) * 8:(p // 8) * 8 + 8] = 1.0
    c["i8blk"] = i8
    return c


def _tap_phase(c, P, phase):
    first = True
    for ph, attr, outname in _TAPS:
        if ph != phase:
            continue
        if first:
            for sem in (c.sPE, c.sSC, c.sVE):
                P.wait("sync", sem, sem.n)
            first = False
        t = getattr(c, attr)
        out = c.nc.dram_tensor("tap_" + outname, list(t.shape), t.dtype,
                               kind="ExternalOutput").ap()
        full = t[tuple(slice(None) for _ in t.shape)]
        P.run("sync", lambda e, out=out, full=full: e.dma_start(out=out, in_=full),
              c.sST, 16)


def build_program():
    nc = bass.Bass("TRN2", target_bir_lowering=False)
    c = Ctx()
    c.nc = nc
    P = Prog(nc)

    def din(name, shape, dt=F32):
        return nc.dram_tensor(name, list(shape), dt, kind="ExternalInput").ap()

    c.x = din("x", (L, D))
    c.wqT = din("wqT", (D, D))
    c.wkT = din("wkT", (D, D))
    c.wvT = din("wvT", (D, D))
    c.bq_rs = din("bq_rs", (128, 8))
    c.bk_rs = din("bk_rs", (128, 8))
    c.bvrow = din("bvrow", (1, D))
    c.wg_aug = din("wg_aug", (D + 2, NEXP))
    c.w1 = din("w1", (NEXP, D, DH))
    c.b1r = din("b1r", (NEXP, 128, 16))
    c.w2 = din("w2", (NEXP, DH, D))
    c.b2 = din("b2", (NEXP, D))
    c.identf = din("identf", (128, 128))
    c.identb = din("identb", (128, 128), BF16)
    c.onesb = din("onesb", (128, 128), BF16)
    c.ebig = din("ebig", (128, 2048), BF16)
    c.band5k = din("band5k", (64, 64))
    c.abm = din("abm", (2048, 64), BF16)
    c.i8blk = din("i8blk", (128, 128))
    c.y = nc.dram_tensor("y", [L, D], F32, kind="ExternalOutput").ap()
    c.norm_scr = nc.dram_tensor("norm_scr", [NH, 4096], F32, kind="Internal").ap()
    c.m01_scr = nc.dram_tensor("m01_scr", [NH, 4096], F32, kind="Internal").ap()

    for s in ["sLoad", "sWt", "sX0", "sX1", "sWp0", "sWp1", "sG", "sB1",
              "sPE", "sSC", "sVE", "sST"]:
        setattr(c, s, Sem(nc, s))
    P.esem = {"vector": c.sVE, "scalar": c.sSC}

    with ExitStack() as outer:
        oc = outer.enter_context
        c.identf_sb = oc(nc.sbuf_tensor("identf_sb", [128, 128], F32))
        c.identb_sb = oc(nc.sbuf_tensor("identb_sb", [128, 128], BF16))
        c.onesb_sb = oc(nc.sbuf_tensor("onesb_sb", [128, 128], BF16))
        c.i8_sb = oc(nc.sbuf_tensor("i8_sb", [128, 128], F32))

        with ExitStack() as ab:
            abo = ab.enter_context
            c.QT = abo(nc.sbuf_tensor("QT", [128, 8, L], BF16))
            c.KsT = abo(nc.sbuf_tensor("KsT", [128, 8, L], BF16))
            c.Vaug = abo(nc.sbuf_tensor("Vaug", [128, NTC, NH, HD + 1], BF16))
            c.Vh_aug = abo(nc.sbuf_tensor("Vh_aug", [64, NH, HD + 1], BF16))
            c.band_sb = abo(nc.sbuf_tensor("band_sb", [64, 64], F32))
            c.norm_all = abo(nc.sbuf_tensor("norm_all", [64, NH, 64], F32))
            c.rm10k = abo(nc.sbuf_tensor("rm10k", [64, NH], F32))

            _phase_a(c, P)

            c.ACC = oc(nc.sbuf_tensor("ACC", [128, NTC * D], F32, side="right"))

            _phase_b(c, P, ab)

        with ExitStack() as cd:
            cdo = cd.enter_context
            c.x1T = cdo(nc.sbuf_tensor("x1T", [128, 8, L], BF16))
            c.w1b = [cdo(nc.sbuf_tensor(f"w1b{i}", [128, 8, 1024], BF16)) for i in range(2)]
            c.w2b = [cdo(nc.sbuf_tensor(f"w2b{i}", [128, 8, 1024], BF16)) for i in range(2)]
            c.b1_sb = cdo(nc.sbuf_tensor("b1_sb", [128, 32], F32))
            c.ht_sb = [cdo(nc.sbuf_tensor(f"ht{i}", [128, 256], BF16)) for i in range(2)]
            c.x0t2 = cdo(nc.sbuf_tensor("x0t2", [128, D], F32))
            c.sdump = cdo(nc.sbuf_tensor("sdump", [128, D], F32))
            c.stats = cdo(nc.sbuf_tensor("stats", [128, 32], F32))
            c.top8 = cdo(nc.sbuf_tensor("top8", [128, 8], F32))
            c.lg_sb = cdo(nc.sbuf_tensor("lg_sb", [128, 32], F32))
            c.t16 = cdo(nc.sbuf_tensor("t16", [128, 48], F32))
            c.wg_sb = cdo(nc.sbuf_tensor("wg_sb", [128, 144], BF16))
            c.b2a_sb = cdo(nc.sbuf_tensor("b2a_sb", [NEXP, D], BF16))
            c.cw_sb = cdo(nc.sbuf_tensor("cw_sb", [128, NTC * NEXP], F32))
            c.cwT_sb = cdo(nc.sbuf_tensor("cwT_sb", [NEXP, L], BF16))

            _phase_cde(c, P, cd)

    return nc, c


def _phase_a(c, P):
    nc = c.nc
    with ExitStack() as sc_a:
        ao = sc_a.enter_context
        c.x0T = x0T = ao(nc.sbuf_tensor("x0T", [128, 8, L], BF16))
        wt = ao(nc.sbuf_tensor("wt", [128, 8, 1024], BF16))
        x0b = [ao(nc.sbuf_tensor(f"x0b{i}", [128, D], BF16)) for i in range(2)]
        bvr_sb = ao(nc.sbuf_tensor("bvr_sb", [1, D], BF16))
        c.xh_sb = xh_sb = ao(nc.sbuf_tensor("xh_sb", [64, D], F32))
        abm_sb = ao(nc.sbuf_tensor("abm_sb", [128, NTC, 64], BF16))
        c.XhT = XhT = ao(nc.sbuf_tensor("XhT", [128, 8, 64], BF16))
        c.QhT = QhT = ao(nc.sbuf_tensor("QhT", [128, 8, 64], BF16))
        c.KhT = KhT = ao(nc.sbuf_tensor("KhT", [128, 8, 64], BF16))
        bq_sb = ao(nc.sbuf_tensor("bq_sb", [128, 8], F32))
        bks_sb = ao(nc.sbuf_tensor("bks_sb", [128, 8], F32))

        with ExitStack() as ps:
            po = ps.enter_context
            pk = [po(nc.psum_tensor(f"pA{i}", [128, 512], F32)) for i in range(2)]
            pb16 = [po(nc.psum_tensor(f"pAb{i}", [128, 1024], BF16)) for i in range(2)]
            pxh = [po(nc.psum_tensor(f"pXh{i}", [128, 512], F32)) for i in range(2)]
            plow = [po(nc.psum_tensor(f"plow{i}", [128, 512], F32)) for i in range(2)]

            S, PE, SC, VE = c.sLoad, c.sPE, c.sSC, c.sVE
            Wt, sX = c.sWt, [c.sX0, c.sX1]

            for dst, src in [
                (c.identf_sb[:, :], c.identf), (c.identb_sb[:, :], c.identb),
                (c.onesb_sb[:, :], c.onesb), (c.i8_sb[:, :], c.i8blk),
                (c.band_sb[:, :], c.band5k),
                (bq_sb[:, :], c.bq_rs), (bks_sb[:, :], c.bk_rs),
                (abm_sb[:, :, :], c.abm.rearrange("(tc p) n -> p tc n", p=128)),
            ]:
                P.run("sync", lambda e, d=dst, s=src: e.dma_start(out=d, in_=s), S, 16)
            ld_consts = S.n

            P.run("gpsimd", lambda e: e.dma_start(out=bvr_sb[:, :], in_=c.bvrow), Wt, 16)

            P.wait("vector", S, ld_consts)
            ve_bks = P.run("vector", lambda e: e.tensor_scalar_mul(
                bks_sb[:, :], bks_sb[:, :], SCALE), VE, 1)
            P.run("vector", lambda e: e.memset(c.Vaug[:, :, :, HD:HD + 1], 1.0), VE, 1)
            ve_init = P.run("vector", lambda e: e.memset(
                c.Vh_aug[0:64, :, HD:HD + 1], 32.0), VE, 1)

            ld_x0, pe_tile, sc_x0T = {}, {}, {}
            for tc in range(NTC):
                if tc >= 2:
                    P.wait("gpsimd", PE, pe_tile[tc - 2])
                ld_x0[tc] = P.run("gpsimd", lambda e, tc=tc: e.dma_start(
                    out=x0b[tc % 2][:, :], in_=c.x[128 * tc:128 * (tc + 1), :]),
                    sX[tc % 2], 16)

                P.wait("tensor", sX[tc % 2], ld_x0[tc])
                if tc == 0:
                    P.wait("tensor", S, ld_consts)
                for dd in range(2):
                    P.run("tensor", lambda e, tc=tc, dd=dd: e.matmul(
                        pxh[dd][0:64, :], abm_sb[:, tc, :],
                        x0b[tc % 2][:, 512 * dd:512 * (dd + 1)],
                        start=(tc == 0), stop=(tc == NTC - 1)))
                if tc >= 2:
                    P.wait("tensor", SC, sc_x0T[tc - 2])
                for dc in range(8):
                    pe_tile[tc] = P.run("tensor", lambda e, tc=tc, dc=dc: e.transpose(
                        pb16[tc % 2][:, 128 * dc:128 * (dc + 1)],
                        x0b[tc % 2][:, 128 * dc:128 * (dc + 1)],
                        c.identb_sb[:, :]), PE if dc == 7 else None, 1)

                P.wait("scalar", PE, pe_tile[tc])
                sc_x0T[tc] = P.run("scalar", lambda e, tc=tc: e.copy(
                    x0T[:, :, 128 * tc:128 * (tc + 1)],
                    pb16[tc % 2][:, :].rearrange("p (cc t) -> p cc t", cc=8)), SC, 1)

            ld_wq = P.run("gpsimd", lambda e: e.dma_start(
                out=wt[:, :, :], in_=c.wqT.rearrange("(cc p) d -> p cc d", p=128)), Wt, 16)

            P.wait("scalar", PE, pe_tile[NTC - 1])
            sc_xh = None
            for dd in range(2):
                sc_xh = P.run("scalar", lambda e, dd=dd: e.copy(
                    xh_sb[0:64, 512 * dd:512 * (dd + 1)], pxh[dd][0:64, :]), SC, 1)
            P.wait("tensor", SC, sc_xh)
            pe_xhT = None
            for dc in range(8):
                pe_xhT = P.run("tensor", lambda e, dc=dc: e.transpose(
                    pk[0][:, 64 * dc:64 * (dc + 1)],
                    xh_sb[0:64, 128 * dc:128 * (dc + 1)],
                    c.identf_sb[0:64, 0:64]), PE if dc == 7 else None, 1)
            P.wait("scalar", PE, pe_xhT)
            sc_xhT = P.run("scalar", lambda e: e.copy(
                XhT[:, :, :], pk[0][:, :].rearrange("p (cc n) -> p cc n", cc=8)), SC, 1)

            def proj_T(outT, bias_sb, scale, n_free, nt, w_ld, prev_sc):
                sc_evac = {}
                first = True
                for dc in range(8):
                    for tt in range(nt):
                        idx = dc * nt + tt
                        bank = pk[idx % 2]
                        if first:
                            P.wait("tensor", Wt, w_ld)
                            P.wait("tensor", SC, sc_xhT)
                            first = False
                        if idx >= 2:
                            P.wait("tensor", SC, sc_evac[idx - 2])
                        elif prev_sc:
                            P.wait("tensor", SC, prev_sc)
                        pe_g = None
                        for cc in range(8):
                            if n_free == 512:
                                def mk(e, dc=dc, tt=tt, cc=cc, bank=bank):
                                    return e.matmul(
                                        bank[:, 0:512],
                                        wt[:, cc, 128 * dc:128 * (dc + 1)],
                                        x0T[:, cc, 512 * tt:512 * (tt + 1)],
                                        start=(cc == 0), stop=(cc == 7))
                            else:
                                def mk(e, dc=dc, cc=cc, bank=bank):
                                    return e.matmul(
                                        bank[:, 0:64],
                                        wt[:, cc, 128 * dc:128 * (dc + 1)],
                                        XhT[:, cc, :],
                                        start=(cc == 0), stop=(cc == 7))
                            pe_g = P.run("tensor", mk, PE if cc == 7 else None, 1)
                        P.wait("scalar", PE, pe_g)
                        sc_evac[idx] = P.run("scalar",
                            lambda e, dc=dc, tt=tt, bank=bank: e.activation(
                                outT[:, dc, n_free * tt:n_free * (tt + 1)],
                                bank[:, 0:n_free], AF.Identity,
                                bias=bias_sb[:, dc:dc + 1], scale=scale), SC, 1)
                return sc_evac[8 * nt - 1]

            P.wait("scalar", VE, ve_bks)
            sc_q = proj_T(c.QT, bq_sb, 1.0, 512, 4, ld_wq, sc_xhT)
            sc_qh = proj_T(QhT, bq_sb, 1.0, 64, 1, ld_wq, sc_q)
            pe_after_q = PE.n

            P.wait("gpsimd", PE, pe_after_q)
            ld_wk = P.run("gpsimd", lambda e: e.dma_start(
                out=wt[:, :, :], in_=c.wkT.rearrange("(cc p) d -> p cc d", p=128)), Wt, 16)

            sc_k = proj_T(c.KsT, bks_sb, SCALE, 512, 4, ld_wk, sc_qh)
            sc_kh = proj_T(KhT, bks_sb, SCALE, 64, 1, ld_wk, sc_k)
            pe_after_k = PE.n

            P.wait("gpsimd", PE, pe_after_k)
            ld_wv = P.run("gpsimd", lambda e: e.dma_start(
                out=wt[:, :, :], in_=c.wvT.rearrange("(cc p) d -> p cc d", p=128)), Wt, 16)

            sc_v = {}
            idx = 0
            for tc in range(NTC):
                for dd in range(2):
                    bank = pk[idx % 2]
                    if idx == 0:
                        P.wait("tensor", Wt, ld_wv)
                        P.wait("tensor", VE, ve_init)
                    if idx >= 2:
                        P.wait("tensor", SC, sc_v[idx - 2])
                    elif idx < 2:
                        P.wait("tensor", SC, sc_kh)
                    for cc in range(8):
                        P.run("tensor", lambda e, tc=tc, dd=dd, cc=cc, bank=bank: e.matmul(
                            bank[:, :],
                            x0T[:, cc, 128 * tc:128 * (tc + 1)],
                            wt[:, cc, 512 * dd:512 * (dd + 1)],
                            start=(cc == 0), stop=False))
                    pe_g = P.run("tensor", lambda e, dd=dd, bank=bank: e.matmul(
                        bank[:, :], c.onesb_sb[0:1, 0:128],
                        bvr_sb[0:1, 512 * dd:512 * (dd + 1)],
                        start=False, stop=True), PE, 1)
                    P.wait("scalar", PE, pe_g)
                    sc_v[idx] = P.run("scalar", lambda e, tc=tc, dd=dd, bank=bank: e.copy(
                        c.Vaug[:, tc, 8 * dd:8 * (dd + 1), 0:HD],
                        bank[:, :].rearrange("p (h n) -> p h n", h=8)), SC, 1)
                    idx += 1

            sc_vh = {}
            for dd in range(2):
                bank = pk[dd % 2]
                P.wait("tensor", SC, sc_v[idx - 2 + dd])
                for cc in range(8):
                    P.run("tensor", lambda e, dd=dd, cc=cc, bank=bank: e.matmul(
                        bank[0:64, :], XhT[:, cc, :],
                        wt[:, cc, 512 * dd:512 * (dd + 1)],
                        start=(cc == 0), stop=False))
                pe_g = P.run("tensor", lambda e, dd=dd, bank=bank: e.matmul(
                    bank[0:64, :], c.onesb_sb[0:1, 0:64],
                    bvr_sb[0:1, 512 * dd:512 * (dd + 1)],
                    start=False, stop=True), PE, 1)
                P.wait("scalar", PE, pe_g)
                sc_vh[dd] = P.run("scalar", lambda e, dd=dd, bank=bank: e.activation(
                    c.Vh_aug[0:64, 8 * dd:8 * (dd + 1), 0:HD],
                    bank[0:64, :].rearrange("p (h n) -> p h n", h=8),
                    AF.Identity, scale=32.0), SC, 1)

            ve_low = {}
            for h in range(NH):
                b = 64 * (h % 2)
                hc = h // 2
                bank = plow[h % 2]
                if h == 0:
                    P.wait("tensor", SC, max(sc_qh, sc_kh))
                if h >= 2:
                    P.wait("tensor", VE, ve_low[h - 2])
                pe_l = P.run("tensor", lambda e, b=b, hc=hc, bank=bank: e.matmul(
                    bank[0:64, 0:64],
                    QhT[b:b + 64, hc, :], KhT[b:b + 64, hc, :],
                    start=True, stop=True), PE, 1)
                P.wait("vector", PE, pe_l)
                ve_rm = P.run("vector", lambda e, h=h, bank=bank: e.reduce_max(
                    c.rm10k[0:64, h:h + 1], bank[0:64, 0:64], axis=AX.X), VE, 1)
                P.wait("vector", VE, ve_rm)
                P.run("vector", lambda e, h=h, bank=bank: e.scalar_tensor_tensor(
                    c.norm_all[0:64, h, :], bank[0:64, 0:64],
                    c.rm10k[0:64, h:h + 1], c.band_sb[:, :],
                    ALU.subtract, ALU.add), VE, 1)
                ve_low[h] = P.run("vector", lambda e, h=h: e.tensor_scalar_add(
                    c.rm10k[0:64, h:h + 1], c.rm10k[0:64, h:h + 1], -10000.0), VE, 1)

            P.wait("sync", VE, ve_low[NH - 1])
            c.ld_norm_scr = P.run("sync", lambda e: e.dma_start(
                out=c.norm_scr.rearrange("h (q k) -> q h k", q=64),
                in_=c.norm_all[:, :, :]), S, 16)

            _tap_phase(c, P, "A")
            P.emit("phaseA")


def _phase_b(c, P, scope):
    nc = c.nc
    bo = scope.enter_context
    c.ebig_sb = bo(nc.sbuf_tensor("ebig_sb", [128, 2048], BF16))
    c.M01cT = bo(nc.sbuf_tensor("M01cT", [128, NH, 64], BF16))
    c.lowWT = bo(nc.sbuf_tensor("lowWT", [64, NH, 64], BF16))
    c.nfl = nfl = bo(nc.sbuf_tensor("nfl", [128, 512], F32))
    tmpf = bo(nc.sbuf_tensor("tmpf", [128, 512], F32))
    part = bo(nc.sbuf_tensor("part", [128, 1], F32))
    c.lo = lo = bo(nc.sbuf_tensor("lo", [128, 1], F32))
    hi = bo(nc.sbuf_tensor("hi", [128, 1], F32))
    mid = bo(nc.sbuf_tensor("mid", [128, 1], F32))
    mU = bo(nc.sbuf_tensor("mU", [128, 1], U8))
    mU2 = bo(nc.sbuf_tensor("mU2", [128, 1], U8))
    c.m01fl = m01fl = bo(nc.sbuf_tensor("m01fl", [128, 512], F32))
    c.m01c_sb = m01c = bo(nc.sbuf_tensor("m01c", [64, NH, 64], F32))
    t64 = bo(nc.sbuf_tensor("t64", [64, 64], F32))
    lwf = bo(nc.sbuf_tensor("lwf", [64, 64], F32))
    c.PT = PT = [bo(nc.sbuf_tensor(f"PT{i}", [128, 1024], BF16)) for i in range(2)]
    c.attnT_sb = attnT = bo(nc.sbuf_tensor("attnT", [65, 1024], F32))
    dden = bo(nc.sbuf_tensor("dden", [128, 2], F32))

    with ExitStack() as ps:
        po = ps.enter_context
        sb = [po(nc.psum_tensor(f"pS{i}", [128, 1024], F32)) for i in range(2)]
        av = [po(nc.psum_tensor(f"pV{i}", [128, 512], F32)) for i in range(4)]

        S, PE, SC, VE = c.sLoad, c.sPE, c.sSC, c.sVE

        P.wait("sync", c.sST, c.sST.n)
        P.wait("gpsimd", c.sST, c.sST.n)
        ld_ebig = P.run("sync", lambda e: e.dma_start(
            out=c.ebig_sb[:, :], in_=c.ebig), S, 16)

        P.wait("sync", S, max(c.ld_norm_scr, ld_ebig))
        ld_flat = P.run("sync", lambda e: e.dma_start(
            out=nfl[:, :], in_=c.norm_scr.rearrange("h (j f) -> (h j) f", j=8)), S, 16)

        P.wait("vector", S, ld_flat)
        P.run("vector", lambda e: e.memset(lo[:, :], -30.0))
        P.run("vector", lambda e: e.memset(hi[:, :], 1.0))
        P.run("vector", lambda e: e.memset(mid[:, :], -14.5), VE, 1)
        ve_upd = VE.n
        for it in range(BIS_IT):
            P.wait("vector", VE, ve_upd)
            P.run("vector", lambda e: e.tensor_scalar(
                tmpf[:, :], nfl[:, :], mid[:, 0:1], None, ALU.is_ge))
            ve_cmp = P.run("vector", lambda e: e.reduce_sum(
                part[:, :], tmpf[:, :], axis=AX.X), VE, 1)
            P.wait("tensor", VE, ve_cmp)
            if it > 0:
                P.wait("tensor", VE, ve_upd)
            pe_bis = P.run("tensor", lambda e: e.matmul(
                sb[0][:, 0:1], c.i8_sb[:, :], part[:, 0:1],
                start=True, stop=True), PE, 1)
            P.wait("vector", PE, pe_bis)
            P.run("vector", lambda e: e.tensor_scalar(
                mU[:, :], sb[0][:, 0:1], 1024.0, None, ALU.is_ge))
            P.run("vector", lambda e: e.tensor_scalar(
                mU2[:, :], sb[0][:, 0:1], 1024.0, None, ALU.is_lt))
            P.run("vector", lambda e: e.copy_predicated(lo[:, 0:1], mU[:, 0:1], mid[:, 0:1]))
            P.run("vector", lambda e: e.copy_predicated(hi[:, 0:1], mU2[:, 0:1], mid[:, 0:1]))
            P.run("vector", lambda e: e.tensor_add(mid[:, 0:1], lo[:, 0:1], hi[:, 0:1]))
            ve_upd = P.run("vector", lambda e: e.tensor_scalar_mul(
                mid[:, 0:1], mid[:, 0:1], 0.5), VE, 1)
        P.wait("vector", VE, ve_upd)
        ve_m01 = P.run("vector", lambda e: e.tensor_scalar(
            m01fl[:, :], nfl[:, :], lo[:, 0:1], 1.0, ALU.is_ge, ALU.subtract), VE, 1)

        P.wait("sync", VE, ve_m01)
        st_m = P.run("sync", lambda e: e.dma_start(
            out=c.m01_scr.rearrange("h (j f) -> (h j) f", j=8), in_=m01fl[:, :]), S, 16)
        P.wait("sync", S, st_m)
        ld_m01 = P.run("sync", lambda e: e.dma_start(
            out=m01c[:, :, :], in_=c.m01_scr.rearrange("h (q k) -> q h k", q=64)), S, 16)

        sc_mT = {}
        P.wait("vector", S, ld_m01)
        for h in range(NH):
            ve_a = P.run("vector", lambda e, h=h: e.scalar_tensor_tensor(
                t64[:, :], c.norm_all[0:64, h, :], c.rm10k[0:64, h:h + 1],
                c.band_sb[:, :], ALU.add, ALU.subtract), VE, 1)
            P.wait("vector", VE, ve_a)
            ve_b = P.run("vector", lambda e, h=h: e.scalar_tensor_tensor(
                t64[:, :], m01c[0:64, h, :], -10000.0, t64[:, :],
                ALU.mult, ALU.add), VE, 1)
            P.wait("scalar", VE, ve_b)
            sc_e = P.run("scalar", lambda e: e.activation(
                lwf[:, :], t64[:, :], AF.Exp), SC, 1)
            P.wait("tensor", SC, sc_e)
            P.run("tensor", lambda e: e.transpose(
                sb[0][0:64, 0:64], lwf[:, :], c.identf_sb[0:64, 0:64]))
            pe_t = P.run("tensor", lambda e, h=h: e.transpose(
                sb[1][0:64, 0:64], m01c[0:64, h, :], c.identf_sb[0:64, 0:64]), PE, 1)
            P.wait("scalar", PE, pe_t)
            P.run("scalar", lambda e, h=h: e.copy(
                c.lowWT[0:64, h, :], sb[0][0:64, 0:64]), SC, 1)
            sc_mT[h] = P.run("scalar", lambda e, h=h: e.copy(
                c.M01cT[0:64, h, :], sb[1][0:64, 0:64]), SC, 1)
            P.wait("vector", PE, pe_t)

        P.wait("sync", SC, sc_mT[NH - 1])
        ld_dup = P.run("sync", lambda e: e.dma_start(
            out=c.M01cT[64:128, :, :], in_=c.M01cT[0:64, :, :]), S, 16)

        sc_pt = {}
        pe_av = {}
        dv = {}

        def _emit_av(h, p, kc):
            P.wait("tensor", SC, sc_pt[(h, p, kc)])
            for qq in range(2):
                pe_av[(h, p, kc)] = P.run("tensor",
                    lambda e, h=h, kc=kc, p=p, qq=qq: e.matmul(
                        av[2 * p + qq][0:65, :],
                        c.Vaug[:, kc, h, :],
                        PT[kc % 2][:, 512 * qq:512 * (qq + 1)],
                        start=False, stop=(kc == NTC - 1)),
                    PE if qq == 1 else None, 1)

        for h in range(NH):
            b = 64 * (h % 2)
            hc = h // 2
            for p in range(2):
                P.wait("tensor", SC, sc_mT[h])
                if h == 0 and p == 0:
                    P.wait("tensor", S, ld_dup)
                for qq in range(2):
                    qt = 2 * p + qq
                    if h > 0:
                        P.wait("tensor", VE, dv[(h - 1, p, 6 + qq)])
                    P.run("tensor", lambda e, h=h, qt=qt, p=p, qq=qq: e.matmul(
                        av[2 * p + qq][0:65, :],
                        c.Vh_aug[0:64, h, :],
                        c.lowWT[0:64, h, 16 * qt:16 * (qt + 1)]
                        .rearrange("p (f o) -> p f o", o=1).to_broadcast([64, 16, 32]),
                        start=True, stop=False))
                for kc in range(NTC):
                    key_prev = (h, p, kc - 2) if kc >= 2 else (
                        (h, p - 1, NTC - 2 + kc) if p > 0 else
                        ((h - 1, 1, NTC - 2 + kc) if h > 0 else None))
                    if key_prev is not None and key_prev in sc_pt:
                        P.wait("tensor", SC, sc_pt[key_prev])
                    pe_s = None
                    for j in range(2):
                        qt = 2 * p + j
                        P.run("tensor", lambda e, b=b, hc=hc, kc=kc, qt=qt, j=j: e.matmul(
                            sb[kc % 2][:, 512 * j:512 * (j + 1)],
                            c.KsT[b:b + 64, hc, 128 * kc:128 * (kc + 1)],
                            c.QT[b:b + 64, hc, 512 * qt:512 * (qt + 1)],
                            start=True, stop=False))
                    for j in range(2):
                        qt = 2 * p + j
                        pe_s = P.run("tensor", lambda e, b=b, h=h, kc=kc, qt=qt, j=j: e.matmul(
                            sb[kc % 2][:, 512 * j:512 * (j + 1)],
                            c.ebig_sb[b:b + 64, 128 * kc:128 * (kc + 1)],
                            c.M01cT[b:b + 64, h, 16 * qt:16 * (qt + 1)]
                            .rearrange("p (f o) -> p f o", o=1).to_broadcast([64, 16, 32]),
                            start=False, stop=True), PE if j == 1 else None, 1)
                    P.wait("scalar", PE, pe_s)
                    keyv = (h, p, kc - 2) if kc >= 2 else (
                        (h, p - 1, NTC - 2 + kc) if p > 0 else
                        ((h - 1, 1, NTC - 2 + kc) if h > 0 else None))
                    if keyv is not None and keyv in pe_av:
                        P.wait("scalar", PE, pe_av[keyv])
                    sc_pt[(h, p, kc)] = P.run(
                        "scalar", lambda e, kc=kc: e.activation(
                            PT[kc % 2][:, :], sb[kc % 2][:, :], AF.Exp), SC, 1)
                    if kc >= 1:
                        _emit_av(h, p, kc - 1)
                _emit_av(h, p, NTC - 1)

                P.wait("scalar", PE, pe_av[(h, p, NTC - 1)])
                sc_ae = None
                for qq in range(2):
                    sc_ae = P.run("scalar", lambda e, p=p, qq=qq: e.copy(
                        attnT[0:65, 512 * qq:512 * (qq + 1)],
                        av[2 * p + qq][0:65, :]), SC, 1)
                P.wait("tensor", SC, sc_ae)
                for i in range(8):
                    t = 8 * p + i
                    if i >= 2:
                        P.wait("tensor", VE, dv[(h, p, i - 2)])
                    pe_tr = P.run("tensor", lambda e, i=i, p=p: e.transpose(
                        av[2 * p + i % 2][:, 0:65], attnT[0:65, 128 * i:128 * (i + 1)],
                        c.identf_sb[0:65, 0:65]), PE, 1)
                    P.wait("vector", PE, pe_tr)
                    P.run("vector", lambda e, i=i, p=p: e.tensor_scalar_add(
                        dden[:, 0:1], av[2 * p + i % 2][:, 64:65], 1e-6))
                    ve_rc = P.run("vector", lambda e: e.reciprocal(
                        dden[:, 1:2], dden[:, 0:1]), VE, 1)
                    P.wait("vector", VE, ve_rc)
                    dv[(h, p, i)] = P.run("vector", lambda e, t=t, h=h, i=i, p=p: e.tensor_scalar_mul(
                        c.ACC[:, 1024 * t + 64 * h:1024 * t + 64 * h + 64],
                        av[2 * p + i % 2][:, 0:64], dden[:, 1:2]), VE, 1)

        _tap_phase(c, P, "B")
        P.emit("phaseB")


def _phase_cde(c, P, scope):
    nc = c.nc

    with ExitStack() as ps:
        po = ps.enter_context
        ptr = [po(nc.psum_tensor(f"pT{i}", [128, 512], F32)) for i in range(4)]
        plg = po(nc.psum_tensor("plg", [128, 512], F32))
        pcw = po(nc.psum_tensor("pcw", [128, 512], F32))
        pb2 = [po(nc.psum_tensor(f"pB2_{i}", [128, 512], F32)) for i in range(2)]

        S, PE, SC, VE = c.sLoad, c.sPE, c.sSC, c.sVE
        G, sWp = c.sG, [c.sWp0, c.sWp1]

        def ln_stats(tc, col0):
            st = c.stats
            ve_m = P.run("vector", lambda e, tc=tc, col0=col0: e.reduce_sum(
                st[:, col0:col0 + 1], c.ACC[:, 1024 * tc:1024 * (tc + 1)],
                axis=AX.X), VE, 1)
            P.wait("scalar", VE, ve_m)
            sc_sq = P.run("scalar", lambda e, tc=tc, col0=col0: e.activation(
                c.sdump[:, :], c.ACC[:, 1024 * tc:1024 * (tc + 1)], AF.Square,
                accum_out=st[:, col0 + 1:col0 + 2]), SC, 1)
            P.wait("vector", SC, sc_sq)
            P.run("vector", lambda e, col0=col0: e.tensor_scalar_mul(
                st[:, col0 + 2:col0 + 3], st[:, col0:col0 + 1], 1.0 / 1024.0))
            P.run("vector", lambda e, col0=col0: e.tensor_mul(
                st[:, col0 + 3:col0 + 4], st[:, col0 + 2:col0 + 3], st[:, col0 + 2:col0 + 3]))
            P.run("vector", lambda e, col0=col0: e.scalar_tensor_tensor(
                st[:, col0 + 4:col0 + 5], st[:, col0 + 1:col0 + 2], 1.0 / 1024.0,
                st[:, col0 + 3:col0 + 4], ALU.mult, ALU.subtract))
            ve_v = P.run("vector", lambda e, col0=col0: e.tensor_scalar_add(
                st[:, col0 + 4:col0 + 5], st[:, col0 + 4:col0 + 5], 1e-5), VE, 1)
            P.wait("scalar", VE, ve_v)
            P.run("scalar", lambda e, col0=col0: e.activation(
                st[:, col0 + 5:col0 + 6], st[:, col0 + 4:col0 + 5], AF.Ln))
            sc_r = P.run("scalar", lambda e, col0=col0: e.activation(
                st[:, col0 + 6:col0 + 7], st[:, col0 + 5:col0 + 6], AF.Exp,
                scale=-0.5), SC, 1)
            P.wait("vector", SC, sc_r)
            ve_nm = P.run("vector", lambda e, col0=col0: e.scalar_tensor_tensor(
                st[:, col0 + 7:col0 + 8], st[:, col0 + 2:col0 + 3], -1.0,
                st[:, col0 + 6:col0 + 7], ALU.mult, ALU.mult), VE, 1)
            P.wait("vector", VE, ve_nm)
            return st[:, col0 + 6:col0 + 7], st[:, col0 + 7:col0 + 8]

        P.wait("sync", c.sST, c.sST.n)
        P.wait("gpsimd", c.sST, c.sST.n)
        P.run("gpsimd", lambda e: e.dma_start(
            out=c.wg_sb[:, 0:128].rearrange("p (cc e) -> p cc e", cc=8),
            in_=c.wg_aug[0:1024, :].rearrange("(cc p) e -> p cc e", p=128)), G, 16)
        P.run("gpsimd", lambda e: e.dma_start(
            out=c.wg_sb[0:2, 128:144], in_=c.wg_aug[1024:1026, :]), G, 16)
        c.ld_g = P.run("gpsimd", lambda e: e.dma_start(
            out=c.b2a_sb[:, :], in_=c.b2), G, 16)
        c.wld = {}
        for s in range(2):
            c.wld[(0, s, "w1")] = P.run("gpsimd", lambda e, s=s: e.dma_start(
                out=c.w1b[s][:, :, :],
                in_=c.w1[0, :, 1024 * s:1024 * (s + 1)]
                .rearrange("(cc p) hh -> p cc hh", p=128)), sWp[s], 16)
            c.wld[(0, s, "w2")] = P.run("gpsimd", lambda e, s=s: e.dma_start(
                out=c.w2b[s][:, :, :],
                in_=c.w2[0, 1024 * s:1024 * (s + 1), :]
                .rearrange("(cc p) dd -> p cc dd", p=128)), sWp[s], 16)
        c.ld_b1 = {0: P.run("sync", lambda e: e.dma_start(
            out=c.b1_sb[:, 0:16], in_=c.b1r[0, :, :]), c.sB1, 16)}

        ld2, ve_res, sc_x1T, pe_x1t, ve_b2, ve_cw = {}, {}, {}, {}, {}, {}
        for tc in range(NTC):
            col0 = 8 * (tc % 2)
            if tc >= 1:
                P.wait("sync", VE, ve_res[tc - 1])
            ld2[tc] = P.run("sync", lambda e, tc=tc: e.dma_start(
                out=c.x0t2[:, :], in_=c.x[128 * tc:128 * (tc + 1), :]), S, 16)
            P.wait("vector", S, ld2[tc])
            ve_res[tc] = P.run("vector", lambda e, tc=tc: e.tensor_add(
                c.ACC[:, 1024 * tc:1024 * (tc + 1)],
                c.ACC[:, 1024 * tc:1024 * (tc + 1)], c.x0t2[:, :]), VE, 1)
            rstd, negmr = ln_stats(tc, col0)
            ve_x1 = P.run("vector", lambda e, tc=tc, rstd=rstd, negmr=negmr: e.tensor_scalar(
                c.ACC[:, 1024 * tc:1024 * (tc + 1)],
                c.ACC[:, 1024 * tc:1024 * (tc + 1)],
                rstd, negmr, ALU.mult, ALU.add), VE, 1)

            P.wait("tensor", VE, ve_x1)
            if tc >= 2:
                P.wait("tensor", SC, sc_x1T[tc - 2])
            for dc in range(8):
                bank = ptr[2 * (tc % 2) + dc // 4]
                pe_x1t[tc] = P.run("tensor", lambda e, tc=tc, dc=dc, bank=bank: e.transpose(
                    bank[:, 128 * (dc % 4):128 * (dc % 4 + 1)],
                    c.ACC[:, 1024 * tc + 128 * dc:1024 * tc + 128 * (dc + 1)],
                    c.identf_sb[:, :]), PE if dc == 7 else None, 1)
            P.wait("scalar", PE, pe_x1t[tc])
            for half in range(2):
                bank = ptr[2 * (tc % 2) + half]
                sc_x1T[tc] = P.run("scalar", lambda e, tc=tc, half=half, bank=bank: e.copy(
                    c.x1T[:, 4 * half:4 * (half + 1), 128 * tc:128 * (tc + 1)],
                    bank[:, :].rearrange("p (cc t) -> p cc t", cc=4)), SC, 1)

            P.wait("tensor", SC, sc_x1T[tc])
            if tc == 0:
                P.wait("tensor", G, c.ld_g)
            for cc in range(8):
                P.run("tensor", lambda e, tc=tc, cc=cc: e.matmul(
                    plg[:, 0:16],
                    c.x1T[:, cc, 128 * tc:128 * (tc + 1)],
                    c.wg_sb[:, 16 * cc:16 * (cc + 1)],
                    start=(cc == 0), stop=False))
            pe_lg = P.run("tensor", lambda e: e.matmul(
                plg[:, 0:16], c.onesb_sb[0:2, 0:128], c.wg_sb[0:2, 128:144],
                start=False, stop=True), PE, 1)
            P.wait("scalar", PE, pe_lg)
            sc_lg = P.run("scalar", lambda e, tc=tc: e.copy(
                c.lg_sb[:, 16 * (tc % 2):16 * (tc % 2) + 16], plg[:, 0:16]), SC, 1)

            P.wait("vector", SC, sc_lg)
            lg = c.lg_sb[:, 16 * (tc % 2):16 * (tc % 2) + 16]
            cw = c.cw_sb[:, 16 * tc:16 * (tc + 1)]
            t16 = c.t16
            ve_mx = P.run("vector", lambda e, lg=lg: e.max(
                out=c.top8[:, :], in_=lg), VE, 1)
            ve_d = P.run("vector", lambda e: e.tensor_sub(
                c.stats[:, 16:17], c.top8[:, 1:2], c.top8[:, 0:1]), VE, 1)
            P.wait("scalar", VE, ve_d)
            sc_e2 = P.run("scalar", lambda e: e.activation(
                c.stats[:, 17:18], c.stats[:, 16:17], AF.Exp), SC, 1)
            P.wait("vector", SC, sc_e2)
            P.run("vector", lambda e: e.tensor_scalar_add(
                c.stats[:, 18:19], c.stats[:, 17:18], 1.0))
            ve_rc = P.run("vector", lambda e: e.reciprocal(
                c.stats[:, 19:20], c.stats[:, 18:19]), VE, 1)
            ve_w2 = P.run("vector", lambda e: e.tensor_mul(
                c.stats[:, 20:21], c.stats[:, 17:18], c.stats[:, 19:20]), VE, 1)
            P.wait("vector", VE, ve_mx)
            P.run("vector", lambda e, lg=lg: e.tensor_scalar(
                t16[:, 0:16], lg, c.top8[:, 0:1], None, ALU.is_equal))
            P.run("vector", lambda e, lg=lg: e.scalar_tensor_tensor(
                t16[:, 16:32], t16[:, 0:16], -1e30, lg, ALU.mult, ALU.add))
            P.run("vector", lambda e: e.tensor_scalar(
                t16[:, 32:48], t16[:, 16:32], c.top8[:, 1:2], None, ALU.is_equal))
            P.wait("vector", VE, ve_w2)
            P.run("vector", lambda e, cw=cw: e.tensor_scalar(
                cw, t16[:, 0:16], c.stats[:, 19:20], None, ALU.mult))
            ve_cw[tc] = P.run("vector", lambda e, cw=cw: e.scalar_tensor_tensor(
                cw, t16[:, 32:48], c.stats[:, 20:21], cw, ALU.mult, ALU.add), VE, 1)

            P.wait("tensor", VE, ve_cw[tc])
            pe_cw = P.run("tensor", lambda e, cw=cw: e.transpose(
                pcw[0:16, 0:128], cw, c.identf_sb[:, :]), PE, 1)
            P.wait("scalar", PE, pe_cw)
            sc_cwT = P.run("scalar", lambda e, tc=tc: e.copy(
                c.cwT_sb[0:16, 128 * tc:128 * (tc + 1)], pcw[0:16, 0:128]), SC, 1)
            P.wait("tensor", SC, sc_cwT)
            if tc == 0:
                P.wait("tensor", G, c.ld_g)
            if tc >= 1:
                P.wait("tensor", VE, ve_b2[tc - 1])
            pe_b2 = None
            for dd in range(2):
                pe_b2 = P.run("tensor", lambda e, tc=tc, dd=dd: e.matmul(
                    pb2[dd][:, :], c.cwT_sb[0:16, 128 * tc:128 * (tc + 1)],
                    c.b2a_sb[0:16, 512 * dd:512 * (dd + 1)],
                    start=True, stop=True), PE if dd == 1 else None, 1)
            P.wait("vector", PE, pe_b2)
            for dd in range(2):
                ve_b2[tc] = P.run("vector", lambda e, tc=tc, dd=dd: e.tensor_add(
                    c.ACC[:, 1024 * tc + 512 * dd:1024 * tc + 512 * (dd + 1)],
                    c.ACC[:, 1024 * tc + 512 * dd:1024 * tc + 512 * (dd + 1)],
                    pb2[dd][:, :]), VE if dd == 1 else None, 1)

        _tap_phase(c, P, "C")
        P.emit("phaseC")

        phb = [plg, pcw]
        ybk = ptr

        sc_relu, pe_h, pe_y, pe_yh, ve_y = {}, {}, {}, {}, {}

        def emit_y(e_i, s, tc, Hs, par, first):
            k = (e_i, s, tc, Hs)
            P.wait("tensor", SC, sc_relu[k])
            if Hs == 0:
                if tc > 0:
                    P.wait("tensor", VE, ve_y[(e_i, s, tc - 1)])
                elif not first:
                    ep, sp = divmod(2 * e_i + s - 1, 2)
                    P.wait("tensor", VE, ve_y[(ep, sp, 7)])
            for tsub in range(2):
                for dd in range(2):
                    m = P.run("tensor",
                        lambda e, par=par, Hs=Hs, tsub=tsub, dd=dd: e.matmul(
                            ybk[2 * tsub + dd][:, :],
                            c.ht_sb[Hs % 2][:, 128 * tsub:128 * (tsub + 1)],
                            c.w2b[par][:, Hs, 512 * dd:512 * (dd + 1)],
                            start=(Hs == 0), stop=(Hs == 7)),
                        PE if (tsub == 1 and dd == 1) else None, 1)
            pe_yh[k] = m
            pe_y[(e_i, s, tc)] = m

        for e_i in range(NEXP):
            for s in range(2):
                par = (2 * e_i + s) % 2
                first = (e_i == 0 and s == 0)
                for tc in range(8):
                    for Hs in range(8):
                        Hglob = 8 * s + Hs
                        hb = phb[Hs % 2]
                        k = (e_i, s, tc, Hs)
                        if tc == 0 and Hs == 0:
                            P.wait("tensor", sWp[par], c.wld[(e_i, s, "w2")])
                            P.wait("tensor", c.sB1, c.ld_b1[e_i])
                        if tc > 0 or Hs >= 2:
                            kprev = ((e_i, s, tc, Hs - 2) if Hs >= 2
                                     else (e_i, s, tc - 1, 6 + Hs))
                            P.wait("tensor", SC, sc_relu[kprev])
                        elif not first:
                            ep, sp = divmod(2 * e_i + s - 1, 2)
                            P.wait("tensor", SC, sc_relu[(ep, sp, 7, 6 + Hs)])
                        for cc in range(8):
                            pe_h[k] = P.run("tensor",
                                lambda e, par=par, Hs=Hs, tc=tc, cc=cc, hb=hb: e.matmul(
                                    hb[:, 0:256],
                                    c.w1b[par][:, cc, 128 * Hs:128 * (Hs + 1)],
                                    c.x1T[:, cc, 256 * tc:256 * (tc + 1)],
                                    start=(cc == 0), stop=(cc == 7)),
                                PE if cc == 7 else None, 1)
                        P.wait("scalar", PE, pe_h[k])
                        kht = ((e_i, s, tc, Hs - 2) if Hs >= 2
                               else ((e_i, s, tc - 1, 6 + Hs) if tc > 0 else None))
                        if kht is None and not first:
                            ep, sp = divmod(2 * e_i + s - 1, 2)
                            kht = (ep, sp, 7, 6 + Hs)
                        if kht is not None and kht in pe_yh:
                            P.wait("scalar", PE, pe_yh[kht])
                        sc_relu[k] = P.run("scalar",
                            lambda e, e_i=e_i, Hglob=Hglob, Hs=Hs, hb=hb: e.activation(
                                c.ht_sb[Hs % 2][:, :], hb[:, 0:256], AF.Relu,
                                bias=c.b1_sb[:, 16 * (e_i % 2) + Hglob:
                                             16 * (e_i % 2) + Hglob + 1]), SC, 1)
                        if Hs >= 1:
                            emit_y(e_i, s, tc, Hs - 1, par, first)
                    emit_y(e_i, s, tc, 7, par, first)
                    P.wait("vector", PE, pe_y[(e_i, s, tc)])
                    for tsub in range(2):
                        tcg = 2 * tc + tsub
                        for dd in range(2):
                            ve_y[(e_i, s, tc)] = P.run("vector",
                                lambda e, e_i=e_i, tcg=tcg, tsub=tsub, dd=dd: e.scalar_tensor_tensor(
                                    c.ACC[:, 1024 * tcg + 512 * dd:1024 * tcg + 512 * (dd + 1)],
                                    ybk[2 * tsub + dd][:, :],
                                    c.cw_sb[:, 16 * tcg + e_i:16 * tcg + e_i + 1],
                                    c.ACC[:, 1024 * tcg + 512 * dd:1024 * tcg + 512 * (dd + 1)],
                                    ALU.mult, ALU.add),
                                VE if (tsub == 1 and dd == 1) else None, 1)

                nxt = 2 * e_i + s + 2
                if nxt < 2 * NEXP:
                    en, sn = divmod(nxt, 2)
                    P.wait("gpsimd", PE, pe_y[(e_i, s, 7)])
                    c.wld[(en, sn, "w1")] = P.run("gpsimd",
                        lambda e, en=en, sn=sn, par=par: e.dma_start(
                            out=c.w1b[par][:, :, :],
                            in_=c.w1[en, :, 1024 * sn:1024 * (sn + 1)]
                            .rearrange("(cc p) hh -> p cc hh", p=128)), sWp[par], 16)
                    c.wld[(en, sn, "w2")] = P.run("gpsimd",
                        lambda e, en=en, sn=sn, par=par: e.dma_start(
                            out=c.w2b[par][:, :, :],
                            in_=c.w2[en, 1024 * sn:1024 * (sn + 1), :]
                            .rearrange("(cc p) dd -> p cc dd", p=128)), sWp[par], 16)
                    if sn == 0:
                        if en >= 2:
                            P.wait("sync", SC, sc_relu[(en - 2, 1, 7, 7)])
                        P.wait("sync", c.sB1, c.ld_b1[en - 1])
                        c.ld_b1[en] = P.run("sync", lambda e, en=en: e.dma_start(
                            out=c.b1_sb[:, 16 * (en % 2):16 * (en % 2) + 16],
                            in_=c.b1r[en, :, :]), c.sB1, 16)

        _tap_phase(c, P, "D")
        P.emit("phaseD")

        for tc in range(NTC):
            col0 = 8 * (tc % 2)
            rstd, negmr = ln_stats(tc, col0)
            ve_o = P.run("vector", lambda e, tc=tc, rstd=rstd, negmr=negmr: e.tensor_scalar(
                c.ACC[:, 1024 * tc:1024 * (tc + 1)],
                c.ACC[:, 1024 * tc:1024 * (tc + 1)],
                rstd, negmr, ALU.mult, ALU.add), VE, 1)
            P.wait("sync", VE, ve_o)
            P.run("sync", lambda e, tc=tc: e.dma_start(
                out=c.y[128 * tc:128 * (tc + 1), :],
                in_=c.ACC[:, 1024 * tc:1024 * (tc + 1)]), c.sST, 16)
        _tap_phase(c, P, "E")
        P.wait("sync", c.sST, c.sST.n)
        P.emit("phaseE")


def kernel(**inputs):
    X = np.asarray(inputs["L0_attn_X"], np.float32)
    Wq = np.asarray(inputs["L0_attn_Wq"], np.float32)
    bq = np.asarray(inputs["L0_attn_bq"], np.float32)
    Wk = np.asarray(inputs["L0_attn_Wk"], np.float32)
    bk = np.asarray(inputs["L0_attn_bk"], np.float32)
    Wv = np.asarray(inputs["L0_attn_Wv"], np.float32)
    bv = np.asarray(inputs["L0_attn_bv"], np.float32)
    Wg = np.asarray(inputs["L0_ffn_Wg"], np.float32)
    bg = np.asarray(inputs["L0_ffn_bg"], np.float32)
    bias = np.asarray(inputs["L0_ffn_bias"], np.float32)
    W1 = np.asarray(inputs["L0_ffn_W1"], np.float32)
    b1 = np.asarray(inputs["L0_ffn_b1"], np.float32)
    W2 = np.asarray(inputs["L0_ffn_W2"], np.float32)
    b2 = np.asarray(inputs["L0_ffn_b2"], np.float32)

    cst = _consts()
    common = dict(
        wqT=np.ascontiguousarray(Wq.T),
        wkT=np.ascontiguousarray(Wk.T),
        wvT=np.ascontiguousarray(Wv.T),
        bq_rs=np.ascontiguousarray(bq.reshape(8, 128).T),
        bk_rs=np.ascontiguousarray(bk.reshape(8, 128).T),
        bvrow=np.ascontiguousarray(bv.reshape(1, D)),
        wg_aug=np.ascontiguousarray(
            np.concatenate([Wg, bg[None, :], bias[None, :]], 0)),
        w1=W1, w2=W2, b2=b2,
        b1r=np.ascontiguousarray(b1.reshape(NEXP, 16, 128).transpose(0, 2, 1)),
        **cst,
    )
    nc, _ = build_program()
    in_maps = [dict(common, x=np.ascontiguousarray(X[b])) for b in range(8)]
    res = run_bass_kernel_spmd(nc, in_maps, core_ids=list(range(8)))
    return np.stack([r["y"] for r in res.results], 0)


if __name__ == "__main__":
    import reference as ref
    ins = {k: np.asarray(v) for k, v in ref.setup_inputs(0).items()}
    out = kernel(**ins)
    print("out", out.shape, out.dtype)


# version 04
# speedup vs baseline: 1.2290x; 1.2290x over previous
import numpy as np
import ml_dtypes
from contextlib import ExitStack

import concourse.bass as bass
import concourse.mybir as mybir
from concourse.bass_utils import run_bass_kernel_spmd

F32 = mybir.dt.float32
BF16 = mybir.dt.bfloat16
U8 = mybir.dt.uint8
AF = mybir.ActivationFunctionType
ALU = mybir.AluOpType
AX = mybir.AxisListType

L, D = 2048, 1024
NH, HD, BLK, NB = 16, 64, 32, 64
NTC = 16
SCALE = 0.125
BIS_IT = 52
NEXP, DH = 16, 2048
BF = ml_dtypes.bfloat16
ENGINES = ["sync", "gpsimd", "tensor", "scalar", "vector"]

_TAPS = []


class Sem:
    __slots__ = ("h", "n")

    def __init__(self, nc, name):
        self.h = nc.alloc_semaphore(name=name)
        self.n = 0


class Prog:

    def __init__(self, nc):
        self.nc = nc
        self.esem = {}
        self.reset()

    def reset(self):
        self.q = {e: [] for e in ENGINES}

    def wait(self, eng, sem, n):
        if n and n > 0:
            self.q[eng].append(lambda e: e.wait_ge(sem.h, n))

    def run(self, eng, make, sem=None, k=1):
        es = self.esem.get(eng)
        if es is not None:
            self.wait(eng, es, es.n)
            if sem is None:
                sem, k = es, 1
        if sem is None:
            self.q[eng].append(lambda e: make(e))
            return None
        sem.n += k
        kk = k

        def f(e):
            make(e).then_inc(sem.h, kk)
        self.q[eng].append(f)
        return sem.n

    def emit(self, name):
        with self.nc.Block(name) as block:
            for eng in ENGINES:
                fns = self.q[eng]
                if not fns:
                    continue

                def body(e, fns=fns):
                    for f in fns:
                        f(e)
                getattr(block, eng)(body)
        self.reset()


class Ctx:
    pass


def _consts():
    c = {}
    c["identf"] = np.eye(128, dtype=np.float32)
    c["identb"] = np.eye(128, dtype=BF)
    c["onesb"] = np.ones((128, 128), dtype=BF)
    e = np.zeros((64, 2048), np.float32)
    for nb in range(64):
        e[nb, 32 * nb:32 * nb + 32] = 10000.0
    c["ebig"] = np.concatenate([e, e], 0).astype(BF)
    band = (np.abs(np.arange(NB)[:, None] - np.arange(NB)[None, :]) <= 1)
    c["band5k"] = (band * 5000.0).astype(np.float32)
    ab = np.zeros((2048, 64), np.float32)
    for t in range(2048):
        ab[t, t // 32] = 1.0 / 32.0
    c["abm"] = ab.astype(BF)
    i8 = np.zeros((128, 128), np.float32)
    for p in range(128):
        i8[p, (p // 8) * 8:(p // 8) * 8 + 8] = 1.0
    c["i8blk"] = i8
    return c


def _tap_phase(c, P, phase):
    first = True
    for ph, attr, outname in _TAPS:
        if ph != phase:
            continue
        if first:
            for sem in (c.sPE, c.sSC, c.sVE):
                P.wait("sync", sem, sem.n)
            first = False
        t = getattr(c, attr)
        out = c.nc.dram_tensor("tap_" + outname, list(t.shape), t.dtype,
                               kind="ExternalOutput").ap()
        full = t[tuple(slice(None) for _ in t.shape)]
        P.run("sync", lambda e, out=out, full=full: e.dma_start(out=out, in_=full),
              c.sST, 16)


def build_program():
    nc = bass.Bass("TRN2", target_bir_lowering=False)
    c = Ctx()
    c.nc = nc
    P = Prog(nc)

    def din(name, shape, dt=F32):
        return nc.dram_tensor(name, list(shape), dt, kind="ExternalInput").ap()

    c.x = din("x", (L, D))
    c.wqT = din("wqT", (D, D))
    c.wkT = din("wkT", (D, D))
    c.wvT = din("wvT", (D, D))
    c.bq_rs = din("bq_rs", (128, 8))
    c.bk_rs = din("bk_rs", (128, 8))
    c.bvrow = din("bvrow", (1, D))
    c.wg_aug = din("wg_aug", (D + 2, NEXP))
    c.w1 = din("w1", (NEXP, D, DH))
    c.b1r = din("b1r", (NEXP, 128, 16))
    c.w2 = din("w2", (NEXP, DH, D))
    c.b2 = din("b2", (NEXP, D))
    c.identf = din("identf", (128, 128))
    c.identb = din("identb", (128, 128), BF16)
    c.onesb = din("onesb", (128, 128), BF16)
    c.ebig = din("ebig", (128, 2048), BF16)
    c.band5k = din("band5k", (64, 64))
    c.abm = din("abm", (2048, 64), BF16)
    c.i8blk = din("i8blk", (128, 128))
    c.y = nc.dram_tensor("y", [L, D], F32, kind="ExternalOutput").ap()
    c.norm_scr = nc.dram_tensor("norm_scr", [NH, 4096], F32, kind="Internal").ap()
    c.m01_scr = nc.dram_tensor("m01_scr", [NH, 4096], F32, kind="Internal").ap()

    for s in ["sLoad", "sWt", "sX0", "sX1", "sWp0", "sWp1", "sG", "sB1",
              "sPE", "sSC", "sVE", "sST"]:
        setattr(c, s, Sem(nc, s))
    P.esem = {"vector": c.sVE, "scalar": c.sSC}

    with ExitStack() as outer:
        oc = outer.enter_context
        c.identf_sb = oc(nc.sbuf_tensor("identf_sb", [128, 128], F32))
        c.identb_sb = oc(nc.sbuf_tensor("identb_sb", [128, 128], BF16))
        c.onesb_sb = oc(nc.sbuf_tensor("onesb_sb", [128, 128], BF16))
        c.i8_sb = oc(nc.sbuf_tensor("i8_sb", [128, 128], F32))

        with ExitStack() as ab:
            abo = ab.enter_context
            c.QT = abo(nc.sbuf_tensor("QT", [128, 8, L], BF16))
            c.KsT = abo(nc.sbuf_tensor("KsT", [128, 8, L], BF16))
            c.Vaug = abo(nc.sbuf_tensor("Vaug", [128, NTC, NH, HD + 1], BF16))
            c.Vh_aug = abo(nc.sbuf_tensor("Vh_aug", [64, NH, HD + 1], BF16))
            c.band_sb = abo(nc.sbuf_tensor("band_sb", [64, 64], F32))
            c.norm_all = abo(nc.sbuf_tensor("norm_all", [64, NH, 64], F32))
            c.rm10k = abo(nc.sbuf_tensor("rm10k", [64, NH], F32))

            _phase_a(c, P)

            c.ACC = oc(nc.sbuf_tensor("ACC", [128, NTC * D], F32, side="right"))

            _phase_b(c, P, ab)

        with ExitStack() as cd:
            cdo = cd.enter_context
            c.x1T = cdo(nc.sbuf_tensor("x1T", [128, 8, L], BF16))
            c.w1b = [cdo(nc.sbuf_tensor(f"w1b{i}", [128, 8, 1024], BF16)) for i in range(2)]
            c.w2b = [cdo(nc.sbuf_tensor(f"w2b{i}", [128, 8, 1024], BF16)) for i in range(2)]
            c.b1_sb = cdo(nc.sbuf_tensor("b1_sb", [128, 32], F32))
            c.ht_sb = [cdo(nc.sbuf_tensor(f"ht{i}", [128, 256], BF16)) for i in range(2)]
            c.x0t2 = cdo(nc.sbuf_tensor("x0t2", [128, D], F32))
            c.sdump = cdo(nc.sbuf_tensor("sdump", [128, D], F32))
            c.stats = cdo(nc.sbuf_tensor("stats", [128, 32], F32))
            c.top8 = cdo(nc.sbuf_tensor("top8", [128, 8], F32))
            c.lg_sb = cdo(nc.sbuf_tensor("lg_sb", [128, 32], F32))
            c.t16 = cdo(nc.sbuf_tensor("t16", [128, 48], F32))
            c.wg_sb = cdo(nc.sbuf_tensor("wg_sb", [128, 144], BF16))
            c.b2a_sb = cdo(nc.sbuf_tensor("b2a_sb", [NEXP, D], BF16))
            c.cw_sb = cdo(nc.sbuf_tensor("cw_sb", [128, NTC * NEXP], F32))
            c.cwT_sb = cdo(nc.sbuf_tensor("cwT_sb", [NEXP, L], BF16))

            _phase_cde(c, P, cd)

    return nc, c


def _phase_a(c, P):
    nc = c.nc
    with ExitStack() as sc_a:
        ao = sc_a.enter_context
        c.x0T = x0T = ao(nc.sbuf_tensor("x0T", [128, 8, L], BF16))
        wt = ao(nc.sbuf_tensor("wt", [128, 8, 1024], BF16))
        x0b = [ao(nc.sbuf_tensor(f"x0b{i}", [128, D], BF16)) for i in range(2)]
        bvr_sb = ao(nc.sbuf_tensor("bvr_sb", [1, D], BF16))
        c.xh_sb = xh_sb = ao(nc.sbuf_tensor("xh_sb", [64, D], F32))
        abm_sb = ao(nc.sbuf_tensor("abm_sb", [128, NTC, 64], BF16))
        c.XhT = XhT = ao(nc.sbuf_tensor("XhT", [128, 8, 64], BF16))
        c.QhT = QhT = ao(nc.sbuf_tensor("QhT", [128, 8, 64], BF16))
        c.KhT = KhT = ao(nc.sbuf_tensor("KhT", [128, 8, 64], BF16))
        bq_sb = ao(nc.sbuf_tensor("bq_sb", [128, 8], F32))
        bks_sb = ao(nc.sbuf_tensor("bks_sb", [128, 8], F32))

        with ExitStack() as ps:
            po = ps.enter_context
            pk = [po(nc.psum_tensor(f"pA{i}", [128, 512], F32)) for i in range(2)]
            pb16 = [po(nc.psum_tensor(f"pAb{i}", [128, 1024], BF16)) for i in range(2)]
            pxh = [po(nc.psum_tensor(f"pXh{i}", [128, 512], F32)) for i in range(2)]
            plow = [po(nc.psum_tensor(f"plow{i}", [128, 512], F32)) for i in range(2)]

            S, PE, SC, VE = c.sLoad, c.sPE, c.sSC, c.sVE
            Wt, sX = c.sWt, [c.sX0, c.sX1]

            for dst, src in [
                (c.identf_sb[:, :], c.identf), (c.identb_sb[:, :], c.identb),
                (c.onesb_sb[:, :], c.onesb), (c.i8_sb[:, :], c.i8blk),
                (c.band_sb[:, :], c.band5k),
                (bq_sb[:, :], c.bq_rs), (bks_sb[:, :], c.bk_rs),
                (abm_sb[:, :, :], c.abm.rearrange("(tc p) n -> p tc n", p=128)),
            ]:
                P.run("sync", lambda e, d=dst, s=src: e.dma_start(out=d, in_=s), S, 16)
            ld_consts = S.n

            P.run("gpsimd", lambda e: e.dma_start(out=bvr_sb[:, :], in_=c.bvrow), Wt, 16)

            P.wait("vector", S, ld_consts)
            ve_bks = P.run("vector", lambda e: e.tensor_scalar_mul(
                bks_sb[:, :], bks_sb[:, :], SCALE), VE, 1)
            P.run("vector", lambda e: e.memset(c.Vaug[:, :, :, HD:HD + 1], 1.0), VE, 1)
            ve_init = P.run("vector", lambda e: e.memset(
                c.Vh_aug[0:64, :, HD:HD + 1], 32.0), VE, 1)

            ld_x0, pe_tile, sc_x0T = {}, {}, {}
            for tc in range(NTC):
                if tc >= 2:
                    P.wait("gpsimd", PE, pe_tile[tc - 2])
                ld_x0[tc] = P.run("gpsimd", lambda e, tc=tc: e.dma_start(
                    out=x0b[tc % 2][:, :], in_=c.x[128 * tc:128 * (tc + 1), :]),
                    sX[tc % 2], 16)

                P.wait("tensor", sX[tc % 2], ld_x0[tc])
                if tc == 0:
                    P.wait("tensor", S, ld_consts)
                for dd in range(2):
                    P.run("tensor", lambda e, tc=tc, dd=dd: e.matmul(
                        pxh[dd][0:64, :], abm_sb[:, tc, :],
                        x0b[tc % 2][:, 512 * dd:512 * (dd + 1)],
                        start=(tc == 0), stop=(tc == NTC - 1)))
                if tc >= 2:
                    P.wait("tensor", SC, sc_x0T[tc - 2])
                for dc in range(8):
                    pe_tile[tc] = P.run("tensor", lambda e, tc=tc, dc=dc: e.transpose(
                        pb16[tc % 2][:, 128 * dc:128 * (dc + 1)],
                        x0b[tc % 2][:, 128 * dc:128 * (dc + 1)],
                        c.identb_sb[:, :]), PE if dc == 7 else None, 1)

                P.wait("scalar", PE, pe_tile[tc])
                sc_x0T[tc] = P.run("scalar", lambda e, tc=tc: e.copy(
                    x0T[:, :, 128 * tc:128 * (tc + 1)],
                    pb16[tc % 2][:, :].rearrange("p (cc t) -> p cc t", cc=8)), SC, 1)

            ld_wq = P.run("gpsimd", lambda e: e.dma_start(
                out=wt[:, :, :], in_=c.wqT.rearrange("(cc p) d -> p cc d", p=128)), Wt, 16)

            P.wait("scalar", PE, pe_tile[NTC - 1])
            sc_xh = None
            for dd in range(2):
                sc_xh = P.run("scalar", lambda e, dd=dd: e.copy(
                    xh_sb[0:64, 512 * dd:512 * (dd + 1)], pxh[dd][0:64, :]), SC, 1)
            P.wait("tensor", SC, sc_xh)
            pe_xhT = None
            for dc in range(8):
                pe_xhT = P.run("tensor", lambda e, dc=dc: e.transpose(
                    pk[0][:, 64 * dc:64 * (dc + 1)],
                    xh_sb[0:64, 128 * dc:128 * (dc + 1)],
                    c.identf_sb[0:64, 0:64]), PE if dc == 7 else None, 1)
            P.wait("scalar", PE, pe_xhT)
            sc_xhT = P.run("scalar", lambda e: e.copy(
                XhT[:, :, :], pk[0][:, :].rearrange("p (cc n) -> p cc n", cc=8)), SC, 1)

            def proj_T(outT, bias_sb, scale, n_free, nt, w_ld, prev_sc):
                sc_evac = {}
                first = True
                for dc in range(8):
                    for tt in range(nt):
                        idx = dc * nt + tt
                        bank = pk[idx % 2]
                        if first:
                            P.wait("tensor", Wt, w_ld)
                            P.wait("tensor", SC, sc_xhT)
                            first = False
                        if idx >= 2:
                            P.wait("tensor", SC, sc_evac[idx - 2])
                        elif prev_sc:
                            P.wait("tensor", SC, prev_sc)
                        pe_g = None
                        for cc in range(8):
                            if n_free == 512:
                                def mk(e, dc=dc, tt=tt, cc=cc, bank=bank):
                                    return e.matmul(
                                        bank[:, 0:512],
                                        wt[:, cc, 128 * dc:128 * (dc + 1)],
                                        x0T[:, cc, 512 * tt:512 * (tt + 1)],
                                        start=(cc == 0), stop=(cc == 7))
                            else:
                                def mk(e, dc=dc, cc=cc, bank=bank):
                                    return e.matmul(
                                        bank[:, 0:64],
                                        wt[:, cc, 128 * dc:128 * (dc + 1)],
                                        XhT[:, cc, :],
                                        start=(cc == 0), stop=(cc == 7))
                            pe_g = P.run("tensor", mk, PE if cc == 7 else None, 1)
                        P.wait("scalar", PE, pe_g)
                        sc_evac[idx] = P.run("scalar",
                            lambda e, dc=dc, tt=tt, bank=bank: e.activation(
                                outT[:, dc, n_free * tt:n_free * (tt + 1)],
                                bank[:, 0:n_free], AF.Identity,
                                bias=bias_sb[:, dc:dc + 1], scale=scale), SC, 1)
                return sc_evac[8 * nt - 1]

            P.wait("scalar", VE, ve_bks)
            sc_q = proj_T(c.QT, bq_sb, 1.0, 512, 4, ld_wq, sc_xhT)
            sc_qh = proj_T(QhT, bq_sb, 1.0, 64, 1, ld_wq, sc_q)
            pe_after_q = PE.n

            P.wait("gpsimd", PE, pe_after_q)
            ld_wk = P.run("gpsimd", lambda e: e.dma_start(
                out=wt[:, :, :], in_=c.wkT.rearrange("(cc p) d -> p cc d", p=128)), Wt, 16)

            sc_k = proj_T(c.KsT, bks_sb, SCALE, 512, 4, ld_wk, sc_qh)
            sc_kh = proj_T(KhT, bks_sb, SCALE, 64, 1, ld_wk, sc_k)
            pe_after_k = PE.n

            P.wait("gpsimd", PE, pe_after_k)
            ld_wv = P.run("gpsimd", lambda e: e.dma_start(
                out=wt[:, :, :], in_=c.wvT.rearrange("(cc p) d -> p cc d", p=128)), Wt, 16)

            sc_v = {}
            idx = 0
            for tc in range(NTC):
                for dd in range(2):
                    bank = pk[idx % 2]
                    if idx == 0:
                        P.wait("tensor", Wt, ld_wv)
                        P.wait("tensor", VE, ve_init)
                    if idx >= 2:
                        P.wait("tensor", SC, sc_v[idx - 2])
                    elif idx < 2:
                        P.wait("tensor", SC, sc_kh)
                    for cc in range(8):
                        P.run("tensor", lambda e, tc=tc, dd=dd, cc=cc, bank=bank: e.matmul(
                            bank[:, :],
                            x0T[:, cc, 128 * tc:128 * (tc + 1)],
                            wt[:, cc, 512 * dd:512 * (dd + 1)],
                            start=(cc == 0), stop=False))
                    pe_g = P.run("tensor", lambda e, dd=dd, bank=bank: e.matmul(
                        bank[:, :], c.onesb_sb[0:1, 0:128],
                        bvr_sb[0:1, 512 * dd:512 * (dd + 1)],
                        start=False, stop=True), PE, 1)
                    P.wait("scalar", PE, pe_g)
                    sc_v[idx] = P.run("scalar", lambda e, tc=tc, dd=dd, bank=bank: e.copy(
                        c.Vaug[:, tc, 8 * dd:8 * (dd + 1), 0:HD],
                        bank[:, :].rearrange("p (h n) -> p h n", h=8)), SC, 1)
                    idx += 1

            sc_vh = {}
            for dd in range(2):
                bank = pk[dd % 2]
                P.wait("tensor", SC, sc_v[idx - 2 + dd])
                for cc in range(8):
                    P.run("tensor", lambda e, dd=dd, cc=cc, bank=bank: e.matmul(
                        bank[0:64, :], XhT[:, cc, :],
                        wt[:, cc, 512 * dd:512 * (dd + 1)],
                        start=(cc == 0), stop=False))
                pe_g = P.run("tensor", lambda e, dd=dd, bank=bank: e.matmul(
                    bank[0:64, :], c.onesb_sb[0:1, 0:64],
                    bvr_sb[0:1, 512 * dd:512 * (dd + 1)],
                    start=False, stop=True), PE, 1)
                P.wait("scalar", PE, pe_g)
                sc_vh[dd] = P.run("scalar", lambda e, dd=dd, bank=bank: e.activation(
                    c.Vh_aug[0:64, 8 * dd:8 * (dd + 1), 0:HD],
                    bank[0:64, :].rearrange("p (h n) -> p h n", h=8),
                    AF.Identity, scale=32.0), SC, 1)

            ve_low = {}
            for h in range(NH):
                b = 64 * (h % 2)
                hc = h // 2
                bank = plow[h % 2]
                if h == 0:
                    P.wait("tensor", SC, max(sc_qh, sc_kh))
                if h >= 2:
                    P.wait("tensor", VE, ve_low[h - 2])
                pe_l = P.run("tensor", lambda e, b=b, hc=hc, bank=bank: e.matmul(
                    bank[0:64, 0:64],
                    QhT[b:b + 64, hc, :], KhT[b:b + 64, hc, :],
                    start=True, stop=True), PE, 1)
                P.wait("vector", PE, pe_l)
                ve_rm = P.run("vector", lambda e, h=h, bank=bank: e.reduce_max(
                    c.rm10k[0:64, h:h + 1], bank[0:64, 0:64], axis=AX.X), VE, 1)
                P.wait("vector", VE, ve_rm)
                P.run("vector", lambda e, h=h, bank=bank: e.scalar_tensor_tensor(
                    c.norm_all[0:64, h, :], bank[0:64, 0:64],
                    c.rm10k[0:64, h:h + 1], c.band_sb[:, :],
                    ALU.subtract, ALU.add), VE, 1)
                ve_low[h] = P.run("vector", lambda e, h=h: e.tensor_scalar_add(
                    c.rm10k[0:64, h:h + 1], c.rm10k[0:64, h:h + 1], -10000.0), VE, 1)

            P.wait("sync", VE, ve_low[NH - 1])
            c.ld_norm_scr = P.run("sync", lambda e: e.dma_start(
                out=c.norm_scr.rearrange("h (q k) -> q h k", q=64),
                in_=c.norm_all[:, :, :]), S, 16)

            _tap_phase(c, P, "A")
            P.emit("phaseA")


def _phase_b(c, P, scope):
    nc = c.nc
    bo = scope.enter_context
    c.ebig_sb = bo(nc.sbuf_tensor("ebig_sb", [128, 2048], BF16))
    c.M01cT = bo(nc.sbuf_tensor("M01cT", [128, NH, 64], BF16))
    c.lowWT = bo(nc.sbuf_tensor("lowWT", [64, NH, 64], BF16))
    c.nfl = nfl = bo(nc.sbuf_tensor("nfl", [128, 512], F32))
    tmpf = bo(nc.sbuf_tensor("tmpf", [128, 512], F32))
    part = bo(nc.sbuf_tensor("part", [128, 1], F32))
    c.lo = lo = bo(nc.sbuf_tensor("lo", [128, 1], F32))
    hi = bo(nc.sbuf_tensor("hi", [128, 1], F32))
    mid = bo(nc.sbuf_tensor("mid", [128, 1], F32))
    mU = bo(nc.sbuf_tensor("mU", [128, 1], U8))
    mU2 = bo(nc.sbuf_tensor("mU2", [128, 1], U8))
    c.m01fl = m01fl = bo(nc.sbuf_tensor("m01fl", [128, 512], F32))
    c.m01c_sb = m01c = bo(nc.sbuf_tensor("m01c", [64, NH, 64], F32))
    t64 = bo(nc.sbuf_tensor("t64", [64, 64], F32))
    lwf = bo(nc.sbuf_tensor("lwf", [64, 64], F32))
    c.PT = PT = [bo(nc.sbuf_tensor(f"PT{i}", [128, 1024], BF16)) for i in range(2)]
    c.attnT_sb = attnT = [bo(nc.sbuf_tensor(f"attnT{i}", [65, 1024], F32)) for i in range(2)]
    dden = bo(nc.sbuf_tensor("dden", [128, 2], F32))

    with ExitStack() as ps:
        po = ps.enter_context
        sb = [po(nc.psum_tensor(f"pS{i}", [128, 1024], F32)) for i in range(2)]
        av = [po(nc.psum_tensor(f"pV{i}", [128, 512], F32)) for i in range(4)]

        S, PE, SC, VE = c.sLoad, c.sPE, c.sSC, c.sVE

        P.wait("sync", c.sST, c.sST.n)
        P.wait("gpsimd", c.sST, c.sST.n)
        ld_ebig = P.run("sync", lambda e: e.dma_start(
            out=c.ebig_sb[:, :], in_=c.ebig), S, 16)

        P.wait("sync", S, max(c.ld_norm_scr, ld_ebig))
        ld_flat = P.run("sync", lambda e: e.dma_start(
            out=nfl[:, :], in_=c.norm_scr.rearrange("h (j f) -> (h j) f", j=8)), S, 16)

        P.wait("vector", S, ld_flat)
        P.run("vector", lambda e: e.memset(lo[:, :], -30.0))
        P.run("vector", lambda e: e.memset(hi[:, :], 1.0))
        P.run("vector", lambda e: e.memset(mid[:, :], -14.5), VE, 1)
        ve_upd = VE.n
        for it in range(BIS_IT):
            P.wait("vector", VE, ve_upd)
            P.run("vector", lambda e: e.tensor_scalar(
                tmpf[:, :], nfl[:, :], mid[:, 0:1], None, ALU.is_ge))
            ve_cmp = P.run("vector", lambda e: e.reduce_sum(
                part[:, :], tmpf[:, :], axis=AX.X), VE, 1)
            P.wait("tensor", VE, ve_cmp)
            if it > 0:
                P.wait("tensor", VE, ve_upd)
            pe_bis = P.run("tensor", lambda e: e.matmul(
                sb[0][:, 0:1], c.i8_sb[:, :], part[:, 0:1],
                start=True, stop=True), PE, 1)
            P.wait("vector", PE, pe_bis)
            P.run("vector", lambda e: e.tensor_scalar(
                mU[:, :], sb[0][:, 0:1], 1024.0, None, ALU.is_ge))
            P.run("vector", lambda e: e.tensor_scalar(
                mU2[:, :], sb[0][:, 0:1], 1024.0, None, ALU.is_lt))
            P.run("vector", lambda e: e.copy_predicated(lo[:, 0:1], mU[:, 0:1], mid[:, 0:1]))
            P.run("vector", lambda e: e.copy_predicated(hi[:, 0:1], mU2[:, 0:1], mid[:, 0:1]))
            P.run("vector", lambda e: e.tensor_add(mid[:, 0:1], lo[:, 0:1], hi[:, 0:1]))
            ve_upd = P.run("vector", lambda e: e.tensor_scalar_mul(
                mid[:, 0:1], mid[:, 0:1], 0.5), VE, 1)
        P.wait("vector", VE, ve_upd)
        ve_m01 = P.run("vector", lambda e: e.tensor_scalar(
            m01fl[:, :], nfl[:, :], lo[:, 0:1], 1.0, ALU.is_ge, ALU.subtract), VE, 1)

        P.wait("sync", VE, ve_m01)
        st_m = P.run("sync", lambda e: e.dma_start(
            out=c.m01_scr.rearrange("h (j f) -> (h j) f", j=8), in_=m01fl[:, :]), S, 16)
        P.wait("sync", S, st_m)
        ld_m01 = P.run("sync", lambda e: e.dma_start(
            out=m01c[:, :, :], in_=c.m01_scr.rearrange("h (q k) -> q h k", q=64)), S, 16)

        sc_mT = {}
        P.wait("vector", S, ld_m01)
        for h in range(NH):
            ve_a = P.run("vector", lambda e, h=h: e.scalar_tensor_tensor(
                t64[:, :], c.norm_all[0:64, h, :], c.rm10k[0:64, h:h + 1],
                c.band_sb[:, :], ALU.add, ALU.subtract), VE, 1)
            P.wait("vector", VE, ve_a)
            ve_b = P.run("vector", lambda e, h=h: e.scalar_tensor_tensor(
                t64[:, :], m01c[0:64, h, :], -10000.0, t64[:, :],
                ALU.mult, ALU.add), VE, 1)
            P.wait("scalar", VE, ve_b)
            sc_e = P.run("scalar", lambda e: e.activation(
                lwf[:, :], t64[:, :], AF.Exp), SC, 1)
            P.wait("tensor", SC, sc_e)
            P.run("tensor", lambda e: e.transpose(
                sb[0][0:64, 0:64], lwf[:, :], c.identf_sb[0:64, 0:64]))
            pe_t = P.run("tensor", lambda e, h=h: e.transpose(
                sb[1][0:64, 0:64], m01c[0:64, h, :], c.identf_sb[0:64, 0:64]), PE, 1)
            P.wait("scalar", PE, pe_t)
            P.run("scalar", lambda e, h=h: e.copy(
                c.lowWT[0:64, h, :], sb[0][0:64, 0:64]), SC, 1)
            sc_mT[h] = P.run("scalar", lambda e, h=h: e.copy(
                c.M01cT[0:64, h, :], sb[1][0:64, 0:64]), SC, 1)
            P.wait("vector", PE, pe_t)

        P.wait("sync", SC, sc_mT[NH - 1])
        ld_dup = P.run("sync", lambda e: e.dma_start(
            out=c.M01cT[64:128, :, :], in_=c.M01cT[0:64, :, :]), S, 16)

        sc_pt = {}
        pe_av = {}
        dv = {}

        def _emit_av(h, p, kc):
            P.wait("tensor", SC, sc_pt[(h, p, kc)])
            for qq in range(2):
                pe_av[(h, p, kc)] = P.run("tensor",
                    lambda e, h=h, kc=kc, p=p, qq=qq: e.matmul(
                        av[2 * p + qq][0:65, :],
                        c.Vaug[:, kc, h, :],
                        PT[kc % 2][:, 512 * qq:512 * (qq + 1)],
                        start=False, stop=(kc == NTC - 1)),
                    PE if qq == 1 else None, 1)

        def _tail_piece(hp, pp, i):
            at = attnT[pp]
            if i == 0:
                P.wait("scalar", PE, pe_av[(hp, pp, NTC - 1)])
                for qq in range(2):
                    tail_ae[(hp, pp)] = P.run("scalar", lambda e, pp=pp, qq=qq, at=at: e.copy(
                        at[0:65, 512 * qq:512 * (qq + 1)],
                        av[2 * pp + qq][0:65, :]), SC, 1)
                P.wait("tensor", SC, tail_ae[(hp, pp)])
            t = 8 * pp + i
            if i >= 2:
                P.wait("tensor", VE, dv[(hp, pp, i - 2)])
            pe_tr = P.run("tensor", lambda e, i=i, pp=pp, at=at: e.transpose(
                av[2 * pp + i % 2][:, 0:65], at[0:65, 128 * i:128 * (i + 1)],
                c.identf_sb[0:65, 0:65]), PE, 1)
            P.wait("vector", PE, pe_tr)
            P.run("vector", lambda e, i=i, pp=pp: e.tensor_scalar_add(
                dden[:, 0:1], av[2 * pp + i % 2][:, 64:65], 1e-6))
            ve_rc = P.run("vector", lambda e: e.reciprocal(
                dden[:, 1:2], dden[:, 0:1]), VE, 1)
            P.wait("vector", VE, ve_rc)
            dv[(hp, pp, i)] = P.run("vector",
                lambda e, t=t, hp=hp, i=i, pp=pp: e.tensor_scalar_mul(
                    c.ACC[:, 1024 * t + 64 * hp:1024 * t + 64 * hp + 64],
                    av[2 * pp + i % 2][:, 0:64], dden[:, 1:2]), VE, 1)

        tail_ae = {}
        pending = None
        for h in range(NH):
            b = 64 * (h % 2)
            hc = h // 2
            for p in range(2):
                P.wait("tensor", SC, sc_mT[h])
                if h == 0 and p == 0:
                    P.wait("tensor", S, ld_dup)
                for qq in range(2):
                    qt = 2 * p + qq
                    if h > 0:
                        P.wait("tensor", VE, dv[(h - 1, p, 6 + qq)])
                    P.run("tensor", lambda e, h=h, qt=qt, p=p, qq=qq: e.matmul(
                        av[2 * p + qq][0:65, :],
                        c.Vh_aug[0:64, h, :],
                        c.lowWT[0:64, h, 16 * qt:16 * (qt + 1)]
                        .rearrange("p (f o) -> p f o", o=1).to_broadcast([64, 16, 32]),
                        start=True, stop=False))
                for kc in range(NTC):
                    key_prev = (h, p, kc - 2) if kc >= 2 else (
                        (h, p - 1, NTC - 2 + kc) if p > 0 else
                        ((h - 1, 1, NTC - 2 + kc) if h > 0 else None))
                    if key_prev is not None and key_prev in sc_pt:
                        P.wait("tensor", SC, sc_pt[key_prev])
                    pe_s = None
                    for j in range(2):
                        qt = 2 * p + j
                        P.run("tensor", lambda e, b=b, hc=hc, kc=kc, qt=qt, j=j: e.matmul(
                            sb[kc % 2][:, 512 * j:512 * (j + 1)],
                            c.KsT[b:b + 64, hc, 128 * kc:128 * (kc + 1)],
                            c.QT[b:b + 64, hc, 512 * qt:512 * (qt + 1)],
                            start=True, stop=False))
                    for j in range(2):
                        qt = 2 * p + j
                        pe_s = P.run("tensor", lambda e, b=b, h=h, kc=kc, qt=qt, j=j: e.matmul(
                            sb[kc % 2][:, 512 * j:512 * (j + 1)],
                            c.ebig_sb[b:b + 64, 128 * kc:128 * (kc + 1)],
                            c.M01cT[b:b + 64, h, 16 * qt:16 * (qt + 1)]
                            .rearrange("p (f o) -> p f o", o=1).to_broadcast([64, 16, 32]),
                            start=False, stop=True), PE if j == 1 else None, 1)
                    P.wait("scalar", PE, pe_s)
                    keyv = key_prev
                    if keyv is not None and keyv in pe_av:
                        P.wait("scalar", PE, pe_av[keyv])
                    sc_pt[(h, p, kc)] = P.run(
                        "scalar", lambda e, kc=kc: e.activation(
                            PT[kc % 2][:, :], sb[kc % 2][:, :], AF.Exp), SC, 1)
                    if kc >= 1:
                        _emit_av(h, p, kc - 1)
                    if pending is not None and kc % 2 == 1:
                        _tail_piece(pending[0], pending[1], kc // 2)
                _emit_av(h, p, NTC - 1)
                pending = (h, p)
        for i in range(8):
            _tail_piece(NH - 1, 1, i)

        _tap_phase(c, P, "B")
        P.emit("phaseB")


def _phase_cde(c, P, scope):
    nc = c.nc

    with ExitStack() as ps:
        po = ps.enter_context
        ptr = [po(nc.psum_tensor(f"pT{i}", [128, 512], F32)) for i in range(4)]
        plg = po(nc.psum_tensor("plg", [128, 512], F32))
        pcw = po(nc.psum_tensor("pcw", [128, 512], F32))
        pb2 = [po(nc.psum_tensor(f"pB2_{i}", [128, 512], F32)) for i in range(2)]

        S, PE, SC, VE = c.sLoad, c.sPE, c.sSC, c.sVE
        G, sWp = c.sG, [c.sWp0, c.sWp1]

        def ln_stats(tc, col0):
            st = c.stats
            ve_m = P.run("vector", lambda e, tc=tc, col0=col0: e.reduce_sum(
                st[:, col0:col0 + 1], c.ACC[:, 1024 * tc:1024 * (tc + 1)],
                axis=AX.X), VE, 1)
            P.wait("scalar", VE, ve_m)
            sc_sq = P.run("scalar", lambda e, tc=tc, col0=col0: e.activation(
                c.sdump[:, :], c.ACC[:, 1024 * tc:1024 * (tc + 1)], AF.Square,
                accum_out=st[:, col0 + 1:col0 + 2]), SC, 1)
            P.wait("vector", SC, sc_sq)
            P.run("vector", lambda e, col0=col0: e.tensor_scalar_mul(
                st[:, col0 + 2:col0 + 3], st[:, col0:col0 + 1], 1.0 / 1024.0))
            P.run("vector", lambda e, col0=col0: e.tensor_mul(
                st[:, col0 + 3:col0 + 4], st[:, col0 + 2:col0 + 3], st[:, col0 + 2:col0 + 3]))
            P.run("vector", lambda e, col0=col0: e.scalar_tensor_tensor(
                st[:, col0 + 4:col0 + 5], st[:, col0 + 1:col0 + 2], 1.0 / 1024.0,
                st[:, col0 + 3:col0 + 4], ALU.mult, ALU.subtract))
            ve_v = P.run("vector", lambda e, col0=col0: e.tensor_scalar_add(
                st[:, col0 + 4:col0 + 5], st[:, col0 + 4:col0 + 5], 1e-5), VE, 1)
            P.wait("scalar", VE, ve_v)
            P.run("scalar", lambda e, col0=col0: e.activation(
                st[:, col0 + 5:col0 + 6], st[:, col0 + 4:col0 + 5], AF.Ln))
            sc_r = P.run("scalar", lambda e, col0=col0: e.activation(
                st[:, col0 + 6:col0 + 7], st[:, col0 + 5:col0 + 6], AF.Exp,
                scale=-0.5), SC, 1)
            P.wait("vector", SC, sc_r)
            ve_nm = P.run("vector", lambda e, col0=col0: e.scalar_tensor_tensor(
                st[:, col0 + 7:col0 + 8], st[:, col0 + 2:col0 + 3], -1.0,
                st[:, col0 + 6:col0 + 7], ALU.mult, ALU.mult), VE, 1)
            P.wait("vector", VE, ve_nm)
            return st[:, col0 + 6:col0 + 7], st[:, col0 + 7:col0 + 8]

        P.wait("sync", c.sST, c.sST.n)
        P.wait("gpsimd", c.sST, c.sST.n)
        P.run("gpsimd", lambda e: e.dma_start(
            out=c.wg_sb[:, 0:128].rearrange("p (cc e) -> p cc e", cc=8),
            in_=c.wg_aug[0:1024, :].rearrange("(cc p) e -> p cc e", p=128)), G, 16)
        P.run("gpsimd", lambda e: e.dma_start(
            out=c.wg_sb[0:2, 128:144], in_=c.wg_aug[1024:1026, :]), G, 16)
        c.ld_g = P.run("gpsimd", lambda e: e.dma_start(
            out=c.b2a_sb[:, :], in_=c.b2), G, 16)
        c.wld = {}
        for s in range(2):
            c.wld[(0, s, "w1")] = P.run("gpsimd", lambda e, s=s: e.dma_start(
                out=c.w1b[s][:, :, :],
                in_=c.w1[0, :, 1024 * s:1024 * (s + 1)]
                .rearrange("(cc p) hh -> p cc hh", p=128)), sWp[s], 16)
            c.wld[(0, s, "w2")] = P.run("gpsimd", lambda e, s=s: e.dma_start(
                out=c.w2b[s][:, :, :],
                in_=c.w2[0, 1024 * s:1024 * (s + 1), :]
                .rearrange("(cc p) dd -> p cc dd", p=128)), sWp[s], 16)
        c.ld_b1 = {0: P.run("sync", lambda e: e.dma_start(
            out=c.b1_sb[:, 0:16], in_=c.b1r[0, :, :]), c.sB1, 16)}

        ld2, ve_res, sc_x1T, pe_x1t, ve_b2, ve_cw = {}, {}, {}, {}, {}, {}
        for tc in range(NTC):
            col0 = 8 * (tc % 2)
            if tc >= 1:
                P.wait("sync", VE, ve_res[tc - 1])
            ld2[tc] = P.run("sync", lambda e, tc=tc: e.dma_start(
                out=c.x0t2[:, :], in_=c.x[128 * tc:128 * (tc + 1), :]), S, 16)
            P.wait("vector", S, ld2[tc])
            ve_res[tc] = P.run("vector", lambda e, tc=tc: e.tensor_add(
                c.ACC[:, 1024 * tc:1024 * (tc + 1)],
                c.ACC[:, 1024 * tc:1024 * (tc + 1)], c.x0t2[:, :]), VE, 1)
            rstd, negmr = ln_stats(tc, col0)
            ve_x1 = P.run("vector", lambda e, tc=tc, rstd=rstd, negmr=negmr: e.tensor_scalar(
                c.ACC[:, 1024 * tc:1024 * (tc + 1)],
                c.ACC[:, 1024 * tc:1024 * (tc + 1)],
                rstd, negmr, ALU.mult, ALU.add), VE, 1)

            P.wait("tensor", VE, ve_x1)
            if tc >= 2:
                P.wait("tensor", SC, sc_x1T[tc - 2])
            for dc in range(8):
                bank = ptr[2 * (tc % 2) + dc // 4]
                pe_x1t[tc] = P.run("tensor", lambda e, tc=tc, dc=dc, bank=bank: e.transpose(
                    bank[:, 128 * (dc % 4):128 * (dc % 4 + 1)],
                    c.ACC[:, 1024 * tc + 128 * dc:1024 * tc + 128 * (dc + 1)],
                    c.identf_sb[:, :]), PE if dc == 7 else None, 1)
            P.wait("scalar", PE, pe_x1t[tc])
            for half in range(2):
                bank = ptr[2 * (tc % 2) + half]
                sc_x1T[tc] = P.run("scalar", lambda e, tc=tc, half=half, bank=bank: e.copy(
                    c.x1T[:, 4 * half:4 * (half + 1), 128 * tc:128 * (tc + 1)],
                    bank[:, :].rearrange("p (cc t) -> p cc t", cc=4)), SC, 1)

            P.wait("tensor", SC, sc_x1T[tc])
            if tc == 0:
                P.wait("tensor", G, c.ld_g)
            for cc in range(8):
                P.run("tensor", lambda e, tc=tc, cc=cc: e.matmul(
                    plg[:, 0:16],
                    c.x1T[:, cc, 128 * tc:128 * (tc + 1)],
                    c.wg_sb[:, 16 * cc:16 * (cc + 1)],
                    start=(cc == 0), stop=False))
            pe_lg = P.run("tensor", lambda e: e.matmul(
                plg[:, 0:16], c.onesb_sb[0:2, 0:128], c.wg_sb[0:2, 128:144],
                start=False, stop=True), PE, 1)
            P.wait("scalar", PE, pe_lg)
            sc_lg = P.run("scalar", lambda e, tc=tc: e.copy(
                c.lg_sb[:, 16 * (tc % 2):16 * (tc % 2) + 16], plg[:, 0:16]), SC, 1)

            P.wait("vector", SC, sc_lg)
            lg = c.lg_sb[:, 16 * (tc % 2):16 * (tc % 2) + 16]
            cw = c.cw_sb[:, 16 * tc:16 * (tc + 1)]
            t16 = c.t16
            ve_mx = P.run("vector", lambda e, lg=lg: e.max(
                out=c.top8[:, :], in_=lg), VE, 1)
            ve_d = P.run("vector", lambda e: e.tensor_sub(
                c.stats[:, 16:17], c.top8[:, 1:2], c.top8[:, 0:1]), VE, 1)
            P.wait("scalar", VE, ve_d)
            sc_e2 = P.run("scalar", lambda e: e.activation(
                c.stats[:, 17:18], c.stats[:, 16:17], AF.Exp), SC, 1)
            P.wait("vector", SC, sc_e2)
            P.run("vector", lambda e: e.tensor_scalar_add(
                c.stats[:, 18:19], c.stats[:, 17:18], 1.0))
            ve_rc = P.run("vector", lambda e: e.reciprocal(
                c.stats[:, 19:20], c.stats[:, 18:19]), VE, 1)
            ve_w2 = P.run("vector", lambda e: e.tensor_mul(
                c.stats[:, 20:21], c.stats[:, 17:18], c.stats[:, 19:20]), VE, 1)
            P.wait("vector", VE, ve_mx)
            P.run("vector", lambda e, lg=lg: e.tensor_scalar(
                t16[:, 0:16], lg, c.top8[:, 0:1], None, ALU.is_equal))
            P.run("vector", lambda e, lg=lg: e.scalar_tensor_tensor(
                t16[:, 16:32], t16[:, 0:16], -1e30, lg, ALU.mult, ALU.add))
            P.run("vector", lambda e: e.tensor_scalar(
                t16[:, 32:48], t16[:, 16:32], c.top8[:, 1:2], None, ALU.is_equal))
            P.wait("vector", VE, ve_w2)
            P.run("vector", lambda e, cw=cw: e.tensor_scalar(
                cw, t16[:, 0:16], c.stats[:, 19:20], None, ALU.mult))
            ve_cw[tc] = P.run("vector", lambda e, cw=cw: e.scalar_tensor_tensor(
                cw, t16[:, 32:48], c.stats[:, 20:21], cw, ALU.mult, ALU.add), VE, 1)

            P.wait("tensor", VE, ve_cw[tc])
            pe_cw = P.run("tensor", lambda e, cw=cw: e.transpose(
                pcw[0:16, 0:128], cw, c.identf_sb[:, :]), PE, 1)
            P.wait("scalar", PE, pe_cw)
            sc_cwT = P.run("scalar", lambda e, tc=tc: e.copy(
                c.cwT_sb[0:16, 128 * tc:128 * (tc + 1)], pcw[0:16, 0:128]), SC, 1)
            P.wait("tensor", SC, sc_cwT)
            if tc == 0:
                P.wait("tensor", G, c.ld_g)
            if tc >= 1:
                P.wait("tensor", VE, ve_b2[tc - 1])
            pe_b2 = None
            for dd in range(2):
                pe_b2 = P.run("tensor", lambda e, tc=tc, dd=dd: e.matmul(
                    pb2[dd][:, :], c.cwT_sb[0:16, 128 * tc:128 * (tc + 1)],
                    c.b2a_sb[0:16, 512 * dd:512 * (dd + 1)],
                    start=True, stop=True), PE if dd == 1 else None, 1)
            P.wait("vector", PE, pe_b2)
            for dd in range(2):
                ve_b2[tc] = P.run("vector", lambda e, tc=tc, dd=dd: e.tensor_add(
                    c.ACC[:, 1024 * tc + 512 * dd:1024 * tc + 512 * (dd + 1)],
                    c.ACC[:, 1024 * tc + 512 * dd:1024 * tc + 512 * (dd + 1)],
                    pb2[dd][:, :]), VE if dd == 1 else None, 1)

        _tap_phase(c, P, "C")
        P.emit("phaseC")

        phb = [plg, pcw]
        ybk = ptr

        sc_relu, pe_h, pe_y, pe_yh, ve_y = {}, {}, {}, {}, {}

        def emit_y(e_i, s, tc, Hs, par, first):
            k = (e_i, s, tc, Hs)
            P.wait("tensor", SC, sc_relu[k])
            if Hs == 0:
                if tc > 0:
                    P.wait("tensor", VE, ve_y[(e_i, s, tc - 1)])
                elif not first:
                    ep, sp = divmod(2 * e_i + s - 1, 2)
                    P.wait("tensor", VE, ve_y[(ep, sp, 7)])
            for tsub in range(2):
                for dd in range(2):
                    m = P.run("tensor",
                        lambda e, par=par, Hs=Hs, tsub=tsub, dd=dd: e.matmul(
                            ybk[2 * tsub + dd][:, :],
                            c.ht_sb[Hs % 2][:, 128 * tsub:128 * (tsub + 1)],
                            c.w2b[par][:, Hs, 512 * dd:512 * (dd + 1)],
                            start=(Hs == 0), stop=(Hs == 7)),
                        PE if (tsub == 1 and dd == 1) else None, 1)
            pe_yh[k] = m
            pe_y[(e_i, s, tc)] = m

        for e_i in range(NEXP):
            for s in range(2):
                par = (2 * e_i + s) % 2
                first = (e_i == 0 and s == 0)
                for tc in range(8):
                    for Hs in range(8):
                        Hglob = 8 * s + Hs
                        hb = phb[Hs % 2]
                        k = (e_i, s, tc, Hs)
                        if tc == 0 and Hs == 0:
                            P.wait("tensor", sWp[par], c.wld[(e_i, s, "w2")])
                            P.wait("tensor", c.sB1, c.ld_b1[e_i])
                        if tc > 0 or Hs >= 2:
                            kprev = ((e_i, s, tc, Hs - 2) if Hs >= 2
                                     else (e_i, s, tc - 1, 6 + Hs))
                            P.wait("tensor", SC, sc_relu[kprev])
                        elif not first:
                            ep, sp = divmod(2 * e_i + s - 1, 2)
                            P.wait("tensor", SC, sc_relu[(ep, sp, 7, 6 + Hs)])
                        for cc in range(8):
                            pe_h[k] = P.run("tensor",
                                lambda e, par=par, Hs=Hs, tc=tc, cc=cc, hb=hb: e.matmul(
                                    hb[:, 0:256],
                                    c.w1b[par][:, cc, 128 * Hs:128 * (Hs + 1)],
                                    c.x1T[:, cc, 256 * tc:256 * (tc + 1)],
                                    start=(cc == 0), stop=(cc == 7)),
                                PE if cc == 7 else None, 1)
                        P.wait("scalar", PE, pe_h[k])
                        kht = ((e_i, s, tc, Hs - 2) if Hs >= 2
                               else ((e_i, s, tc - 1, 6 + Hs) if tc > 0 else None))
                        if kht is None and not first:
                            ep, sp = divmod(2 * e_i + s - 1, 2)
                            kht = (ep, sp, 7, 6 + Hs)
                        if kht is not None and kht in pe_yh:
                            P.wait("scalar", PE, pe_yh[kht])
                        sc_relu[k] = P.run("scalar",
                            lambda e, e_i=e_i, Hglob=Hglob, Hs=Hs, hb=hb: e.activation(
                                c.ht_sb[Hs % 2][:, :], hb[:, 0:256], AF.Relu,
                                bias=c.b1_sb[:, 16 * (e_i % 2) + Hglob:
                                             16 * (e_i % 2) + Hglob + 1]), SC, 1)
                        if Hs >= 1:
                            emit_y(e_i, s, tc, Hs - 1, par, first)
                    emit_y(e_i, s, tc, 7, par, first)
                    P.wait("vector", PE, pe_y[(e_i, s, tc)])
                    for tsub in range(2):
                        tcg = 2 * tc + tsub
                        for dd in range(2):
                            ve_y[(e_i, s, tc)] = P.run("vector",
                                lambda e, e_i=e_i, tcg=tcg, tsub=tsub, dd=dd: e.scalar_tensor_tensor(
                                    c.ACC[:, 1024 * tcg + 512 * dd:1024 * tcg + 512 * (dd + 1)],
                                    ybk[2 * tsub + dd][:, :],
                                    c.cw_sb[:, 16 * tcg + e_i:16 * tcg + e_i + 1],
                                    c.ACC[:, 1024 * tcg + 512 * dd:1024 * tcg + 512 * (dd + 1)],
                                    ALU.mult, ALU.add),
                                VE if (tsub == 1 and dd == 1) else None, 1)

                nxt = 2 * e_i + s + 2
                if nxt < 2 * NEXP:
                    en, sn = divmod(nxt, 2)
                    P.wait("gpsimd", PE, pe_y[(e_i, s, 7)])
                    c.wld[(en, sn, "w1")] = P.run("gpsimd",
                        lambda e, en=en, sn=sn, par=par: e.dma_start(
                            out=c.w1b[par][:, :, :],
                            in_=c.w1[en, :, 1024 * sn:1024 * (sn + 1)]
                            .rearrange("(cc p) hh -> p cc hh", p=128)), sWp[par], 16)
                    c.wld[(en, sn, "w2")] = P.run("gpsimd",
                        lambda e, en=en, sn=sn, par=par: e.dma_start(
                            out=c.w2b[par][:, :, :],
                            in_=c.w2[en, 1024 * sn:1024 * (sn + 1), :]
                            .rearrange("(cc p) dd -> p cc dd", p=128)), sWp[par], 16)
                    if sn == 0:
                        if en >= 2:
                            P.wait("sync", SC, sc_relu[(en - 2, 1, 7, 7)])
                        P.wait("sync", c.sB1, c.ld_b1[en - 1])
                        c.ld_b1[en] = P.run("sync", lambda e, en=en: e.dma_start(
                            out=c.b1_sb[:, 16 * (en % 2):16 * (en % 2) + 16],
                            in_=c.b1r[en, :, :]), c.sB1, 16)

        _tap_phase(c, P, "D")
        P.emit("phaseD")

        for tc in range(NTC):
            col0 = 8 * (tc % 2)
            rstd, negmr = ln_stats(tc, col0)
            ve_o = P.run("vector", lambda e, tc=tc, rstd=rstd, negmr=negmr: e.tensor_scalar(
                c.ACC[:, 1024 * tc:1024 * (tc + 1)],
                c.ACC[:, 1024 * tc:1024 * (tc + 1)],
                rstd, negmr, ALU.mult, ALU.add), VE, 1)
            P.wait("sync", VE, ve_o)
            P.run("sync", lambda e, tc=tc: e.dma_start(
                out=c.y[128 * tc:128 * (tc + 1), :],
                in_=c.ACC[:, 1024 * tc:1024 * (tc + 1)]), c.sST, 16)
        _tap_phase(c, P, "E")
        P.wait("sync", c.sST, c.sST.n)
        P.emit("phaseE")


def kernel(**inputs):
    X = np.asarray(inputs["L0_attn_X"], np.float32)
    Wq = np.asarray(inputs["L0_attn_Wq"], np.float32)
    bq = np.asarray(inputs["L0_attn_bq"], np.float32)
    Wk = np.asarray(inputs["L0_attn_Wk"], np.float32)
    bk = np.asarray(inputs["L0_attn_bk"], np.float32)
    Wv = np.asarray(inputs["L0_attn_Wv"], np.float32)
    bv = np.asarray(inputs["L0_attn_bv"], np.float32)
    Wg = np.asarray(inputs["L0_ffn_Wg"], np.float32)
    bg = np.asarray(inputs["L0_ffn_bg"], np.float32)
    bias = np.asarray(inputs["L0_ffn_bias"], np.float32)
    W1 = np.asarray(inputs["L0_ffn_W1"], np.float32)
    b1 = np.asarray(inputs["L0_ffn_b1"], np.float32)
    W2 = np.asarray(inputs["L0_ffn_W2"], np.float32)
    b2 = np.asarray(inputs["L0_ffn_b2"], np.float32)

    cst = _consts()
    common = dict(
        wqT=np.ascontiguousarray(Wq.T),
        wkT=np.ascontiguousarray(Wk.T),
        wvT=np.ascontiguousarray(Wv.T),
        bq_rs=np.ascontiguousarray(bq.reshape(8, 128).T),
        bk_rs=np.ascontiguousarray(bk.reshape(8, 128).T),
        bvrow=np.ascontiguousarray(bv.reshape(1, D)),
        wg_aug=np.ascontiguousarray(
            np.concatenate([Wg, bg[None, :], bias[None, :]], 0)),
        w1=W1, w2=W2, b2=b2,
        b1r=np.ascontiguousarray(b1.reshape(NEXP, 16, 128).transpose(0, 2, 1)),
        **cst,
    )
    nc, _ = build_program()
    in_maps = [dict(common, x=np.ascontiguousarray(X[b])) for b in range(8)]
    res = run_bass_kernel_spmd(nc, in_maps, core_ids=list(range(8)))
    return np.stack([r["y"] for r in res.results], 0)


if __name__ == "__main__":
    import reference as ref
    ins = {k: np.asarray(v) for k, v in ref.setup_inputs(0).items()}
    out = kernel(**ins)
    print("out", out.shape, out.dtype)


# version 05
# speedup vs baseline: 1.3205x; 1.0745x over previous
import numpy as np
import ml_dtypes
from contextlib import ExitStack

import concourse.bass as bass
import concourse.mybir as mybir
from concourse.bass_utils import run_bass_kernel_spmd

F32 = mybir.dt.float32
BF16 = mybir.dt.bfloat16
U8 = mybir.dt.uint8
AF = mybir.ActivationFunctionType
ALU = mybir.AluOpType
AX = mybir.AxisListType

L, D = 2048, 1024
NH, HD, BLK, NB = 16, 64, 32, 64
NTC = 16
SCALE = 0.125
BIS_IT = 52
NEXP, DH = 16, 2048
BF = ml_dtypes.bfloat16
ENGINES = ["sync", "gpsimd", "tensor", "scalar", "vector"]

_TAPS = []


class Sem:
    __slots__ = ("h", "n")

    def __init__(self, nc, name):
        self.h = nc.alloc_semaphore(name=name)
        self.n = 0


class Prog:

    def __init__(self, nc):
        self.nc = nc
        self.esem = {}
        self.reset()

    def reset(self):
        self.q = {e: [] for e in ENGINES}

    def wait(self, eng, sem, n):
        if n and n > 0:
            self.q[eng].append(lambda e: e.wait_ge(sem.h, n))

    def run(self, eng, make, sem=None, k=1):
        es = self.esem.get(eng)
        if es is not None:
            self.wait(eng, es, es.n)
            if sem is None:
                sem, k = es, 1
        if sem is None:
            self.q[eng].append(lambda e: make(e))
            return None
        sem.n += k
        kk = k

        def f(e):
            make(e).then_inc(sem.h, kk)
        self.q[eng].append(f)
        return sem.n

    def emit(self, name):
        with self.nc.Block(name) as block:
            for eng in ENGINES:
                fns = self.q[eng]
                if not fns:
                    continue

                def body(e, fns=fns):
                    for f in fns:
                        f(e)
                getattr(block, eng)(body)
        self.reset()


class Ctx:
    pass


def _consts():
    c = {}
    c["identf"] = np.eye(128, dtype=np.float32)
    c["identb"] = np.eye(128, dtype=BF)
    c["onesb"] = np.ones((128, 128), dtype=BF)
    e = np.zeros((64, 2048), np.float32)
    for nb in range(64):
        e[nb, 32 * nb:32 * nb + 32] = 10000.0
    c["ebig"] = np.concatenate([e, e], 0).astype(BF)
    band = (np.abs(np.arange(NB)[:, None] - np.arange(NB)[None, :]) <= 1)
    c["band5k"] = (band * 5000.0).astype(np.float32)
    ab = np.zeros((2048, 64), np.float32)
    for t in range(2048):
        ab[t, t // 32] = 1.0 / 32.0
    c["abm"] = ab.astype(BF)
    i8 = np.zeros((128, 128), np.float32)
    for p in range(128):
        i8[p, (p // 8) * 8:(p // 8) * 8 + 8] = 1.0
    c["i8blk"] = i8
    return c


def _tap_phase(c, P, phase):
    first = True
    for ph, attr, outname in _TAPS:
        if ph != phase:
            continue
        if first:
            for sem in (c.sPE, c.sSC, c.sVE):
                P.wait("sync", sem, sem.n)
            first = False
        t = getattr(c, attr)
        out = c.nc.dram_tensor("tap_" + outname, list(t.shape), t.dtype,
                               kind="ExternalOutput").ap()
        full = t[tuple(slice(None) for _ in t.shape)]
        P.run("sync", lambda e, out=out, full=full: e.dma_start(out=out, in_=full),
              c.sST, 16)


def build_program():
    nc = bass.Bass("TRN2", target_bir_lowering=False)
    c = Ctx()
    c.nc = nc
    P = Prog(nc)

    def din(name, shape, dt=F32):
        return nc.dram_tensor(name, list(shape), dt, kind="ExternalInput").ap()

    c.x = din("x", (L, D))
    c.wqT = din("wqT", (D, D))
    c.wkT = din("wkT", (D, D))
    c.wvT = din("wvT", (D, D))
    c.bq_rs = din("bq_rs", (128, 8))
    c.bk_rs = din("bk_rs", (128, 8))
    c.bvrow = din("bvrow", (1, D))
    c.wg_aug = din("wg_aug", (D + 2, NEXP))
    c.w1 = din("w1", (NEXP, D, DH))
    c.b1r = din("b1r", (NEXP, 128, 16))
    c.w2 = din("w2", (NEXP, DH, D))
    c.b2 = din("b2", (NEXP, D))
    c.identf = din("identf", (128, 128))
    c.identb = din("identb", (128, 128), BF16)
    c.onesb = din("onesb", (128, 128), BF16)
    c.ebig = din("ebig", (128, 2048), BF16)
    c.band5k = din("band5k", (64, 64))
    c.abm = din("abm", (2048, 64), BF16)
    c.i8blk = din("i8blk", (128, 128))
    c.y = nc.dram_tensor("y", [L, D], F32, kind="ExternalOutput").ap()
    c.norm_scr = nc.dram_tensor("norm_scr", [NH, 4096], F32, kind="Internal").ap()
    c.m01_scr = nc.dram_tensor("m01_scr", [NH, 4096], F32, kind="Internal").ap()

    for s in ["sLoad", "sWt", "sX0", "sX1", "sWp0", "sWp1", "sG", "sB1",
              "sPE", "sSC", "sVE", "sST"]:
        setattr(c, s, Sem(nc, s))
    P.esem = {"vector": c.sVE, "scalar": c.sSC}

    with ExitStack() as outer:
        oc = outer.enter_context
        c.identf_sb = oc(nc.sbuf_tensor("identf_sb", [128, 128], F32))
        c.identb_sb = oc(nc.sbuf_tensor("identb_sb", [128, 128], BF16))
        c.onesb_sb = oc(nc.sbuf_tensor("onesb_sb", [128, 128], BF16))
        c.i8_sb = oc(nc.sbuf_tensor("i8_sb", [128, 128], F32))

        with ExitStack() as ab:
            abo = ab.enter_context
            c.QT = abo(nc.sbuf_tensor("QT", [128, 8, L], BF16))
            c.KsT = abo(nc.sbuf_tensor("KsT", [128, 8, L], BF16))
            c.Vaug = abo(nc.sbuf_tensor("Vaug", [128, NTC, NH, HD + 1], BF16))
            c.Vh_aug = abo(nc.sbuf_tensor("Vh_aug", [64, NH, HD + 1], BF16))
            c.band_sb = abo(nc.sbuf_tensor("band_sb", [64, 64], F32))
            c.norm_all = abo(nc.sbuf_tensor("norm_all", [64, NH, 64], F32))
            c.rm10k = abo(nc.sbuf_tensor("rm10k", [64, NH], F32))

            _phase_a(c, P)

            c.ACC = oc(nc.sbuf_tensor("ACC", [128, NTC * D], F32, side="right"))

            _phase_b(c, P, ab)

        with ExitStack() as cd:
            cdo = cd.enter_context
            c.x1T = cdo(nc.sbuf_tensor("x1T", [128, 8, L], BF16))
            c.w1b = [cdo(nc.sbuf_tensor(f"w1b{i}", [128, 8, 1024], BF16)) for i in range(2)]
            c.w2b = [cdo(nc.sbuf_tensor(f"w2b{i}", [128, 8, 1024], BF16)) for i in range(2)]
            c.b1_sb = cdo(nc.sbuf_tensor("b1_sb", [128, 32], F32))
            c.ht_sb = [cdo(nc.sbuf_tensor(f"ht{i}", [128, 256], BF16)) for i in range(3)]
            c.x0t2 = cdo(nc.sbuf_tensor("x0t2", [128, D], F32))
            c.sdump = cdo(nc.sbuf_tensor("sdump", [128, D], F32))
            c.stats = cdo(nc.sbuf_tensor("stats", [128, 32], F32))
            c.top8 = cdo(nc.sbuf_tensor("top8", [128, 8], F32))
            c.lg_sb = cdo(nc.sbuf_tensor("lg_sb", [128, 32], F32))
            c.t16 = cdo(nc.sbuf_tensor("t16", [128, 48], F32))
            c.wg_sb = cdo(nc.sbuf_tensor("wg_sb", [128, 144], BF16))
            c.b2a_sb = cdo(nc.sbuf_tensor("b2a_sb", [NEXP, D], BF16))
            c.cw_sb = cdo(nc.sbuf_tensor("cw_sb", [128, NTC * NEXP], F32))
            c.cwT_sb = cdo(nc.sbuf_tensor("cwT_sb", [NEXP, L], BF16))

            _phase_cde(c, P, cd)

    return nc, c


def _phase_a(c, P):
    nc = c.nc
    with ExitStack() as sc_a:
        ao = sc_a.enter_context
        c.x0T = x0T = ao(nc.sbuf_tensor("x0T", [128, 8, L], BF16))
        wt = ao(nc.sbuf_tensor("wt", [128, 8, 1024], BF16))
        x0b = [ao(nc.sbuf_tensor(f"x0b{i}", [128, D], BF16)) for i in range(2)]
        bvr_sb = ao(nc.sbuf_tensor("bvr_sb", [1, D], BF16))
        c.xh_sb = xh_sb = ao(nc.sbuf_tensor("xh_sb", [64, D], F32))
        abm_sb = ao(nc.sbuf_tensor("abm_sb", [128, NTC, 64], BF16))
        c.XhT = XhT = ao(nc.sbuf_tensor("XhT", [128, 8, 64], BF16))
        c.QhT = QhT = ao(nc.sbuf_tensor("QhT", [128, 8, 64], BF16))
        c.KhT = KhT = ao(nc.sbuf_tensor("KhT", [128, 8, 64], BF16))
        bq_sb = ao(nc.sbuf_tensor("bq_sb", [128, 8], F32))
        bks_sb = ao(nc.sbuf_tensor("bks_sb", [128, 8], F32))

        with ExitStack() as ps:
            po = ps.enter_context
            pk = [po(nc.psum_tensor(f"pA{i}", [128, 512], F32)) for i in range(2)]
            pb16 = [po(nc.psum_tensor(f"pAb{i}", [128, 1024], BF16)) for i in range(2)]
            pxh = [po(nc.psum_tensor(f"pXh{i}", [128, 512], F32)) for i in range(2)]
            plow = [po(nc.psum_tensor(f"plow{i}", [128, 512], F32)) for i in range(2)]

            S, PE, SC, VE = c.sLoad, c.sPE, c.sSC, c.sVE
            Wt, sX = c.sWt, [c.sX0, c.sX1]

            for dst, src in [
                (c.identf_sb[:, :], c.identf), (c.identb_sb[:, :], c.identb),
                (c.onesb_sb[:, :], c.onesb), (c.i8_sb[:, :], c.i8blk),
                (c.band_sb[:, :], c.band5k),
                (bq_sb[:, :], c.bq_rs), (bks_sb[:, :], c.bk_rs),
                (abm_sb[:, :, :], c.abm.rearrange("(tc p) n -> p tc n", p=128)),
            ]:
                P.run("sync", lambda e, d=dst, s=src: e.dma_start(out=d, in_=s), S, 16)
            ld_consts = S.n

            P.run("gpsimd", lambda e: e.dma_start(out=bvr_sb[:, :], in_=c.bvrow), Wt, 16)

            P.wait("vector", S, ld_consts)
            ve_bks = P.run("vector", lambda e: e.tensor_scalar_mul(
                bks_sb[:, :], bks_sb[:, :], SCALE), VE, 1)
            P.run("vector", lambda e: e.memset(c.Vaug[:, :, :, HD:HD + 1], 1.0), VE, 1)
            ve_init = P.run("vector", lambda e: e.memset(
                c.Vh_aug[0:64, :, HD:HD + 1], 32.0), VE, 1)

            ld_x0, pe_tile, sc_x0T = {}, {}, {}
            for tc in range(NTC):
                if tc >= 2:
                    P.wait("gpsimd", PE, pe_tile[tc - 2])
                ld_x0[tc] = P.run("gpsimd", lambda e, tc=tc: e.dma_start(
                    out=x0b[tc % 2][:, :], in_=c.x[128 * tc:128 * (tc + 1), :]),
                    sX[tc % 2], 16)

                P.wait("tensor", sX[tc % 2], ld_x0[tc])
                if tc == 0:
                    P.wait("tensor", S, ld_consts)
                for dd in range(2):
                    P.run("tensor", lambda e, tc=tc, dd=dd: e.matmul(
                        pxh[dd][0:64, :], abm_sb[:, tc, :],
                        x0b[tc % 2][:, 512 * dd:512 * (dd + 1)],
                        start=(tc == 0), stop=(tc == NTC - 1)))
                if tc >= 2:
                    P.wait("tensor", SC, sc_x0T[tc - 2])
                for dc in range(8):
                    pe_tile[tc] = P.run("tensor", lambda e, tc=tc, dc=dc: e.transpose(
                        pb16[tc % 2][:, 128 * dc:128 * (dc + 1)],
                        x0b[tc % 2][:, 128 * dc:128 * (dc + 1)],
                        c.identb_sb[:, :]), PE if dc == 7 else None, 1)

                P.wait("scalar", PE, pe_tile[tc])
                sc_x0T[tc] = P.run("scalar", lambda e, tc=tc: e.copy(
                    x0T[:, :, 128 * tc:128 * (tc + 1)],
                    pb16[tc % 2][:, :].rearrange("p (cc t) -> p cc t", cc=8)), SC, 1)

            ld_wq = P.run("gpsimd", lambda e: e.dma_start(
                out=wt[:, :, :], in_=c.wqT.rearrange("(cc p) d -> p cc d", p=128)), Wt, 16)

            P.wait("scalar", PE, pe_tile[NTC - 1])
            sc_xh = None
            for dd in range(2):
                sc_xh = P.run("scalar", lambda e, dd=dd: e.copy(
                    xh_sb[0:64, 512 * dd:512 * (dd + 1)], pxh[dd][0:64, :]), SC, 1)
            P.wait("tensor", SC, sc_xh)
            pe_xhT = None
            for dc in range(8):
                pe_xhT = P.run("tensor", lambda e, dc=dc: e.transpose(
                    pk[0][:, 64 * dc:64 * (dc + 1)],
                    xh_sb[0:64, 128 * dc:128 * (dc + 1)],
                    c.identf_sb[0:64, 0:64]), PE if dc == 7 else None, 1)
            P.wait("scalar", PE, pe_xhT)
            sc_xhT = P.run("scalar", lambda e: e.copy(
                XhT[:, :, :], pk[0][:, :].rearrange("p (cc n) -> p cc n", cc=8)), SC, 1)

            def proj_T(outT, bias_sb, scale, n_free, nt, w_ld, prev_sc):
                sc_evac = {}
                first = True
                for dc in range(8):
                    for tt in range(nt):
                        idx = dc * nt + tt
                        bank = pk[idx % 2]
                        if first:
                            P.wait("tensor", Wt, w_ld)
                            P.wait("tensor", SC, sc_xhT)
                            first = False
                        if idx >= 2:
                            P.wait("tensor", SC, sc_evac[idx - 2])
                        elif prev_sc:
                            P.wait("tensor", SC, prev_sc)
                        pe_g = None
                        for cc in range(8):
                            if n_free == 512:
                                def mk(e, dc=dc, tt=tt, cc=cc, bank=bank):
                                    return e.matmul(
                                        bank[:, 0:512],
                                        wt[:, cc, 128 * dc:128 * (dc + 1)],
                                        x0T[:, cc, 512 * tt:512 * (tt + 1)],
                                        start=(cc == 0), stop=(cc == 7))
                            else:
                                def mk(e, dc=dc, cc=cc, bank=bank):
                                    return e.matmul(
                                        bank[:, 0:64],
                                        wt[:, cc, 128 * dc:128 * (dc + 1)],
                                        XhT[:, cc, :],
                                        start=(cc == 0), stop=(cc == 7))
                            pe_g = P.run("tensor", mk, PE if cc == 7 else None, 1)
                        P.wait("scalar", PE, pe_g)
                        sc_evac[idx] = P.run("scalar",
                            lambda e, dc=dc, tt=tt, bank=bank: e.activation(
                                outT[:, dc, n_free * tt:n_free * (tt + 1)],
                                bank[:, 0:n_free], AF.Identity,
                                bias=bias_sb[:, dc:dc + 1], scale=scale), SC, 1)
                return sc_evac[8 * nt - 1]

            P.wait("scalar", VE, ve_bks)
            sc_q = proj_T(c.QT, bq_sb, 1.0, 512, 4, ld_wq, sc_xhT)
            sc_qh = proj_T(QhT, bq_sb, 1.0, 64, 1, ld_wq, sc_q)
            pe_after_q = PE.n

            P.wait("gpsimd", PE, pe_after_q)
            ld_wk = P.run("gpsimd", lambda e: e.dma_start(
                out=wt[:, :, :], in_=c.wkT.rearrange("(cc p) d -> p cc d", p=128)), Wt, 16)

            sc_k = proj_T(c.KsT, bks_sb, SCALE, 512, 4, ld_wk, sc_qh)
            sc_kh = proj_T(KhT, bks_sb, SCALE, 64, 1, ld_wk, sc_k)
            pe_after_k = PE.n

            P.wait("gpsimd", PE, pe_after_k)
            ld_wv = P.run("gpsimd", lambda e: e.dma_start(
                out=wt[:, :, :], in_=c.wvT.rearrange("(cc p) d -> p cc d", p=128)), Wt, 16)

            sc_v = {}
            idx = 0
            for tc in range(NTC):
                for dd in range(2):
                    bank = pk[idx % 2]
                    if idx == 0:
                        P.wait("tensor", Wt, ld_wv)
                        P.wait("tensor", VE, ve_init)
                    if idx >= 2:
                        P.wait("tensor", SC, sc_v[idx - 2])
                    elif idx < 2:
                        P.wait("tensor", SC, sc_kh)
                    for cc in range(8):
                        P.run("tensor", lambda e, tc=tc, dd=dd, cc=cc, bank=bank: e.matmul(
                            bank[:, :],
                            x0T[:, cc, 128 * tc:128 * (tc + 1)],
                            wt[:, cc, 512 * dd:512 * (dd + 1)],
                            start=(cc == 0), stop=False))
                    pe_g = P.run("tensor", lambda e, dd=dd, bank=bank: e.matmul(
                        bank[:, :], c.onesb_sb[0:1, 0:128],
                        bvr_sb[0:1, 512 * dd:512 * (dd + 1)],
                        start=False, stop=True), PE, 1)
                    P.wait("scalar", PE, pe_g)
                    sc_v[idx] = P.run("scalar", lambda e, tc=tc, dd=dd, bank=bank: e.copy(
                        c.Vaug[:, tc, 8 * dd:8 * (dd + 1), 0:HD],
                        bank[:, :].rearrange("p (h n) -> p h n", h=8)), SC, 1)
                    idx += 1

            sc_vh = {}
            for dd in range(2):
                bank = pk[dd % 2]
                P.wait("tensor", SC, sc_v[idx - 2 + dd])
                for cc in range(8):
                    P.run("tensor", lambda e, dd=dd, cc=cc, bank=bank: e.matmul(
                        bank[0:64, :], XhT[:, cc, :],
                        wt[:, cc, 512 * dd:512 * (dd + 1)],
                        start=(cc == 0), stop=False))
                pe_g = P.run("tensor", lambda e, dd=dd, bank=bank: e.matmul(
                    bank[0:64, :], c.onesb_sb[0:1, 0:64],
                    bvr_sb[0:1, 512 * dd:512 * (dd + 1)],
                    start=False, stop=True), PE, 1)
                P.wait("scalar", PE, pe_g)
                sc_vh[dd] = P.run("scalar", lambda e, dd=dd, bank=bank: e.activation(
                    c.Vh_aug[0:64, 8 * dd:8 * (dd + 1), 0:HD],
                    bank[0:64, :].rearrange("p (h n) -> p h n", h=8),
                    AF.Identity, scale=32.0), SC, 1)

            ve_low = {}
            for h in range(NH):
                b = 64 * (h % 2)
                hc = h // 2
                bank = plow[h % 2]
                if h == 0:
                    P.wait("tensor", SC, max(sc_qh, sc_kh))
                if h >= 2:
                    P.wait("tensor", VE, ve_low[h - 2])
                pe_l = P.run("tensor", lambda e, b=b, hc=hc, bank=bank: e.matmul(
                    bank[0:64, 0:64],
                    QhT[b:b + 64, hc, :], KhT[b:b + 64, hc, :],
                    start=True, stop=True), PE, 1)
                P.wait("vector", PE, pe_l)
                ve_rm = P.run("vector", lambda e, h=h, bank=bank: e.reduce_max(
                    c.rm10k[0:64, h:h + 1], bank[0:64, 0:64], axis=AX.X), VE, 1)
                P.wait("vector", VE, ve_rm)
                P.run("vector", lambda e, h=h, bank=bank: e.scalar_tensor_tensor(
                    c.norm_all[0:64, h, :], bank[0:64, 0:64],
                    c.rm10k[0:64, h:h + 1], c.band_sb[:, :],
                    ALU.subtract, ALU.add), VE, 1)
                ve_low[h] = P.run("vector", lambda e, h=h: e.tensor_scalar_add(
                    c.rm10k[0:64, h:h + 1], c.rm10k[0:64, h:h + 1], -10000.0), VE, 1)

            P.wait("sync", VE, ve_low[NH - 1])
            c.ld_norm_scr = P.run("sync", lambda e: e.dma_start(
                out=c.norm_scr.rearrange("h (q k) -> q h k", q=64),
                in_=c.norm_all[:, :, :]), S, 16)

            _tap_phase(c, P, "A")
            P.emit("phaseA")


def _phase_b(c, P, scope):
    nc = c.nc
    bo = scope.enter_context
    c.ebig_sb = bo(nc.sbuf_tensor("ebig_sb", [128, 2048], BF16))
    c.M01cT = bo(nc.sbuf_tensor("M01cT", [128, NH, 64], BF16))
    c.lowWT = bo(nc.sbuf_tensor("lowWT", [64, NH, 64], BF16))
    c.nfl = nfl = bo(nc.sbuf_tensor("nfl", [128, 512], F32))
    tmpf = bo(nc.sbuf_tensor("tmpf", [128, 512], F32))
    part = bo(nc.sbuf_tensor("part", [128, 1], F32))
    c.lo = lo = bo(nc.sbuf_tensor("lo", [128, 1], F32))
    hi = bo(nc.sbuf_tensor("hi", [128, 1], F32))
    mid = bo(nc.sbuf_tensor("mid", [128, 1], F32))
    mU = bo(nc.sbuf_tensor("mU", [128, 1], U8))
    mU2 = bo(nc.sbuf_tensor("mU2", [128, 1], U8))
    c.m01fl = m01fl = bo(nc.sbuf_tensor("m01fl", [128, 512], F32))
    c.m01c_sb = m01c = bo(nc.sbuf_tensor("m01c", [64, NH, 64], F32))
    t64 = bo(nc.sbuf_tensor("t64", [64, 64], F32))
    lwf = bo(nc.sbuf_tensor("lwf", [64, 64], F32))
    c.PT = PT = [bo(nc.sbuf_tensor(f"PT{i}", [128, 1024], BF16)) for i in range(2)]
    c.attnT_sb = attnT = [bo(nc.sbuf_tensor(f"attnT{i}", [65, 1024], F32)) for i in range(2)]
    dden = bo(nc.sbuf_tensor("dden", [128, 2], F32))

    with ExitStack() as ps:
        po = ps.enter_context
        sb = [po(nc.psum_tensor(f"pS{i}", [128, 1024], F32)) for i in range(2)]
        av = [po(nc.psum_tensor(f"pV{i}", [128, 512], F32)) for i in range(4)]

        S, PE, SC, VE = c.sLoad, c.sPE, c.sSC, c.sVE

        P.wait("sync", c.sST, c.sST.n)
        P.wait("gpsimd", c.sST, c.sST.n)
        ld_ebig = P.run("sync", lambda e: e.dma_start(
            out=c.ebig_sb[:, :], in_=c.ebig), S, 16)

        P.wait("sync", S, max(c.ld_norm_scr, ld_ebig))
        ld_flat = P.run("sync", lambda e: e.dma_start(
            out=nfl[:, :], in_=c.norm_scr.rearrange("h (j f) -> (h j) f", j=8)), S, 16)

        P.wait("vector", S, ld_flat)
        P.run("vector", lambda e: e.memset(lo[:, :], -30.0))
        P.run("vector", lambda e: e.memset(hi[:, :], 1.0))
        P.run("vector", lambda e: e.memset(mid[:, :], -14.5), VE, 1)
        ve_upd = VE.n
        for it in range(BIS_IT):
            P.wait("vector", VE, ve_upd)
            P.run("vector", lambda e: e.tensor_scalar(
                tmpf[:, :], nfl[:, :], mid[:, 0:1], None, ALU.is_ge))
            ve_cmp = P.run("vector", lambda e: e.reduce_sum(
                part[:, :], tmpf[:, :], axis=AX.X), VE, 1)
            P.wait("tensor", VE, ve_cmp)
            if it > 0:
                P.wait("tensor", VE, ve_upd)
            pe_bis = P.run("tensor", lambda e: e.matmul(
                sb[0][:, 0:1], c.i8_sb[:, :], part[:, 0:1],
                start=True, stop=True), PE, 1)
            P.wait("vector", PE, pe_bis)
            P.run("vector", lambda e: e.tensor_scalar(
                mU[:, :], sb[0][:, 0:1], 1024.0, None, ALU.is_ge))
            P.run("vector", lambda e: e.tensor_scalar(
                mU2[:, :], sb[0][:, 0:1], 1024.0, None, ALU.is_lt))
            P.run("vector", lambda e: e.copy_predicated(lo[:, 0:1], mU[:, 0:1], mid[:, 0:1]))
            P.run("vector", lambda e: e.copy_predicated(hi[:, 0:1], mU2[:, 0:1], mid[:, 0:1]))
            P.run("vector", lambda e: e.tensor_add(mid[:, 0:1], lo[:, 0:1], hi[:, 0:1]))
            ve_upd = P.run("vector", lambda e: e.tensor_scalar_mul(
                mid[:, 0:1], mid[:, 0:1], 0.5), VE, 1)
        P.wait("vector", VE, ve_upd)
        ve_m01 = P.run("vector", lambda e: e.tensor_scalar(
            m01fl[:, :], nfl[:, :], lo[:, 0:1], 1.0, ALU.is_ge, ALU.subtract), VE, 1)

        P.wait("sync", VE, ve_m01)
        st_m = P.run("sync", lambda e: e.dma_start(
            out=c.m01_scr.rearrange("h (j f) -> (h j) f", j=8), in_=m01fl[:, :]), S, 16)
        P.wait("sync", S, st_m)
        ld_m01 = P.run("sync", lambda e: e.dma_start(
            out=m01c[:, :, :], in_=c.m01_scr.rearrange("h (q k) -> q h k", q=64)), S, 16)

        sc_mT = {}
        P.wait("vector", S, ld_m01)
        for h in range(NH):
            ve_a = P.run("vector", lambda e, h=h: e.scalar_tensor_tensor(
                t64[:, :], c.norm_all[0:64, h, :], c.rm10k[0:64, h:h + 1],
                c.band_sb[:, :], ALU.add, ALU.subtract), VE, 1)
            P.wait("vector", VE, ve_a)
            ve_b = P.run("vector", lambda e, h=h: e.scalar_tensor_tensor(
                t64[:, :], m01c[0:64, h, :], -10000.0, t64[:, :],
                ALU.mult, ALU.add), VE, 1)
            P.wait("scalar", VE, ve_b)
            sc_e = P.run("scalar", lambda e: e.activation(
                lwf[:, :], t64[:, :], AF.Exp), SC, 1)
            P.wait("tensor", SC, sc_e)
            P.run("tensor", lambda e: e.transpose(
                sb[0][0:64, 0:64], lwf[:, :], c.identf_sb[0:64, 0:64]))
            pe_t = P.run("tensor", lambda e, h=h: e.transpose(
                sb[1][0:64, 0:64], m01c[0:64, h, :], c.identf_sb[0:64, 0:64]), PE, 1)
            P.wait("scalar", PE, pe_t)
            P.run("scalar", lambda e, h=h: e.copy(
                c.lowWT[0:64, h, :], sb[0][0:64, 0:64]), SC, 1)
            sc_mT[h] = P.run("scalar", lambda e, h=h: e.copy(
                c.M01cT[0:64, h, :], sb[1][0:64, 0:64]), SC, 1)
            P.wait("vector", PE, pe_t)

        P.wait("sync", SC, sc_mT[NH - 1])
        ld_dup = P.run("sync", lambda e: e.dma_start(
            out=c.M01cT[64:128, :, :], in_=c.M01cT[0:64, :, :]), S, 16)

        sc_pt = {}
        pe_av = {}
        dv = {}

        def _emit_av(h, p, kc):
            P.wait("tensor", SC, sc_pt[(h, p, kc)])
            for qq in range(2):
                pe_av[(h, p, kc)] = P.run("tensor",
                    lambda e, h=h, kc=kc, p=p, qq=qq: e.matmul(
                        av[2 * p + qq][0:65, :],
                        c.Vaug[:, kc, h, :],
                        PT[kc % 2][:, 512 * qq:512 * (qq + 1)],
                        start=False, stop=(kc == NTC - 1)),
                    PE if qq == 1 else None, 1)

        def _tail_piece(hp, pp, i):
            at = attnT[pp]
            if i == 0:
                P.wait("scalar", PE, pe_av[(hp, pp, NTC - 1)])
                for qq in range(2):
                    tail_ae[(hp, pp)] = P.run("scalar", lambda e, pp=pp, qq=qq, at=at: e.copy(
                        at[0:65, 512 * qq:512 * (qq + 1)],
                        av[2 * pp + qq][0:65, :]), SC, 1)
                P.wait("tensor", SC, tail_ae[(hp, pp)])
            t = 8 * pp + i
            if i >= 2:
                P.wait("tensor", VE, dv[(hp, pp, i - 2)])
            pe_tr = P.run("tensor", lambda e, i=i, pp=pp, at=at: e.transpose(
                av[2 * pp + i % 2][:, 0:65], at[0:65, 128 * i:128 * (i + 1)],
                c.identf_sb[0:65, 0:65]), PE, 1)
            P.wait("vector", PE, pe_tr)
            P.run("vector", lambda e, i=i, pp=pp: e.tensor_scalar_add(
                dden[:, 0:1], av[2 * pp + i % 2][:, 64:65], 1e-6))
            ve_rc = P.run("vector", lambda e: e.reciprocal(
                dden[:, 1:2], dden[:, 0:1]), VE, 1)
            P.wait("vector", VE, ve_rc)
            dv[(hp, pp, i)] = P.run("vector",
                lambda e, t=t, hp=hp, i=i, pp=pp: e.tensor_scalar_mul(
                    c.ACC[:, 1024 * t + 64 * hp:1024 * t + 64 * hp + 64],
                    av[2 * pp + i % 2][:, 0:64], dden[:, 1:2]), VE, 1)

        tail_ae = {}
        pending = None
        for h in range(NH):
            b = 64 * (h % 2)
            hc = h // 2
            for p in range(2):
                P.wait("tensor", SC, sc_mT[h])
                if h == 0 and p == 0:
                    P.wait("tensor", S, ld_dup)
                for qq in range(2):
                    qt = 2 * p + qq
                    if h > 0:
                        P.wait("tensor", VE, dv[(h - 1, p, 6 + qq)])
                    P.run("tensor", lambda e, h=h, qt=qt, p=p, qq=qq: e.matmul(
                        av[2 * p + qq][0:65, :],
                        c.Vh_aug[0:64, h, :],
                        c.lowWT[0:64, h, 16 * qt:16 * (qt + 1)]
                        .rearrange("p (f o) -> p f o", o=1).to_broadcast([64, 16, 32]),
                        start=True, stop=False))
                for kc in range(NTC):
                    key_prev = (h, p, kc - 2) if kc >= 2 else (
                        (h, p - 1, NTC - 2 + kc) if p > 0 else
                        ((h - 1, 1, NTC - 2 + kc) if h > 0 else None))
                    if key_prev is not None and key_prev in sc_pt:
                        P.wait("tensor", SC, sc_pt[key_prev])
                    pe_s = None
                    for j in range(2):
                        qt = 2 * p + j
                        P.run("tensor", lambda e, b=b, hc=hc, kc=kc, qt=qt, j=j: e.matmul(
                            sb[kc % 2][:, 512 * j:512 * (j + 1)],
                            c.KsT[b:b + 64, hc, 128 * kc:128 * (kc + 1)],
                            c.QT[b:b + 64, hc, 512 * qt:512 * (qt + 1)],
                            start=True, stop=False))
                    for j in range(2):
                        qt = 2 * p + j
                        pe_s = P.run("tensor", lambda e, b=b, h=h, kc=kc, qt=qt, j=j: e.matmul(
                            sb[kc % 2][:, 512 * j:512 * (j + 1)],
                            c.ebig_sb[b:b + 64, 128 * kc:128 * (kc + 1)],
                            c.M01cT[b:b + 64, h, 16 * qt:16 * (qt + 1)]
                            .rearrange("p (f o) -> p f o", o=1).to_broadcast([64, 16, 32]),
                            start=False, stop=True), PE if j == 1 else None, 1)
                    P.wait("scalar", PE, pe_s)
                    keyv = key_prev
                    if keyv is not None and keyv in pe_av:
                        P.wait("scalar", PE, pe_av[keyv])
                    sc_pt[(h, p, kc)] = P.run(
                        "scalar", lambda e, kc=kc: e.activation(
                            PT[kc % 2][:, :], sb[kc % 2][:, :], AF.Exp), SC, 1)
                    if kc >= 1:
                        _emit_av(h, p, kc - 1)
                    if pending is not None and kc % 2 == 1:
                        _tail_piece(pending[0], pending[1], kc // 2)
                _emit_av(h, p, NTC - 1)
                pending = (h, p)
        for i in range(8):
            _tail_piece(NH - 1, 1, i)

        _tap_phase(c, P, "B")
        P.emit("phaseB")


def _phase_cde(c, P, scope):
    nc = c.nc

    with ExitStack() as ps:
        po = ps.enter_context
        ptr = [po(nc.psum_tensor(f"pT{i}", [128, 512], F32)) for i in range(4)]
        plg = po(nc.psum_tensor("plg", [128, 512], F32))
        pcw = po(nc.psum_tensor("pcw", [128, 512], F32))
        pb2 = [po(nc.psum_tensor(f"pB2_{i}", [128, 512], F32)) for i in range(2)]

        S, PE, SC, VE = c.sLoad, c.sPE, c.sSC, c.sVE
        G, sWp = c.sG, [c.sWp0, c.sWp1]

        def ln_stats(tc, col0):
            st = c.stats
            ve_m = P.run("vector", lambda e, tc=tc, col0=col0: e.reduce_sum(
                st[:, col0:col0 + 1], c.ACC[:, 1024 * tc:1024 * (tc + 1)],
                axis=AX.X), VE, 1)
            P.wait("scalar", VE, ve_m)
            sc_sq = P.run("scalar", lambda e, tc=tc, col0=col0: e.activation(
                c.sdump[:, :], c.ACC[:, 1024 * tc:1024 * (tc + 1)], AF.Square,
                accum_out=st[:, col0 + 1:col0 + 2]), SC, 1)
            P.wait("vector", SC, sc_sq)
            P.run("vector", lambda e, col0=col0: e.tensor_scalar_mul(
                st[:, col0 + 2:col0 + 3], st[:, col0:col0 + 1], 1.0 / 1024.0))
            P.run("vector", lambda e, col0=col0: e.tensor_mul(
                st[:, col0 + 3:col0 + 4], st[:, col0 + 2:col0 + 3], st[:, col0 + 2:col0 + 3]))
            P.run("vector", lambda e, col0=col0: e.scalar_tensor_tensor(
                st[:, col0 + 4:col0 + 5], st[:, col0 + 1:col0 + 2], 1.0 / 1024.0,
                st[:, col0 + 3:col0 + 4], ALU.mult, ALU.subtract))
            ve_v = P.run("vector", lambda e, col0=col0: e.tensor_scalar_add(
                st[:, col0 + 4:col0 + 5], st[:, col0 + 4:col0 + 5], 1e-5), VE, 1)
            P.wait("scalar", VE, ve_v)
            P.run("scalar", lambda e, col0=col0: e.activation(
                st[:, col0 + 5:col0 + 6], st[:, col0 + 4:col0 + 5], AF.Ln))
            sc_r = P.run("scalar", lambda e, col0=col0: e.activation(
                st[:, col0 + 6:col0 + 7], st[:, col0 + 5:col0 + 6], AF.Exp,
                scale=-0.5), SC, 1)
            P.wait("vector", SC, sc_r)
            ve_nm = P.run("vector", lambda e, col0=col0: e.scalar_tensor_tensor(
                st[:, col0 + 7:col0 + 8], st[:, col0 + 2:col0 + 3], -1.0,
                st[:, col0 + 6:col0 + 7], ALU.mult, ALU.mult), VE, 1)
            P.wait("vector", VE, ve_nm)
            return st[:, col0 + 6:col0 + 7], st[:, col0 + 7:col0 + 8]

        P.wait("sync", c.sST, c.sST.n)
        P.wait("gpsimd", c.sST, c.sST.n)
        P.run("gpsimd", lambda e: e.dma_start(
            out=c.wg_sb[:, 0:128].rearrange("p (cc e) -> p cc e", cc=8),
            in_=c.wg_aug[0:1024, :].rearrange("(cc p) e -> p cc e", p=128)), G, 16)
        P.run("gpsimd", lambda e: e.dma_start(
            out=c.wg_sb[0:2, 128:144], in_=c.wg_aug[1024:1026, :]), G, 16)
        c.ld_g = P.run("gpsimd", lambda e: e.dma_start(
            out=c.b2a_sb[:, :], in_=c.b2), G, 16)
        c.wld = {}
        for s in range(2):
            c.wld[(0, s, "w1")] = P.run("gpsimd", lambda e, s=s: e.dma_start(
                out=c.w1b[s][:, :, :],
                in_=c.w1[0, :, 1024 * s:1024 * (s + 1)]
                .rearrange("(cc p) hh -> p cc hh", p=128)), sWp[s], 16)
            c.wld[(0, s, "w2")] = P.run("gpsimd", lambda e, s=s: e.dma_start(
                out=c.w2b[s][:, :, :],
                in_=c.w2[0, 1024 * s:1024 * (s + 1), :]
                .rearrange("(cc p) dd -> p cc dd", p=128)), sWp[s], 16)
        c.ld_b1 = {0: P.run("sync", lambda e: e.dma_start(
            out=c.b1_sb[:, 0:16], in_=c.b1r[0, :, :]), c.sB1, 16)}

        ld2, ve_res, sc_x1T, pe_x1t, ve_b2, ve_cw = {}, {}, {}, {}, {}, {}
        for tc in range(NTC):
            col0 = 8 * (tc % 2)
            if tc >= 1:
                P.wait("sync", VE, ve_res[tc - 1])
            ld2[tc] = P.run("sync", lambda e, tc=tc: e.dma_start(
                out=c.x0t2[:, :], in_=c.x[128 * tc:128 * (tc + 1), :]), S, 16)
            P.wait("vector", S, ld2[tc])
            ve_res[tc] = P.run("vector", lambda e, tc=tc: e.tensor_add(
                c.ACC[:, 1024 * tc:1024 * (tc + 1)],
                c.ACC[:, 1024 * tc:1024 * (tc + 1)], c.x0t2[:, :]), VE, 1)
            rstd, negmr = ln_stats(tc, col0)
            ve_x1 = P.run("vector", lambda e, tc=tc, rstd=rstd, negmr=negmr: e.tensor_scalar(
                c.ACC[:, 1024 * tc:1024 * (tc + 1)],
                c.ACC[:, 1024 * tc:1024 * (tc + 1)],
                rstd, negmr, ALU.mult, ALU.add), VE, 1)

            P.wait("tensor", VE, ve_x1)
            if tc >= 2:
                P.wait("tensor", SC, sc_x1T[tc - 2])
            for dc in range(8):
                bank = ptr[2 * (tc % 2) + dc // 4]
                pe_x1t[tc] = P.run("tensor", lambda e, tc=tc, dc=dc, bank=bank: e.transpose(
                    bank[:, 128 * (dc % 4):128 * (dc % 4 + 1)],
                    c.ACC[:, 1024 * tc + 128 * dc:1024 * tc + 128 * (dc + 1)],
                    c.identf_sb[:, :]), PE if dc == 7 else None, 1)
            P.wait("scalar", PE, pe_x1t[tc])
            for half in range(2):
                bank = ptr[2 * (tc % 2) + half]
                sc_x1T[tc] = P.run("scalar", lambda e, tc=tc, half=half, bank=bank: e.copy(
                    c.x1T[:, 4 * half:4 * (half + 1), 128 * tc:128 * (tc + 1)],
                    bank[:, :].rearrange("p (cc t) -> p cc t", cc=4)), SC, 1)

            P.wait("tensor", SC, sc_x1T[tc])
            if tc == 0:
                P.wait("tensor", G, c.ld_g)
            for cc in range(8):
                P.run("tensor", lambda e, tc=tc, cc=cc: e.matmul(
                    plg[:, 0:16],
                    c.x1T[:, cc, 128 * tc:128 * (tc + 1)],
                    c.wg_sb[:, 16 * cc:16 * (cc + 1)],
                    start=(cc == 0), stop=False))
            pe_lg = P.run("tensor", lambda e: e.matmul(
                plg[:, 0:16], c.onesb_sb[0:2, 0:128], c.wg_sb[0:2, 128:144],
                start=False, stop=True), PE, 1)
            P.wait("scalar", PE, pe_lg)
            sc_lg = P.run("scalar", lambda e, tc=tc: e.copy(
                c.lg_sb[:, 16 * (tc % 2):16 * (tc % 2) + 16], plg[:, 0:16]), SC, 1)

            P.wait("vector", SC, sc_lg)
            lg = c.lg_sb[:, 16 * (tc % 2):16 * (tc % 2) + 16]
            cw = c.cw_sb[:, 16 * tc:16 * (tc + 1)]
            t16 = c.t16
            ve_mx = P.run("vector", lambda e, lg=lg: e.max(
                out=c.top8[:, :], in_=lg), VE, 1)
            ve_d = P.run("vector", lambda e: e.tensor_sub(
                c.stats[:, 16:17], c.top8[:, 1:2], c.top8[:, 0:1]), VE, 1)
            P.wait("scalar", VE, ve_d)
            sc_e2 = P.run("scalar", lambda e: e.activation(
                c.stats[:, 17:18], c.stats[:, 16:17], AF.Exp), SC, 1)
            P.wait("vector", SC, sc_e2)
            P.run("vector", lambda e: e.tensor_scalar_add(
                c.stats[:, 18:19], c.stats[:, 17:18], 1.0))
            ve_rc = P.run("vector", lambda e: e.reciprocal(
                c.stats[:, 19:20], c.stats[:, 18:19]), VE, 1)
            ve_w2 = P.run("vector", lambda e: e.tensor_mul(
                c.stats[:, 20:21], c.stats[:, 17:18], c.stats[:, 19:20]), VE, 1)
            P.wait("vector", VE, ve_mx)
            P.run("vector", lambda e, lg=lg: e.tensor_scalar(
                t16[:, 0:16], lg, c.top8[:, 0:1], None, ALU.is_equal))
            P.run("vector", lambda e, lg=lg: e.scalar_tensor_tensor(
                t16[:, 16:32], t16[:, 0:16], -1e30, lg, ALU.mult, ALU.add))
            P.run("vector", lambda e: e.tensor_scalar(
                t16[:, 32:48], t16[:, 16:32], c.top8[:, 1:2], None, ALU.is_equal))
            P.wait("vector", VE, ve_w2)
            P.run("vector", lambda e, cw=cw: e.tensor_scalar(
                cw, t16[:, 0:16], c.stats[:, 19:20], None, ALU.mult))
            ve_cw[tc] = P.run("vector", lambda e, cw=cw: e.scalar_tensor_tensor(
                cw, t16[:, 32:48], c.stats[:, 20:21], cw, ALU.mult, ALU.add), VE, 1)

            P.wait("tensor", VE, ve_cw[tc])
            pe_cw = P.run("tensor", lambda e, cw=cw: e.transpose(
                pcw[0:16, 0:128], cw, c.identf_sb[:, :]), PE, 1)
            P.wait("scalar", PE, pe_cw)
            sc_cwT = P.run("scalar", lambda e, tc=tc: e.copy(
                c.cwT_sb[0:16, 128 * tc:128 * (tc + 1)], pcw[0:16, 0:128]), SC, 1)
            P.wait("tensor", SC, sc_cwT)
            if tc == 0:
                P.wait("tensor", G, c.ld_g)
            if tc >= 1:
                P.wait("tensor", VE, ve_b2[tc - 1])
            pe_b2 = None
            for dd in range(2):
                pe_b2 = P.run("tensor", lambda e, tc=tc, dd=dd: e.matmul(
                    pb2[dd][:, :], c.cwT_sb[0:16, 128 * tc:128 * (tc + 1)],
                    c.b2a_sb[0:16, 512 * dd:512 * (dd + 1)],
                    start=True, stop=True), PE if dd == 1 else None, 1)
            P.wait("vector", PE, pe_b2)
            for dd in range(2):
                ve_b2[tc] = P.run("vector", lambda e, tc=tc, dd=dd: e.tensor_add(
                    c.ACC[:, 1024 * tc + 512 * dd:1024 * tc + 512 * (dd + 1)],
                    c.ACC[:, 1024 * tc + 512 * dd:1024 * tc + 512 * (dd + 1)],
                    pb2[dd][:, :]), VE if dd == 1 else None, 1)

        _tap_phase(c, P, "C")
        P.emit("phaseC")

        phb = [plg, pcw]
        ybk = ptr

        sc_relu, pe_h, pe_y, pe_yh, ve_y = {}, {}, {}, {}, {}

        def emit_y(e_i, s, tc, Hs, par, first):
            k = (e_i, s, tc, Hs)
            P.wait("tensor", SC, sc_relu[k])
            if Hs == 0:
                if tc > 0:
                    P.wait("tensor", VE, ve_y[(e_i, s, tc - 1)])
                elif not first:
                    ep, sp = divmod(2 * e_i + s - 1, 2)
                    P.wait("tensor", VE, ve_y[(ep, sp, 7)])
            for tsub in range(2):
                for dd in range(2):
                    m = P.run("tensor",
                        lambda e, par=par, Hs=Hs, tsub=tsub, dd=dd: e.matmul(
                            ybk[2 * tsub + dd][:, :],
                            c.ht_sb[Hs % 3][:, 128 * tsub:128 * (tsub + 1)],
                            c.w2b[par][:, Hs, 512 * dd:512 * (dd + 1)],
                            start=(Hs == 0), stop=(Hs == 7)),
                        PE if (tsub == 1 and dd == 1) else None, 1)
            pe_yh[k] = m
            pe_y[(e_i, s, tc)] = m

        def _hk_prev(e_i, s, tc, Hs, d, first):
            idx = (2 * e_i + s) * 64 + tc * 8 + Hs - d
            if idx < 0:
                return None
            es, r = divmod(idx, 64)
            ep, sp = divmod(es, 2)
            return (ep, sp, r // 8, r % 8)

        for e_i in range(NEXP):
            for s in range(2):
                par = (2 * e_i + s) % 2
                first = (e_i == 0 and s == 0)
                for tc in range(8):
                    for Hs in range(8):
                        Hglob = 8 * s + Hs
                        hb = phb[Hs % 2]
                        k = (e_i, s, tc, Hs)
                        if tc == 0 and Hs == 0:
                            P.wait("tensor", sWp[par], c.wld[(e_i, s, "w2")])
                            P.wait("tensor", c.sB1, c.ld_b1[e_i])
                        kprev = _hk_prev(e_i, s, tc, Hs, 2, e_i == 0 and s == 0)
                        if kprev is not None and kprev in sc_relu:
                            P.wait("tensor", SC, sc_relu[kprev])
                        for cc in range(8):
                            pe_h[k] = P.run("tensor",
                                lambda e, par=par, Hs=Hs, tc=tc, cc=cc, hb=hb: e.matmul(
                                    hb[:, 0:256],
                                    c.w1b[par][:, cc, 128 * Hs:128 * (Hs + 1)],
                                    c.x1T[:, cc, 256 * tc:256 * (tc + 1)],
                                    start=(cc == 0), stop=(cc == 7)),
                                PE if cc == 7 else None, 1)
                        P.wait("scalar", PE, pe_h[k])
                        kht = _hk_prev(e_i, s, tc, Hs, 3, e_i == 0 and s == 0)
                        if kht is not None and kht in pe_yh:
                            P.wait("scalar", PE, pe_yh[kht])
                        sc_relu[k] = P.run("scalar",
                            lambda e, e_i=e_i, Hglob=Hglob, Hs=Hs, hb=hb: e.activation(
                                c.ht_sb[Hs % 3][:, :], hb[:, 0:256], AF.Relu,
                                bias=c.b1_sb[:, 16 * (e_i % 2) + Hglob:
                                             16 * (e_i % 2) + Hglob + 1]), SC, 1)
                        if Hs >= 2:
                            emit_y(e_i, s, tc, Hs - 2, par, first)
                    emit_y(e_i, s, tc, 6, par, first)
                    emit_y(e_i, s, tc, 7, par, first)
                    P.wait("vector", PE, pe_y[(e_i, s, tc)])
                    for tsub in range(2):
                        tcg = 2 * tc + tsub
                        for dd in range(2):
                            ve_y[(e_i, s, tc)] = P.run("vector",
                                lambda e, e_i=e_i, tcg=tcg, tsub=tsub, dd=dd: e.scalar_tensor_tensor(
                                    c.ACC[:, 1024 * tcg + 512 * dd:1024 * tcg + 512 * (dd + 1)],
                                    ybk[2 * tsub + dd][:, :],
                                    c.cw_sb[:, 16 * tcg + e_i:16 * tcg + e_i + 1],
                                    c.ACC[:, 1024 * tcg + 512 * dd:1024 * tcg + 512 * (dd + 1)],
                                    ALU.mult, ALU.add),
                                VE if (tsub == 1 and dd == 1) else None, 1)

                nxt = 2 * e_i + s + 2
                if nxt < 2 * NEXP:
                    en, sn = divmod(nxt, 2)
                    P.wait("gpsimd", PE, pe_y[(e_i, s, 7)])
                    c.wld[(en, sn, "w1")] = P.run("gpsimd",
                        lambda e, en=en, sn=sn, par=par: e.dma_start(
                            out=c.w1b[par][:, :, :],
                            in_=c.w1[en, :, 1024 * sn:1024 * (sn + 1)]
                            .rearrange("(cc p) hh -> p cc hh", p=128)), sWp[par], 16)
                    c.wld[(en, sn, "w2")] = P.run("gpsimd",
                        lambda e, en=en, sn=sn, par=par: e.dma_start(
                            out=c.w2b[par][:, :, :],
                            in_=c.w2[en, 1024 * sn:1024 * (sn + 1), :]
                            .rearrange("(cc p) dd -> p cc dd", p=128)), sWp[par], 16)
                    if sn == 0:
                        if en >= 2:
                            P.wait("sync", SC, sc_relu[(en - 2, 1, 7, 7)])
                        P.wait("sync", c.sB1, c.ld_b1[en - 1])
                        c.ld_b1[en] = P.run("sync", lambda e, en=en: e.dma_start(
                            out=c.b1_sb[:, 16 * (en % 2):16 * (en % 2) + 16],
                            in_=c.b1r[en, :, :]), c.sB1, 16)

        _tap_phase(c, P, "D")
        P.emit("phaseD")

        for tc in range(NTC):
            col0 = 8 * (tc % 2)
            rstd, negmr = ln_stats(tc, col0)
            ve_o = P.run("vector", lambda e, tc=tc, rstd=rstd, negmr=negmr: e.tensor_scalar(
                c.ACC[:, 1024 * tc:1024 * (tc + 1)],
                c.ACC[:, 1024 * tc:1024 * (tc + 1)],
                rstd, negmr, ALU.mult, ALU.add), VE, 1)
            P.wait("sync", VE, ve_o)
            P.run("sync", lambda e, tc=tc: e.dma_start(
                out=c.y[128 * tc:128 * (tc + 1), :],
                in_=c.ACC[:, 1024 * tc:1024 * (tc + 1)]), c.sST, 16)
        _tap_phase(c, P, "E")
        P.wait("sync", c.sST, c.sST.n)
        P.emit("phaseE")


def kernel(**inputs):
    X = np.asarray(inputs["L0_attn_X"], np.float32)
    Wq = np.asarray(inputs["L0_attn_Wq"], np.float32)
    bq = np.asarray(inputs["L0_attn_bq"], np.float32)
    Wk = np.asarray(inputs["L0_attn_Wk"], np.float32)
    bk = np.asarray(inputs["L0_attn_bk"], np.float32)
    Wv = np.asarray(inputs["L0_attn_Wv"], np.float32)
    bv = np.asarray(inputs["L0_attn_bv"], np.float32)
    Wg = np.asarray(inputs["L0_ffn_Wg"], np.float32)
    bg = np.asarray(inputs["L0_ffn_bg"], np.float32)
    bias = np.asarray(inputs["L0_ffn_bias"], np.float32)
    W1 = np.asarray(inputs["L0_ffn_W1"], np.float32)
    b1 = np.asarray(inputs["L0_ffn_b1"], np.float32)
    W2 = np.asarray(inputs["L0_ffn_W2"], np.float32)
    b2 = np.asarray(inputs["L0_ffn_b2"], np.float32)

    cst = _consts()
    common = dict(
        wqT=np.ascontiguousarray(Wq.T),
        wkT=np.ascontiguousarray(Wk.T),
        wvT=np.ascontiguousarray(Wv.T),
        bq_rs=np.ascontiguousarray(bq.reshape(8, 128).T),
        bk_rs=np.ascontiguousarray(bk.reshape(8, 128).T),
        bvrow=np.ascontiguousarray(bv.reshape(1, D)),
        wg_aug=np.ascontiguousarray(
            np.concatenate([Wg, bg[None, :], bias[None, :]], 0)),
        w1=W1, w2=W2, b2=b2,
        b1r=np.ascontiguousarray(b1.reshape(NEXP, 16, 128).transpose(0, 2, 1)),
        **cst,
    )
    nc, _ = build_program()
    in_maps = [dict(common, x=np.ascontiguousarray(X[b])) for b in range(8)]
    res = run_bass_kernel_spmd(nc, in_maps, core_ids=list(range(8)))
    return np.stack([r["y"] for r in res.results], 0)


if __name__ == "__main__":
    import reference as ref
    ins = {k: np.asarray(v) for k, v in ref.setup_inputs(0).items()}
    out = kernel(**ins)
    print("out", out.shape, out.dtype)


# version 06
# speedup vs baseline: 1.4318x; 1.0843x over previous
import numpy as np
import ml_dtypes
from contextlib import ExitStack

import concourse.bass as bass
import concourse.mybir as mybir
from concourse.bass_utils import run_bass_kernel_spmd

F32 = mybir.dt.float32
BF16 = mybir.dt.bfloat16
U8 = mybir.dt.uint8
AF = mybir.ActivationFunctionType
ALU = mybir.AluOpType
AX = mybir.AxisListType

L, D = 2048, 1024
NH, HD, BLK, NB = 16, 64, 32, 64
NTC = 16
SCALE = 0.125
BIS_IT = 52
NEXP, DH = 16, 2048
BF = ml_dtypes.bfloat16
ENGINES = ["sync", "gpsimd", "tensor", "scalar", "vector"]

_TAPS = []


class Sem:
    __slots__ = ("h", "n")

    def __init__(self, nc, name):
        self.h = nc.alloc_semaphore(name=name)
        self.n = 0


class Prog:

    def __init__(self, nc):
        self.nc = nc
        self.esem = {}
        self.reset()

    def reset(self):
        self.q = {e: [] for e in ENGINES}

    def wait(self, eng, sem, n):
        if n and n > 0:
            self.q[eng].append(lambda e: e.wait_ge(sem.h, n))

    def run(self, eng, make, sem=None, k=1):
        es = self.esem.get(eng)
        if es is not None:
            self.wait(eng, es, es.n)
            if sem is None:
                sem, k = es, 1
        if sem is None:
            self.q[eng].append(lambda e: make(e))
            return None
        sem.n += k
        kk = k

        def f(e):
            make(e).then_inc(sem.h, kk)
        self.q[eng].append(f)
        return sem.n

    def emit(self, name):
        with self.nc.Block(name) as block:
            for eng in ENGINES:
                fns = self.q[eng]
                if not fns:
                    continue

                def body(e, fns=fns):
                    for f in fns:
                        f(e)
                getattr(block, eng)(body)
        self.reset()


class Ctx:
    pass


def _consts():
    c = {}
    c["identf"] = np.eye(128, dtype=np.float32)
    c["identb"] = np.eye(128, dtype=BF)
    c["onesb"] = np.ones((128, 128), dtype=BF)
    e = np.zeros((64, 2048), np.float32)
    for nb in range(64):
        e[nb, 32 * nb:32 * nb + 32] = 10000.0
    c["ebig"] = np.concatenate([e, e], 0).astype(BF)
    band = (np.abs(np.arange(NB)[:, None] - np.arange(NB)[None, :]) <= 1)
    c["band5k"] = (band * 5000.0).astype(np.float32)
    ab = np.zeros((2048, 64), np.float32)
    for t in range(2048):
        ab[t, t // 32] = 1.0 / 32.0
    c["abm"] = ab.astype(BF)
    i8 = np.zeros((128, 128), np.float32)
    for p in range(128):
        i8[p, (p // 8) * 8:(p // 8) * 8 + 8] = 1.0
    c["i8blk"] = i8
    return c


def _tap_phase(c, P, phase):
    first = True
    for ph, attr, outname in _TAPS:
        if ph != phase:
            continue
        if first:
            for sem in (c.sPE, c.sSC, c.sVE):
                P.wait("sync", sem, sem.n)
            first = False
        t = getattr(c, attr)
        out = c.nc.dram_tensor("tap_" + outname, list(t.shape), t.dtype,
                               kind="ExternalOutput").ap()
        full = t[tuple(slice(None) for _ in t.shape)]
        P.run("sync", lambda e, out=out, full=full: e.dma_start(out=out, in_=full),
              c.sST, 16)


def build_program():
    nc = bass.Bass("TRN2", target_bir_lowering=False)
    c = Ctx()
    c.nc = nc
    P = Prog(nc)

    def din(name, shape, dt=F32):
        return nc.dram_tensor(name, list(shape), dt, kind="ExternalInput").ap()

    c.x = din("x", (L, D))
    c.wqT = din("wqT", (D, D))
    c.wkT = din("wkT", (D, D))
    c.wvT = din("wvT", (D, D))
    c.bq_rs = din("bq_rs", (128, 8))
    c.bk_rs = din("bk_rs", (128, 8))
    c.bvrow = din("bvrow", (1, D))
    c.wg_aug = din("wg_aug", (D + 2, NEXP))
    c.w1 = din("w1", (NEXP, D, DH))
    c.b1r = din("b1r", (NEXP, 128, 16))
    c.w2 = din("w2", (NEXP, DH, D))
    c.b2 = din("b2", (NEXP, D))
    c.identf = din("identf", (128, 128))
    c.identb = din("identb", (128, 128), BF16)
    c.onesb = din("onesb", (128, 128), BF16)
    c.ebig = din("ebig", (128, 2048), BF16)
    c.band5k = din("band5k", (64, 64))
    c.abm = din("abm", (2048, 64), BF16)
    c.i8blk = din("i8blk", (128, 128))
    c.y = nc.dram_tensor("y", [L, D], F32, kind="ExternalOutput").ap()
    c.norm_scr = nc.dram_tensor("norm_scr", [NH, 4096], F32, kind="Internal").ap()
    c.m01_scr = nc.dram_tensor("m01_scr", [NH, 4096], F32, kind="Internal").ap()

    for s in ["sLoad", "sWt", "sX0", "sX1", "sWp0", "sWp1", "sG", "sB1",
              "sPE", "sSC", "sVE", "sST"]:
        setattr(c, s, Sem(nc, s))
    P.esem = {"vector": c.sVE, "scalar": c.sSC}

    with ExitStack() as outer:
        oc = outer.enter_context
        c.identf_sb = oc(nc.sbuf_tensor("identf_sb", [128, 128], F32))
        c.identb_sb = oc(nc.sbuf_tensor("identb_sb", [128, 128], BF16))
        c.onesb_sb = oc(nc.sbuf_tensor("onesb_sb", [128, 128], BF16))
        c.i8_sb = oc(nc.sbuf_tensor("i8_sb", [128, 128], F32))

        with ExitStack() as ab:
            abo = ab.enter_context
            c.QT = abo(nc.sbuf_tensor("QT", [128, 8, L], BF16))
            c.KsT = abo(nc.sbuf_tensor("KsT", [128, 8, L], BF16))
            c.Vaug = abo(nc.sbuf_tensor("Vaug", [128, NTC, NH, HD + 1], BF16))
            c.Vh_aug = abo(nc.sbuf_tensor("Vh_aug", [64, NH, HD + 1], BF16))
            c.band_sb = abo(nc.sbuf_tensor("band_sb", [64, 64], F32))
            c.norm_all = abo(nc.sbuf_tensor("norm_all", [64, NH, 64], F32))
            c.rm10k = abo(nc.sbuf_tensor("rm10k", [64, NH], F32))

            _phase_a(c, P)

            c.ACC = oc(nc.sbuf_tensor("ACC", [128, NTC * D], F32, side="right"))

            _phase_b(c, P, ab)

        with ExitStack() as cd:
            cdo = cd.enter_context
            c.x1T = cdo(nc.sbuf_tensor("x1T", [128, 8, L], BF16))
            c.w1b = [cdo(nc.sbuf_tensor(f"w1b{i}", [128, 8, 1024], BF16)) for i in range(2)]
            c.w2b = [cdo(nc.sbuf_tensor(f"w2b{i}", [128, 8, 1024], BF16)) for i in range(2)]
            c.b1_sb = cdo(nc.sbuf_tensor("b1_sb", [128, 32], F32))
            c.ht_sb = [cdo(nc.sbuf_tensor(f"ht{i}", [128, 256], BF16)) for i in range(3)]
            c.x0t2 = cdo(nc.sbuf_tensor("x0t2", [128, D], F32))
            c.sdump = cdo(nc.sbuf_tensor("sdump", [128, D], F32))
            c.stats = cdo(nc.sbuf_tensor("stats", [128, 32], F32))
            c.top8 = cdo(nc.sbuf_tensor("top8", [128, 8], F32))
            c.lg_sb = cdo(nc.sbuf_tensor("lg_sb", [128, 32], F32))
            c.t16 = cdo(nc.sbuf_tensor("t16", [128, 48], F32))
            c.wg_sb = cdo(nc.sbuf_tensor("wg_sb", [128, 144], BF16))
            c.b2a_sb = cdo(nc.sbuf_tensor("b2a_sb", [NEXP, D], BF16))
            c.cw_sb = cdo(nc.sbuf_tensor("cw_sb", [128, NTC * NEXP], F32))
            c.cwT_sb = cdo(nc.sbuf_tensor("cwT_sb", [NEXP, L], BF16))

            _phase_cde(c, P, cd)

    return nc, c


def _phase_a(c, P):
    nc = c.nc
    with ExitStack() as sc_a:
        ao = sc_a.enter_context
        c.x0T = x0T = ao(nc.sbuf_tensor("x0T", [128, 8, L], BF16))
        wt = ao(nc.sbuf_tensor("wt", [128, 8, 1024], BF16))
        x0b = [ao(nc.sbuf_tensor(f"x0b{i}", [128, D], BF16)) for i in range(2)]
        bvr_sb = ao(nc.sbuf_tensor("bvr_sb", [1, D], BF16))
        c.xh_sb = xh_sb = ao(nc.sbuf_tensor("xh_sb", [64, D], F32))
        abm_sb = ao(nc.sbuf_tensor("abm_sb", [128, NTC, 64], BF16))
        c.XhT = XhT = ao(nc.sbuf_tensor("XhT", [128, 8, 64], BF16))
        c.QhT = QhT = ao(nc.sbuf_tensor("QhT", [128, 8, 64], BF16))
        c.KhT = KhT = ao(nc.sbuf_tensor("KhT", [128, 8, 64], BF16))
        bq_sb = ao(nc.sbuf_tensor("bq_sb", [128, 8], F32))
        bks_sb = ao(nc.sbuf_tensor("bks_sb", [128, 8], F32))

        with ExitStack() as ps:
            po = ps.enter_context
            pk = [po(nc.psum_tensor(f"pA{i}", [128, 512], F32)) for i in range(2)]
            pb16 = [po(nc.psum_tensor(f"pAb{i}", [128, 1024], BF16)) for i in range(2)]
            pxh = [po(nc.psum_tensor(f"pXh{i}", [128, 512], F32)) for i in range(2)]
            plow = [po(nc.psum_tensor(f"plow{i}", [128, 512], F32)) for i in range(2)]

            S, PE, SC, VE = c.sLoad, c.sPE, c.sSC, c.sVE
            Wt, sX = c.sWt, [c.sX0, c.sX1]

            for dst, src in [
                (c.identf_sb[:, :], c.identf), (c.identb_sb[:, :], c.identb),
                (c.onesb_sb[:, :], c.onesb), (c.i8_sb[:, :], c.i8blk),
                (c.band_sb[:, :], c.band5k),
                (bq_sb[:, :], c.bq_rs), (bks_sb[:, :], c.bk_rs),
                (abm_sb[:, :, :], c.abm.rearrange("(tc p) n -> p tc n", p=128)),
            ]:
                P.run("sync", lambda e, d=dst, s=src: e.dma_start(out=d, in_=s), S, 16)
            ld_consts = S.n

            P.run("gpsimd", lambda e: e.dma_start(out=bvr_sb[:, :], in_=c.bvrow), Wt, 16)

            P.wait("vector", S, ld_consts)
            ve_bks = P.run("vector", lambda e: e.tensor_scalar_mul(
                bks_sb[:, :], bks_sb[:, :], SCALE), VE, 1)
            P.run("vector", lambda e: e.memset(c.Vaug[:, :, :, HD:HD + 1], 1.0), VE, 1)
            ve_init = P.run("vector", lambda e: e.memset(
                c.Vh_aug[0:64, :, HD:HD + 1], 32.0), VE, 1)

            ld_x0, pe_tile, sc_x0T = {}, {}, {}
            for tc in range(NTC):
                if tc >= 2:
                    P.wait("gpsimd", PE, pe_tile[tc - 2])
                ld_x0[tc] = P.run("gpsimd", lambda e, tc=tc: e.dma_start(
                    out=x0b[tc % 2][:, :], in_=c.x[128 * tc:128 * (tc + 1), :]),
                    sX[tc % 2], 16)

                P.wait("tensor", sX[tc % 2], ld_x0[tc])
                if tc == 0:
                    P.wait("tensor", S, ld_consts)
                for dd in range(2):
                    P.run("tensor", lambda e, tc=tc, dd=dd: e.matmul(
                        pxh[dd][0:64, :], abm_sb[:, tc, :],
                        x0b[tc % 2][:, 512 * dd:512 * (dd + 1)],
                        start=(tc == 0), stop=(tc == NTC - 1)))
                if tc >= 2:
                    P.wait("tensor", SC, sc_x0T[tc - 2])
                for dc in range(8):
                    pe_tile[tc] = P.run("tensor", lambda e, tc=tc, dc=dc: e.transpose(
                        pb16[tc % 2][:, 128 * dc:128 * (dc + 1)],
                        x0b[tc % 2][:, 128 * dc:128 * (dc + 1)],
                        c.identb_sb[:, :]), PE if dc == 7 else None, 1)

                P.wait("scalar", PE, pe_tile[tc])
                sc_x0T[tc] = P.run("scalar", lambda e, tc=tc: e.copy(
                    x0T[:, :, 128 * tc:128 * (tc + 1)],
                    pb16[tc % 2][:, :].rearrange("p (cc t) -> p cc t", cc=8)), SC, 1)

            ld_wq = P.run("gpsimd", lambda e: e.dma_start(
                out=wt[:, :, :], in_=c.wqT.rearrange("(cc p) d -> p cc d", p=128)), Wt, 16)

            P.wait("scalar", PE, pe_tile[NTC - 1])
            sc_xh = None
            for dd in range(2):
                sc_xh = P.run("scalar", lambda e, dd=dd: e.copy(
                    xh_sb[0:64, 512 * dd:512 * (dd + 1)], pxh[dd][0:64, :]), SC, 1)
            P.wait("tensor", SC, sc_xh)
            pe_xhT = None
            for dc in range(8):
                pe_xhT = P.run("tensor", lambda e, dc=dc: e.transpose(
                    pk[0][:, 64 * dc:64 * (dc + 1)],
                    xh_sb[0:64, 128 * dc:128 * (dc + 1)],
                    c.identf_sb[0:64, 0:64]), PE if dc == 7 else None, 1)
            P.wait("scalar", PE, pe_xhT)
            sc_xhT = P.run("scalar", lambda e: e.copy(
                XhT[:, :, :], pk[0][:, :].rearrange("p (cc n) -> p cc n", cc=8)), SC, 1)

            def proj_T(outT, bias_sb, scale, n_free, nt, w_ld, prev_sc):
                sc_evac = {}
                first = True
                for dc in range(8):
                    for tt in range(nt):
                        idx = dc * nt + tt
                        bank = pk[idx % 2]
                        if first:
                            P.wait("tensor", Wt, w_ld)
                            P.wait("tensor", SC, sc_xhT)
                            first = False
                        if idx >= 2:
                            P.wait("tensor", SC, sc_evac[idx - 2])
                        elif prev_sc:
                            P.wait("tensor", SC, prev_sc)
                        pe_g = None
                        for cc in range(8):
                            if n_free == 512:
                                def mk(e, dc=dc, tt=tt, cc=cc, bank=bank):
                                    return e.matmul(
                                        bank[:, 0:512],
                                        wt[:, cc, 128 * dc:128 * (dc + 1)],
                                        x0T[:, cc, 512 * tt:512 * (tt + 1)],
                                        start=(cc == 0), stop=(cc == 7))
                            else:
                                def mk(e, dc=dc, cc=cc, bank=bank):
                                    return e.matmul(
                                        bank[:, 0:64],
                                        wt[:, cc, 128 * dc:128 * (dc + 1)],
                                        XhT[:, cc, :],
                                        start=(cc == 0), stop=(cc == 7))
                            pe_g = P.run("tensor", mk, PE if cc == 7 else None, 1)
                        P.wait("scalar", PE, pe_g)
                        sc_evac[idx] = P.run("scalar",
                            lambda e, dc=dc, tt=tt, bank=bank: e.activation(
                                outT[:, dc, n_free * tt:n_free * (tt + 1)],
                                bank[:, 0:n_free], AF.Identity,
                                bias=bias_sb[:, dc:dc + 1], scale=scale), SC, 1)
                return sc_evac[8 * nt - 1]

            P.wait("scalar", VE, ve_bks)
            sc_q = proj_T(c.QT, bq_sb, 1.0, 512, 4, ld_wq, sc_xhT)
            sc_qh = proj_T(QhT, bq_sb, 1.0, 64, 1, ld_wq, sc_q)
            pe_after_q = PE.n

            P.wait("gpsimd", PE, pe_after_q)
            ld_wk = P.run("gpsimd", lambda e: e.dma_start(
                out=wt[:, :, :], in_=c.wkT.rearrange("(cc p) d -> p cc d", p=128)), Wt, 16)

            sc_k = proj_T(c.KsT, bks_sb, SCALE, 512, 4, ld_wk, sc_qh)
            sc_kh = proj_T(KhT, bks_sb, SCALE, 64, 1, ld_wk, sc_k)
            pe_after_k = PE.n

            P.wait("gpsimd", PE, pe_after_k)
            ld_wv = P.run("gpsimd", lambda e: e.dma_start(
                out=wt[:, :, :], in_=c.wvT.rearrange("(cc p) d -> p cc d", p=128)), Wt, 16)

            sc_v = {}
            idx = 0
            for tc in range(NTC):
                for dd in range(2):
                    bank = pk[idx % 2]
                    if idx == 0:
                        P.wait("tensor", Wt, ld_wv)
                        P.wait("tensor", VE, ve_init)
                    if idx >= 2:
                        P.wait("tensor", SC, sc_v[idx - 2])
                    elif idx < 2:
                        P.wait("tensor", SC, sc_kh)
                    for cc in range(8):
                        P.run("tensor", lambda e, tc=tc, dd=dd, cc=cc, bank=bank: e.matmul(
                            bank[:, :],
                            x0T[:, cc, 128 * tc:128 * (tc + 1)],
                            wt[:, cc, 512 * dd:512 * (dd + 1)],
                            start=(cc == 0), stop=False))
                    pe_g = P.run("tensor", lambda e, dd=dd, bank=bank: e.matmul(
                        bank[:, :], c.onesb_sb[0:1, 0:128],
                        bvr_sb[0:1, 512 * dd:512 * (dd + 1)],
                        start=False, stop=True), PE, 1)
                    P.wait("scalar", PE, pe_g)
                    sc_v[idx] = P.run("scalar", lambda e, tc=tc, dd=dd, bank=bank: e.copy(
                        c.Vaug[:, tc, 8 * dd:8 * (dd + 1), 0:HD],
                        bank[:, :].rearrange("p (h n) -> p h n", h=8)), SC, 1)
                    idx += 1

            sc_vh = {}
            for dd in range(2):
                bank = pk[dd % 2]
                P.wait("tensor", SC, sc_v[idx - 2 + dd])
                for cc in range(8):
                    P.run("tensor", lambda e, dd=dd, cc=cc, bank=bank: e.matmul(
                        bank[0:64, :], XhT[:, cc, :],
                        wt[:, cc, 512 * dd:512 * (dd + 1)],
                        start=(cc == 0), stop=False))
                pe_g = P.run("tensor", lambda e, dd=dd, bank=bank: e.matmul(
                    bank[0:64, :], c.onesb_sb[0:1, 0:64],
                    bvr_sb[0:1, 512 * dd:512 * (dd + 1)],
                    start=False, stop=True), PE, 1)
                P.wait("scalar", PE, pe_g)
                sc_vh[dd] = P.run("scalar", lambda e, dd=dd, bank=bank: e.activation(
                    c.Vh_aug[0:64, 8 * dd:8 * (dd + 1), 0:HD],
                    bank[0:64, :].rearrange("p (h n) -> p h n", h=8),
                    AF.Identity, scale=32.0), SC, 1)

            ve_low = {}
            for h in range(NH):
                b = 64 * (h % 2)
                hc = h // 2
                bank = plow[h % 2]
                if h == 0:
                    P.wait("tensor", SC, max(sc_qh, sc_kh))
                if h >= 2:
                    P.wait("tensor", VE, ve_low[h - 2])
                pe_l = P.run("tensor", lambda e, b=b, hc=hc, bank=bank: e.matmul(
                    bank[0:64, 0:64],
                    QhT[b:b + 64, hc, :], KhT[b:b + 64, hc, :],
                    start=True, stop=True), PE, 1)
                P.wait("vector", PE, pe_l)
                ve_rm = P.run("vector", lambda e, h=h, bank=bank: e.reduce_max(
                    c.rm10k[0:64, h:h + 1], bank[0:64, 0:64], axis=AX.X), VE, 1)
                P.wait("vector", VE, ve_rm)
                P.run("vector", lambda e, h=h, bank=bank: e.scalar_tensor_tensor(
                    c.norm_all[0:64, h, :], bank[0:64, 0:64],
                    c.rm10k[0:64, h:h + 1], c.band_sb[:, :],
                    ALU.subtract, ALU.add), VE, 1)
                ve_low[h] = P.run("vector", lambda e, h=h: e.tensor_scalar_add(
                    c.rm10k[0:64, h:h + 1], c.rm10k[0:64, h:h + 1], -10000.0), VE, 1)

            P.wait("sync", VE, ve_low[NH - 1])
            c.ld_norm_scr = P.run("sync", lambda e: e.dma_start(
                out=c.norm_scr.rearrange("h (q k) -> q h k", q=64),
                in_=c.norm_all[:, :, :]), S, 16)

            _tap_phase(c, P, "A")
            P.emit("phaseA")


def _phase_b(c, P, scope):
    nc = c.nc
    bo = scope.enter_context
    c.ebig_sb = bo(nc.sbuf_tensor("ebig_sb", [128, 2048], BF16))
    c.M01cT = bo(nc.sbuf_tensor("M01cT", [128, NH, 64], BF16))
    c.lowWT = bo(nc.sbuf_tensor("lowWT", [64, NH, 64], BF16))
    c.nfl = nfl = bo(nc.sbuf_tensor("nfl", [128, 512], F32))
    tmpf = bo(nc.sbuf_tensor("tmpf", [128, 512], F32))
    part = bo(nc.sbuf_tensor("part", [128, 1], F32))
    c.lo = lo = bo(nc.sbuf_tensor("lo", [128, 1], F32))
    hi = bo(nc.sbuf_tensor("hi", [128, 1], F32))
    mid = bo(nc.sbuf_tensor("mid", [128, 1], F32))
    mU = bo(nc.sbuf_tensor("mU", [128, 1], U8))
    mU2 = bo(nc.sbuf_tensor("mU2", [128, 1], U8))
    c.m01fl = m01fl = bo(nc.sbuf_tensor("m01fl", [128, 512], F32))
    c.m01c_sb = m01c = bo(nc.sbuf_tensor("m01c", [64, NH, 64], F32))
    t64 = bo(nc.sbuf_tensor("t64", [64, 64], F32))
    lwf = bo(nc.sbuf_tensor("lwf", [64, 64], F32))
    c.PT = PT = [[bo(nc.sbuf_tensor(f"PT{i}_{j}", [128, 512], BF16)) for j in range(2)] for i in range(2)]
    c.attnT_sb = attnT = [bo(nc.sbuf_tensor(f"attnT{i}", [65, 1024], F32)) for i in range(2)]
    dden = bo(nc.sbuf_tensor("dden", [128, 2], F32))

    with ExitStack() as ps:
        po = ps.enter_context
        sb = [po(nc.psum_tensor(f"pS{i}", [128, 1024], F32)) for i in range(2)]
        sb2 = [[sb[ho][:, 0:512], sb[ho][:, 512:1024]] for ho in range(2)]
        av = [po(nc.psum_tensor(f"pV{i}", [128, 512], F32)) for i in range(4)]

        S, PE, SC, VE = c.sLoad, c.sPE, c.sSC, c.sVE

        P.wait("sync", c.sST, c.sST.n)
        P.wait("gpsimd", c.sST, c.sST.n)
        ld_ebig = P.run("sync", lambda e: e.dma_start(
            out=c.ebig_sb[:, :], in_=c.ebig), S, 16)

        P.wait("sync", S, max(c.ld_norm_scr, ld_ebig))
        ld_flat = P.run("sync", lambda e: e.dma_start(
            out=nfl[:, :], in_=c.norm_scr.rearrange("h (j f) -> (h j) f", j=8)), S, 16)

        P.wait("vector", S, ld_flat)
        P.run("vector", lambda e: e.memset(lo[:, :], -30.0))
        P.run("vector", lambda e: e.memset(hi[:, :], 1.0))
        P.run("vector", lambda e: e.memset(mid[:, :], -14.5), VE, 1)
        ve_upd = VE.n
        for it in range(BIS_IT):
            P.wait("vector", VE, ve_upd)
            P.run("vector", lambda e: e.tensor_scalar(
                tmpf[:, :], nfl[:, :], mid[:, 0:1], None, ALU.is_ge))
            ve_cmp = P.run("vector", lambda e: e.reduce_sum(
                part[:, :], tmpf[:, :], axis=AX.X), VE, 1)
            P.wait("tensor", VE, ve_cmp)
            if it > 0:
                P.wait("tensor", VE, ve_upd)
            pe_bis = P.run("tensor", lambda e: e.matmul(
                sb[0][:, 0:1], c.i8_sb[:, :], part[:, 0:1],
                start=True, stop=True), PE, 1)
            P.wait("vector", PE, pe_bis)
            P.run("vector", lambda e: e.tensor_scalar(
                mU[:, :], sb[0][:, 0:1], 1024.0, None, ALU.is_ge))
            P.run("vector", lambda e: e.tensor_scalar(
                mU2[:, :], sb[0][:, 0:1], 1024.0, None, ALU.is_lt))
            P.run("vector", lambda e: e.copy_predicated(lo[:, 0:1], mU[:, 0:1], mid[:, 0:1]))
            P.run("vector", lambda e: e.copy_predicated(hi[:, 0:1], mU2[:, 0:1], mid[:, 0:1]))
            P.run("vector", lambda e: e.tensor_add(mid[:, 0:1], lo[:, 0:1], hi[:, 0:1]))
            ve_upd = P.run("vector", lambda e: e.tensor_scalar_mul(
                mid[:, 0:1], mid[:, 0:1], 0.5), VE, 1)
        P.wait("vector", VE, ve_upd)
        ve_m01 = P.run("vector", lambda e: e.tensor_scalar(
            m01fl[:, :], nfl[:, :], lo[:, 0:1], 1.0, ALU.is_ge, ALU.subtract), VE, 1)

        P.wait("sync", VE, ve_m01)
        st_m = P.run("sync", lambda e: e.dma_start(
            out=c.m01_scr.rearrange("h (j f) -> (h j) f", j=8), in_=m01fl[:, :]), S, 16)
        P.wait("sync", S, st_m)
        ld_m01 = P.run("sync", lambda e: e.dma_start(
            out=m01c[:, :, :], in_=c.m01_scr.rearrange("h (q k) -> q h k", q=64)), S, 16)

        sc_mT = {}
        P.wait("vector", S, ld_m01)
        for h in range(NH):
            ve_a = P.run("vector", lambda e, h=h: e.scalar_tensor_tensor(
                t64[:, :], c.norm_all[0:64, h, :], c.rm10k[0:64, h:h + 1],
                c.band_sb[:, :], ALU.add, ALU.subtract), VE, 1)
            P.wait("vector", VE, ve_a)
            ve_b = P.run("vector", lambda e, h=h: e.scalar_tensor_tensor(
                t64[:, :], m01c[0:64, h, :], -10000.0, t64[:, :],
                ALU.mult, ALU.add), VE, 1)
            P.wait("scalar", VE, ve_b)
            sc_e = P.run("scalar", lambda e: e.activation(
                lwf[:, :], t64[:, :], AF.Exp), SC, 1)
            P.wait("tensor", SC, sc_e)
            P.run("tensor", lambda e: e.transpose(
                sb[0][0:64, 0:64], lwf[:, :], c.identf_sb[0:64, 0:64]))
            pe_t = P.run("tensor", lambda e, h=h: e.transpose(
                sb[1][0:64, 0:64], m01c[0:64, h, :], c.identf_sb[0:64, 0:64]), PE, 1)
            P.wait("scalar", PE, pe_t)
            P.run("scalar", lambda e, h=h: e.copy(
                c.lowWT[0:64, h, :], sb[0][0:64, 0:64]), SC, 1)
            sc_mT[h] = P.run("scalar", lambda e, h=h: e.copy(
                c.M01cT[0:64, h, :], sb[1][0:64, 0:64]), SC, 1)
            P.wait("vector", PE, pe_t)

        P.wait("sync", SC, sc_mT[NH - 1])
        ld_dup = P.run("sync", lambda e: e.dma_start(
            out=c.M01cT[64:128, :, :], in_=c.M01cT[0:64, :, :]), S, 16)

        sc_pt = {}
        pe_av = {}
        pe_sm = {}
        dv = {}
        tail_ae = {}

        def _emit_av(g, qp, kc):
            for ho in range(2):
                h = 2 * g + ho
                P.wait("tensor", SC, sc_pt[(g, qp, kc, ho)])
                pe_av[(g, qp, kc, ho)] = P.run("tensor",
                    lambda e, h=h, kc=kc, qp=qp, ho=ho: e.matmul(
                        av[2 * (qp % 2) + ho][0:65, :],
                        c.Vaug[:, kc, h, :],
                        PT[ho][kc % 2][:, :],
                        start=False, stop=(kc == NTC - 1)),
                    PE, 1)

        def _tail_piece(g, qp, i):
            at = attnT[qp % 2]
            if i == 0:
                P.wait("scalar", PE, pe_av[(g, qp, NTC - 1, 1)])
                for ho in range(2):
                    tail_ae[(g, qp)] = P.run("scalar",
                        lambda e, qp=qp, ho=ho, at=at: e.copy(
                            at[0:65, 512 * ho:512 * (ho + 1)],
                            av[2 * (qp % 2) + ho][0:65, :]), SC, 1)
                P.wait("tensor", SC, tail_ae[(g, qp)])
            ho = i // 4
            t = 4 * qp + (i % 4)
            h = 2 * g + ho
            bank = av[2 * (qp % 2) + i % 2]
            if i >= 2:
                P.wait("tensor", VE, dv[(g, qp, i - 2)])
            pe_tr = P.run("tensor", lambda e, i=i, bank=bank, at=at: e.transpose(
                bank[:, 0:65], at[0:65, 128 * i:128 * (i + 1)],
                c.identf_sb[0:65, 0:65]), PE, 1)
            P.wait("vector", PE, pe_tr)
            P.run("vector", lambda e, bank=bank: e.tensor_scalar_add(
                dden[:, 0:1], bank[:, 64:65], 1e-6))
            ve_rc = P.run("vector", lambda e: e.reciprocal(
                dden[:, 1:2], dden[:, 0:1]), VE, 1)
            P.wait("vector", VE, ve_rc)
            dv[(g, qp, i)] = P.run("vector",
                lambda e, t=t, h=h, bank=bank: e.tensor_scalar_mul(
                    c.ACC[:, 1024 * t + 64 * h:1024 * t + 64 * h + 64],
                    bank[:, 0:64], dden[:, 1:2]), VE, 1)

        pending = None
        for g in range(NH // 2):
            hc = g
            for qp in range(4):
                P.wait("tensor", SC, max(sc_mT[2 * g], sc_mT[2 * g + 1]))
                if g == 0 and qp == 0:
                    P.wait("tensor", S, ld_dup)
                prev_pass = (g, qp - 2) if qp >= 2 else (
                    (g - 1, qp + 2) if g > 0 else None)
                for ho in range(2):
                    h = 2 * g + ho
                    if prev_pass is not None:
                        P.wait("tensor", VE, dv[prev_pass + (6 + ho,)])
                    P.run("tensor", lambda e, h=h, qp=qp, ho=ho: e.matmul(
                        av[2 * (qp % 2) + ho][0:65, :],
                        c.Vh_aug[0:64, h, :],
                        c.lowWT[0:64, h, 16 * qp:16 * (qp + 1)]
                        .rearrange("p (f o) -> p f o", o=1).to_broadcast([64, 16, 32]),
                        start=True, stop=False))
                for kc in range(NTC):
                    key_prev = (g, qp, kc - 2) if kc >= 2 else (
                        (g, qp - 1, NTC - 2 + kc) if qp > 0 else
                        ((g - 1, 3, NTC - 2 + kc) if g > 0 else None))
                    if key_prev is not None:
                        P.wait("tensor", SC, max(sc_pt[key_prev + (0,)],
                                                 sc_pt[key_prev + (1,)]))
                    for ho in range(2):
                        b = 64 * ho
                        P.run("tensor", lambda e, b=b, hc=hc, kc=kc, qp=qp, ho=ho: e.matmul(
                            sb2[ho][kc % 2][:, :],
                            c.KsT[b:b + 64, hc, 128 * kc:128 * (kc + 1)],
                            c.QT[b:b + 64, hc, 512 * qp:512 * (qp + 1)],
                            start=True, stop=False))
                    for ho in range(2):
                        b = 64 * ho
                        h = 2 * g + ho
                        pe_sm[(g, qp, kc, ho)] = P.run("tensor",
                            lambda e, b=b, h=h, kc=kc, qp=qp, ho=ho: e.matmul(
                                sb2[ho][kc % 2][:, :],
                                c.ebig_sb[b:b + 64, 128 * kc:128 * (kc + 1)],
                                c.M01cT[b:b + 64, h, 16 * qp:16 * (qp + 1)]
                                .rearrange("p (f o) -> p f o", o=1).to_broadcast([64, 16, 32]),
                                start=False, stop=True), PE, 1)
                    for ho in range(2):
                        P.wait("scalar", PE, pe_sm[(g, qp, kc, ho)])
                        keyv = key_prev
                        if keyv is not None and (keyv + (ho,)) in pe_av:
                            P.wait("scalar", PE, pe_av[keyv + (ho,)])
                        sc_pt[(g, qp, kc, ho)] = P.run(
                            "scalar", lambda e, kc=kc, ho=ho: e.activation(
                                PT[ho][kc % 2][:, :], sb2[ho][kc % 2][:, :],
                                AF.Exp), SC, 1)
                    if kc >= 1:
                        _emit_av(g, qp, kc - 1)
                    if pending is not None and kc % 2 == 1:
                        _tail_piece(pending[0], pending[1], kc // 2)
                _emit_av(g, qp, NTC - 1)
                pending = (g, qp)
        for i in range(8):
            _tail_piece(NH // 2 - 1, 3, i)

        _tap_phase(c, P, "B")
        P.emit("phaseB")


def _phase_cde(c, P, scope):
    nc = c.nc

    with ExitStack() as ps:
        po = ps.enter_context
        ptr = [po(nc.psum_tensor(f"pT{i}", [128, 512], F32)) for i in range(4)]
        plg = po(nc.psum_tensor("plg", [128, 512], F32))
        pcw = po(nc.psum_tensor("pcw", [128, 512], F32))
        pb2 = [po(nc.psum_tensor(f"pB2_{i}", [128, 512], F32)) for i in range(2)]

        S, PE, SC, VE = c.sLoad, c.sPE, c.sSC, c.sVE
        G, sWp = c.sG, [c.sWp0, c.sWp1]

        def ln_stats(tc, col0):
            st = c.stats
            ve_m = P.run("vector", lambda e, tc=tc, col0=col0: e.reduce_sum(
                st[:, col0:col0 + 1], c.ACC[:, 1024 * tc:1024 * (tc + 1)],
                axis=AX.X), VE, 1)
            P.wait("scalar", VE, ve_m)
            sc_sq = P.run("scalar", lambda e, tc=tc, col0=col0: e.activation(
                c.sdump[:, :], c.ACC[:, 1024 * tc:1024 * (tc + 1)], AF.Square,
                accum_out=st[:, col0 + 1:col0 + 2]), SC, 1)
            P.wait("vector", SC, sc_sq)
            P.run("vector", lambda e, col0=col0: e.tensor_scalar_mul(
                st[:, col0 + 2:col0 + 3], st[:, col0:col0 + 1], 1.0 / 1024.0))
            P.run("vector", lambda e, col0=col0: e.tensor_mul(
                st[:, col0 + 3:col0 + 4], st[:, col0 + 2:col0 + 3], st[:, col0 + 2:col0 + 3]))
            P.run("vector", lambda e, col0=col0: e.scalar_tensor_tensor(
                st[:, col0 + 4:col0 + 5], st[:, col0 + 1:col0 + 2], 1.0 / 1024.0,
                st[:, col0 + 3:col0 + 4], ALU.mult, ALU.subtract))
            ve_v = P.run("vector", lambda e, col0=col0: e.tensor_scalar_add(
                st[:, col0 + 4:col0 + 5], st[:, col0 + 4:col0 + 5], 1e-5), VE, 1)
            P.wait("scalar", VE, ve_v)
            P.run("scalar", lambda e, col0=col0: e.activation(
                st[:, col0 + 5:col0 + 6], st[:, col0 + 4:col0 + 5], AF.Ln))
            sc_r = P.run("scalar", lambda e, col0=col0: e.activation(
                st[:, col0 + 6:col0 + 7], st[:, col0 + 5:col0 + 6], AF.Exp,
                scale=-0.5), SC, 1)
            P.wait("vector", SC, sc_r)
            ve_nm = P.run("vector", lambda e, col0=col0: e.scalar_tensor_tensor(
                st[:, col0 + 7:col0 + 8], st[:, col0 + 2:col0 + 3], -1.0,
                st[:, col0 + 6:col0 + 7], ALU.mult, ALU.mult), VE, 1)
            P.wait("vector", VE, ve_nm)
            return st[:, col0 + 6:col0 + 7], st[:, col0 + 7:col0 + 8]

        P.wait("sync", c.sST, c.sST.n)
        P.wait("gpsimd", c.sST, c.sST.n)
        P.run("gpsimd", lambda e: e.dma_start(
            out=c.wg_sb[:, 0:128].rearrange("p (cc e) -> p cc e", cc=8),
            in_=c.wg_aug[0:1024, :].rearrange("(cc p) e -> p cc e", p=128)), G, 16)
        P.run("gpsimd", lambda e: e.dma_start(
            out=c.wg_sb[0:2, 128:144], in_=c.wg_aug[1024:1026, :]), G, 16)
        c.ld_g = P.run("gpsimd", lambda e: e.dma_start(
            out=c.b2a_sb[:, :], in_=c.b2), G, 16)
        c.wld = {}
        for s in range(2):
            c.wld[(0, s, "w1")] = P.run("gpsimd", lambda e, s=s: e.dma_start(
                out=c.w1b[s][:, :, :],
                in_=c.w1[0, :, 1024 * s:1024 * (s + 1)]
                .rearrange("(cc p) hh -> p cc hh", p=128)), sWp[s], 16)
            c.wld[(0, s, "w2")] = P.run("gpsimd", lambda e, s=s: e.dma_start(
                out=c.w2b[s][:, :, :],
                in_=c.w2[0, 1024 * s:1024 * (s + 1), :]
                .rearrange("(cc p) dd -> p cc dd", p=128)), sWp[s], 16)
        c.ld_b1 = {0: P.run("sync", lambda e: e.dma_start(
            out=c.b1_sb[:, 0:16], in_=c.b1r[0, :, :]), c.sB1, 16)}

        ld2, ve_res, sc_x1T, pe_x1t, ve_b2, ve_cw = {}, {}, {}, {}, {}, {}
        for tc in range(NTC):
            col0 = 8 * (tc % 2)
            if tc >= 1:
                P.wait("sync", VE, ve_res[tc - 1])
            ld2[tc] = P.run("sync", lambda e, tc=tc: e.dma_start(
                out=c.x0t2[:, :], in_=c.x[128 * tc:128 * (tc + 1), :]), S, 16)
            P.wait("vector", S, ld2[tc])
            ve_res[tc] = P.run("vector", lambda e, tc=tc: e.tensor_add(
                c.ACC[:, 1024 * tc:1024 * (tc + 1)],
                c.ACC[:, 1024 * tc:1024 * (tc + 1)], c.x0t2[:, :]), VE, 1)
            rstd, negmr = ln_stats(tc, col0)
            ve_x1 = P.run("vector", lambda e, tc=tc, rstd=rstd, negmr=negmr: e.tensor_scalar(
                c.ACC[:, 1024 * tc:1024 * (tc + 1)],
                c.ACC[:, 1024 * tc:1024 * (tc + 1)],
                rstd, negmr, ALU.mult, ALU.add), VE, 1)

            P.wait("tensor", VE, ve_x1)
            if tc >= 2:
                P.wait("tensor", SC, sc_x1T[tc - 2])
            for dc in range(8):
                bank = ptr[2 * (tc % 2) + dc // 4]
                pe_x1t[tc] = P.run("tensor", lambda e, tc=tc, dc=dc, bank=bank: e.transpose(
                    bank[:, 128 * (dc % 4):128 * (dc % 4 + 1)],
                    c.ACC[:, 1024 * tc + 128 * dc:1024 * tc + 128 * (dc + 1)],
                    c.identf_sb[:, :]), PE if dc == 7 else None, 1)
            P.wait("scalar", PE, pe_x1t[tc])
            for half in range(2):
                bank = ptr[2 * (tc % 2) + half]
                sc_x1T[tc] = P.run("scalar", lambda e, tc=tc, half=half, bank=bank: e.copy(
                    c.x1T[:, 4 * half:4 * (half + 1), 128 * tc:128 * (tc + 1)],
                    bank[:, :].rearrange("p (cc t) -> p cc t", cc=4)), SC, 1)

            P.wait("tensor", SC, sc_x1T[tc])
            if tc == 0:
                P.wait("tensor", G, c.ld_g)
            for cc in range(8):
                P.run("tensor", lambda e, tc=tc, cc=cc: e.matmul(
                    plg[:, 0:16],
                    c.x1T[:, cc, 128 * tc:128 * (tc + 1)],
                    c.wg_sb[:, 16 * cc:16 * (cc + 1)],
                    start=(cc == 0), stop=False))
            pe_lg = P.run("tensor", lambda e: e.matmul(
                plg[:, 0:16], c.onesb_sb[0:2, 0:128], c.wg_sb[0:2, 128:144],
                start=False, stop=True), PE, 1)
            P.wait("scalar", PE, pe_lg)
            sc_lg = P.run("scalar", lambda e, tc=tc: e.copy(
                c.lg_sb[:, 16 * (tc % 2):16 * (tc % 2) + 16], plg[:, 0:16]), SC, 1)

            P.wait("vector", SC, sc_lg)
            lg = c.lg_sb[:, 16 * (tc % 2):16 * (tc % 2) + 16]
            cw = c.cw_sb[:, 16 * tc:16 * (tc + 1)]
            t16 = c.t16
            ve_mx = P.run("vector", lambda e, lg=lg: e.max(
                out=c.top8[:, :], in_=lg), VE, 1)
            ve_d = P.run("vector", lambda e: e.tensor_sub(
                c.stats[:, 16:17], c.top8[:, 1:2], c.top8[:, 0:1]), VE, 1)
            P.wait("scalar", VE, ve_d)
            sc_e2 = P.run("scalar", lambda e: e.activation(
                c.stats[:, 17:18], c.stats[:, 16:17], AF.Exp), SC, 1)
            P.wait("vector", SC, sc_e2)
            P.run("vector", lambda e: e.tensor_scalar_add(
                c.stats[:, 18:19], c.stats[:, 17:18], 1.0))
            ve_rc = P.run("vector", lambda e: e.reciprocal(
                c.stats[:, 19:20], c.stats[:, 18:19]), VE, 1)
            ve_w2 = P.run("vector", lambda e: e.tensor_mul(
                c.stats[:, 20:21], c.stats[:, 17:18], c.stats[:, 19:20]), VE, 1)
            P.wait("vector", VE, ve_mx)
            P.run("vector", lambda e, lg=lg: e.tensor_scalar(
                t16[:, 0:16], lg, c.top8[:, 0:1], None, ALU.is_equal))
            P.run("vector", lambda e, lg=lg: e.scalar_tensor_tensor(
                t16[:, 16:32], t16[:, 0:16], -1e30, lg, ALU.mult, ALU.add))
            P.run("vector", lambda e: e.tensor_scalar(
                t16[:, 32:48], t16[:, 16:32], c.top8[:, 1:2], None, ALU.is_equal))
            P.wait("vector", VE, ve_w2)
            P.run("vector", lambda e, cw=cw: e.tensor_scalar(
                cw, t16[:, 0:16], c.stats[:, 19:20], None, ALU.mult))
            ve_cw[tc] = P.run("vector", lambda e, cw=cw: e.scalar_tensor_tensor(
                cw, t16[:, 32:48], c.stats[:, 20:21], cw, ALU.mult, ALU.add), VE, 1)

            P.wait("tensor", VE, ve_cw[tc])
            pe_cw = P.run("tensor", lambda e, cw=cw: e.transpose(
                pcw[0:16, 0:128], cw, c.identf_sb[:, :]), PE, 1)
            P.wait("scalar", PE, pe_cw)
            sc_cwT = P.run("scalar", lambda e, tc=tc: e.copy(
                c.cwT_sb[0:16, 128 * tc:128 * (tc + 1)], pcw[0:16, 0:128]), SC, 1)
            P.wait("tensor", SC, sc_cwT)
            if tc == 0:
                P.wait("tensor", G, c.ld_g)
            if tc >= 1:
                P.wait("tensor", VE, ve_b2[tc - 1])
            pe_b2 = None
            for dd in range(2):
                pe_b2 = P.run("tensor", lambda e, tc=tc, dd=dd: e.matmul(
                    pb2[dd][:, :], c.cwT_sb[0:16, 128 * tc:128 * (tc + 1)],
                    c.b2a_sb[0:16, 512 * dd:512 * (dd + 1)],
                    start=True, stop=True), PE if dd == 1 else None, 1)
            P.wait("vector", PE, pe_b2)
            for dd in range(2):
                ve_b2[tc] = P.run("vector", lambda e, tc=tc, dd=dd: e.tensor_add(
                    c.ACC[:, 1024 * tc + 512 * dd:1024 * tc + 512 * (dd + 1)],
                    c.ACC[:, 1024 * tc + 512 * dd:1024 * tc + 512 * (dd + 1)],
                    pb2[dd][:, :]), VE if dd == 1 else None, 1)

        _tap_phase(c, P, "C")
        P.emit("phaseC")

        phb = [plg, pcw]
        ybk = ptr

        sc_relu, pe_h, pe_y, pe_yh, ve_y = {}, {}, {}, {}, {}

        def emit_y(e_i, s, tc, Hs, par, first):
            k = (e_i, s, tc, Hs)
            P.wait("tensor", SC, sc_relu[k])
            if Hs == 0:
                if tc > 0:
                    P.wait("tensor", VE, ve_y[(e_i, s, tc - 1)])
                elif not first:
                    ep, sp = divmod(2 * e_i + s - 1, 2)
                    P.wait("tensor", VE, ve_y[(ep, sp, 7)])
            for tsub in range(2):
                for dd in range(2):
                    m = P.run("tensor",
                        lambda e, par=par, Hs=Hs, tsub=tsub, dd=dd: e.matmul(
                            ybk[2 * tsub + dd][:, :],
                            c.ht_sb[Hs % 3][:, 128 * tsub:128 * (tsub + 1)],
                            c.w2b[par][:, Hs, 512 * dd:512 * (dd + 1)],
                            start=(Hs == 0), stop=(Hs == 7)),
                        PE if (tsub == 1 and dd == 1) else None, 1)
            pe_yh[k] = m
            pe_y[(e_i, s, tc)] = m

        def _hk_prev(e_i, s, tc, Hs, d, first):
            idx = (2 * e_i + s) * 64 + tc * 8 + Hs - d
            if idx < 0:
                return None
            es, r = divmod(idx, 64)
            ep, sp = divmod(es, 2)
            return (ep, sp, r // 8, r % 8)

        for e_i in range(NEXP):
            for s in range(2):
                par = (2 * e_i + s) % 2
                first = (e_i == 0 and s == 0)
                for tc in range(8):
                    for Hs in range(8):
                        Hglob = 8 * s + Hs
                        hb = phb[Hs % 2]
                        k = (e_i, s, tc, Hs)
                        if tc == 0 and Hs == 0:
                            P.wait("tensor", sWp[par], c.wld[(e_i, s, "w2")])
                            P.wait("tensor", c.sB1, c.ld_b1[e_i])
                        kprev = _hk_prev(e_i, s, tc, Hs, 2, e_i == 0 and s == 0)
                        if kprev is not None and kprev in sc_relu:
                            P.wait("tensor", SC, sc_relu[kprev])
                        for cc in range(8):
                            pe_h[k] = P.run("tensor",
                                lambda e, par=par, Hs=Hs, tc=tc, cc=cc, hb=hb: e.matmul(
                                    hb[:, 0:256],
                                    c.w1b[par][:, cc, 128 * Hs:128 * (Hs + 1)],
                                    c.x1T[:, cc, 256 * tc:256 * (tc + 1)],
                                    start=(cc == 0), stop=(cc == 7)),
                                PE if cc == 7 else None, 1)
                        P.wait("scalar", PE, pe_h[k])
                        kht = _hk_prev(e_i, s, tc, Hs, 3, e_i == 0 and s == 0)
                        if kht is not None and kht in pe_yh:
                            P.wait("scalar", PE, pe_yh[kht])
                        sc_relu[k] = P.run("scalar",
                            lambda e, e_i=e_i, Hglob=Hglob, Hs=Hs, hb=hb: e.activation(
                                c.ht_sb[Hs % 3][:, :], hb[:, 0:256], AF.Relu,
                                bias=c.b1_sb[:, 16 * (e_i % 2) + Hglob:
                                             16 * (e_i % 2) + Hglob + 1]), SC, 1)
                        if Hs >= 2:
                            emit_y(e_i, s, tc, Hs - 2, par, first)
                    emit_y(e_i, s, tc, 6, par, first)
                    emit_y(e_i, s, tc, 7, par, first)
                    P.wait("vector", PE, pe_y[(e_i, s, tc)])
                    for tsub in range(2):
                        tcg = 2 * tc + tsub
                        for dd in range(2):
                            ve_y[(e_i, s, tc)] = P.run("vector",
                                lambda e, e_i=e_i, tcg=tcg, tsub=tsub, dd=dd: e.scalar_tensor_tensor(
                                    c.ACC[:, 1024 * tcg + 512 * dd:1024 * tcg + 512 * (dd + 1)],
                                    ybk[2 * tsub + dd][:, :],
                                    c.cw_sb[:, 16 * tcg + e_i:16 * tcg + e_i + 1],
                                    c.ACC[:, 1024 * tcg + 512 * dd:1024 * tcg + 512 * (dd + 1)],
                                    ALU.mult, ALU.add),
                                VE if (tsub == 1 and dd == 1) else None, 1)

                nxt = 2 * e_i + s + 2
                if nxt < 2 * NEXP:
                    en, sn = divmod(nxt, 2)
                    P.wait("gpsimd", PE, pe_y[(e_i, s, 7)])
                    c.wld[(en, sn, "w1")] = P.run("gpsimd",
                        lambda e, en=en, sn=sn, par=par: e.dma_start(
                            out=c.w1b[par][:, :, :],
                            in_=c.w1[en, :, 1024 * sn:1024 * (sn + 1)]
                            .rearrange("(cc p) hh -> p cc hh", p=128)), sWp[par], 16)
                    c.wld[(en, sn, "w2")] = P.run("gpsimd",
                        lambda e, en=en, sn=sn, par=par: e.dma_start(
                            out=c.w2b[par][:, :, :],
                            in_=c.w2[en, 1024 * sn:1024 * (sn + 1), :]
                            .rearrange("(cc p) dd -> p cc dd", p=128)), sWp[par], 16)
                    if sn == 0:
                        if en >= 2:
                            P.wait("sync", SC, sc_relu[(en - 2, 1, 7, 7)])
                        P.wait("sync", c.sB1, c.ld_b1[en - 1])
                        c.ld_b1[en] = P.run("sync", lambda e, en=en: e.dma_start(
                            out=c.b1_sb[:, 16 * (en % 2):16 * (en % 2) + 16],
                            in_=c.b1r[en, :, :]), c.sB1, 16)

        _tap_phase(c, P, "D")
        P.emit("phaseD")

        for tc in range(NTC):
            col0 = 8 * (tc % 2)
            rstd, negmr = ln_stats(tc, col0)
            ve_o = P.run("vector", lambda e, tc=tc, rstd=rstd, negmr=negmr: e.tensor_scalar(
                c.ACC[:, 1024 * tc:1024 * (tc + 1)],
                c.ACC[:, 1024 * tc:1024 * (tc + 1)],
                rstd, negmr, ALU.mult, ALU.add), VE, 1)
            P.wait("sync", VE, ve_o)
            P.run("sync", lambda e, tc=tc: e.dma_start(
                out=c.y[128 * tc:128 * (tc + 1), :],
                in_=c.ACC[:, 1024 * tc:1024 * (tc + 1)]), c.sST, 16)
        _tap_phase(c, P, "E")
        P.wait("sync", c.sST, c.sST.n)
        P.emit("phaseE")


def kernel(**inputs):
    X = np.asarray(inputs["L0_attn_X"], np.float32)
    Wq = np.asarray(inputs["L0_attn_Wq"], np.float32)
    bq = np.asarray(inputs["L0_attn_bq"], np.float32)
    Wk = np.asarray(inputs["L0_attn_Wk"], np.float32)
    bk = np.asarray(inputs["L0_attn_bk"], np.float32)
    Wv = np.asarray(inputs["L0_attn_Wv"], np.float32)
    bv = np.asarray(inputs["L0_attn_bv"], np.float32)
    Wg = np.asarray(inputs["L0_ffn_Wg"], np.float32)
    bg = np.asarray(inputs["L0_ffn_bg"], np.float32)
    bias = np.asarray(inputs["L0_ffn_bias"], np.float32)
    W1 = np.asarray(inputs["L0_ffn_W1"], np.float32)
    b1 = np.asarray(inputs["L0_ffn_b1"], np.float32)
    W2 = np.asarray(inputs["L0_ffn_W2"], np.float32)
    b2 = np.asarray(inputs["L0_ffn_b2"], np.float32)

    cst = _consts()
    common = dict(
        wqT=np.ascontiguousarray(Wq.T),
        wkT=np.ascontiguousarray(Wk.T),
        wvT=np.ascontiguousarray(Wv.T),
        bq_rs=np.ascontiguousarray(bq.reshape(8, 128).T),
        bk_rs=np.ascontiguousarray(bk.reshape(8, 128).T),
        bvrow=np.ascontiguousarray(bv.reshape(1, D)),
        wg_aug=np.ascontiguousarray(
            np.concatenate([Wg, bg[None, :], bias[None, :]], 0)),
        w1=W1, w2=W2, b2=b2,
        b1r=np.ascontiguousarray(b1.reshape(NEXP, 16, 128).transpose(0, 2, 1)),
        **cst,
    )
    nc, _ = build_program()
    in_maps = [dict(common, x=np.ascontiguousarray(X[b])) for b in range(8)]
    res = run_bass_kernel_spmd(nc, in_maps, core_ids=list(range(8)))
    return np.stack([r["y"] for r in res.results], 0)


if __name__ == "__main__":
    import reference as ref
    ins = {k: np.asarray(v) for k, v in ref.setup_inputs(0).items()}
    out = kernel(**ins)
    print("out", out.shape, out.dtype)


# version 07
# speedup vs baseline: 1.4378x; 1.0041x over previous
import numpy as np
import ml_dtypes
from contextlib import ExitStack

import concourse.bass as bass
import concourse.mybir as mybir
from concourse.bass_utils import run_bass_kernel_spmd

F32 = mybir.dt.float32
BF16 = mybir.dt.bfloat16
U8 = mybir.dt.uint8
AF = mybir.ActivationFunctionType
ALU = mybir.AluOpType
AX = mybir.AxisListType

L, D = 2048, 1024
NH, HD, BLK, NB = 16, 64, 32, 64
NTC = 16
SCALE = 0.125
BIS_IT = 52
NEXP, DH = 16, 2048
BF = ml_dtypes.bfloat16
ENGINES = ["sync", "gpsimd", "tensor", "scalar", "vector"]

_TAPS = []


class Sem:
    __slots__ = ("h", "n")

    def __init__(self, nc, name):
        self.h = nc.alloc_semaphore(name=name)
        self.n = 0


class Prog:

    def __init__(self, nc):
        self.nc = nc
        self.esem = {}
        self.reset()

    def reset(self):
        self.q = {e: [] for e in ENGINES}

    def wait(self, eng, sem, n):
        if n and n > 0:
            self.q[eng].append(lambda e: e.wait_ge(sem.h, n))

    def run(self, eng, make, sem=None, k=1):
        es = self.esem.get(eng)
        if es is not None:
            self.wait(eng, es, es.n)
            if sem is None:
                sem, k = es, 1
        if sem is None:
            self.q[eng].append(lambda e: make(e))
            return None
        sem.n += k
        kk = k

        def f(e):
            make(e).then_inc(sem.h, kk)
        self.q[eng].append(f)
        return sem.n

    def emit(self, name):
        with self.nc.Block(name) as block:
            for eng in ENGINES:
                fns = self.q[eng]
                if not fns:
                    continue

                def body(e, fns=fns):
                    for f in fns:
                        f(e)
                getattr(block, eng)(body)
        self.reset()


class Ctx:
    pass


def _consts():
    c = {}
    c["identf"] = np.eye(128, dtype=np.float32)
    c["identb"] = np.eye(128, dtype=BF)
    c["onesb"] = np.ones((128, 128), dtype=BF)
    e = np.zeros((64, 2048), np.float32)
    for nb in range(64):
        e[nb, 32 * nb:32 * nb + 32] = 10000.0
    c["ebig"] = np.concatenate([e, e], 0).astype(BF)
    band = (np.abs(np.arange(NB)[:, None] - np.arange(NB)[None, :]) <= 1)
    c["band5k"] = (band * 5000.0).astype(np.float32)
    ab = np.zeros((2048, 64), np.float32)
    for t in range(2048):
        ab[t, t // 32] = 1.0 / 32.0
    c["abm"] = ab.astype(BF)
    i8 = np.zeros((128, 128), np.float32)
    for p in range(128):
        i8[p, (p // 8) * 8:(p // 8) * 8 + 8] = 1.0
    c["i8blk"] = i8
    return c


def _tap_phase(c, P, phase):
    first = True
    for ph, attr, outname in _TAPS:
        if ph != phase:
            continue
        if first:
            for sem in (c.sPE, c.sSC, c.sVE):
                P.wait("sync", sem, sem.n)
            first = False
        t = getattr(c, attr)
        out = c.nc.dram_tensor("tap_" + outname, list(t.shape), t.dtype,
                               kind="ExternalOutput").ap()
        full = t[tuple(slice(None) for _ in t.shape)]
        P.run("sync", lambda e, out=out, full=full: e.dma_start(out=out, in_=full),
              c.sST, 16)


def build_program():
    nc = bass.Bass("TRN2", target_bir_lowering=False)
    c = Ctx()
    c.nc = nc
    P = Prog(nc)

    def din(name, shape, dt=F32):
        return nc.dram_tensor(name, list(shape), dt, kind="ExternalInput").ap()

    c.x = din("x", (L, D))
    c.wqT = din("wqT", (D, D))
    c.wkT = din("wkT", (D, D))
    c.wvT = din("wvT", (D, D))
    c.bq_rs = din("bq_rs", (128, 8))
    c.bk_rs = din("bk_rs", (128, 8))
    c.bvrow = din("bvrow", (1, D))
    c.wg_aug = din("wg_aug", (D + 2, NEXP))
    c.w1 = din("w1", (NEXP, D, DH))
    c.b1r = din("b1r", (NEXP, 128, 16))
    c.w2 = din("w2", (NEXP, DH, D))
    c.b2 = din("b2", (NEXP, D))
    c.identf = din("identf", (128, 128))
    c.identb = din("identb", (128, 128), BF16)
    c.onesb = din("onesb", (128, 128), BF16)
    c.ebig = din("ebig", (128, 2048), BF16)
    c.band5k = din("band5k", (64, 64))
    c.abm = din("abm", (2048, 64), BF16)
    c.i8blk = din("i8blk", (128, 128))
    c.y = nc.dram_tensor("y", [L, D], F32, kind="ExternalOutput").ap()
    c.norm_scr = nc.dram_tensor("norm_scr", [NH, 4096], F32, kind="Internal").ap()
    c.m01_scr = nc.dram_tensor("m01_scr", [NH, 4096], F32, kind="Internal").ap()

    for s in ["sLoad", "sWt", "sX0", "sX1", "sWp0", "sWp1", "sG", "sB1",
              "sPE", "sSC", "sVE", "sST"]:
        setattr(c, s, Sem(nc, s))
    P.esem = {"vector": c.sVE, "scalar": c.sSC}

    with ExitStack() as outer:
        oc = outer.enter_context
        c.identf_sb = oc(nc.sbuf_tensor("identf_sb", [128, 128], F32))
        c.identb_sb = oc(nc.sbuf_tensor("identb_sb", [128, 128], BF16))
        c.onesb_sb = oc(nc.sbuf_tensor("onesb_sb", [128, 128], BF16))
        c.i8_sb = oc(nc.sbuf_tensor("i8_sb", [128, 128], F32))

        with ExitStack() as ab:
            abo = ab.enter_context
            c.QT = abo(nc.sbuf_tensor("QT", [128, 8, L], BF16))
            c.KsT = abo(nc.sbuf_tensor("KsT", [128, 8, L], BF16))
            c.Vaug = abo(nc.sbuf_tensor("Vaug", [128, NTC, NH, HD + 1], BF16))
            c.Vh_aug = abo(nc.sbuf_tensor("Vh_aug", [64, NH, HD + 1], BF16))
            c.band_sb = abo(nc.sbuf_tensor("band_sb", [64, 64], F32))
            c.norm_all = abo(nc.sbuf_tensor("norm_all", [64, NH, 64], F32))
            c.rm10k = abo(nc.sbuf_tensor("rm10k", [64, NH], F32))

            _phase_a(c, P)

            c.ACC = oc(nc.sbuf_tensor("ACC", [128, NTC * D], F32, side="right"))

            _phase_b(c, P, ab)

        with ExitStack() as cd:
            cdo = cd.enter_context
            c.x1T = cdo(nc.sbuf_tensor("x1T", [128, 8, L], BF16))
            c.w1b = [cdo(nc.sbuf_tensor(f"w1b{i}", [128, 8, 1024], BF16)) for i in range(2)]
            c.w2b = [cdo(nc.sbuf_tensor(f"w2b{i}", [128, 8, 1024], BF16)) for i in range(2)]
            c.b1_sb = cdo(nc.sbuf_tensor("b1_sb", [128, 32], F32))
            c.ht_sb = [cdo(nc.sbuf_tensor(f"ht{i}", [128, 256], BF16)) for i in range(3)]
            c.x0t2 = cdo(nc.sbuf_tensor("x0t2", [128, D], F32))
            c.sdump = cdo(nc.sbuf_tensor("sdump", [128, D], F32))
            c.stats = cdo(nc.sbuf_tensor("stats", [128, 32], F32))
            c.top8 = cdo(nc.sbuf_tensor("top8", [128, 8], F32))
            c.lg_sb = cdo(nc.sbuf_tensor("lg_sb", [128, 32], F32))
            c.t16 = cdo(nc.sbuf_tensor("t16", [128, 48], F32))
            c.wg_sb = cdo(nc.sbuf_tensor("wg_sb", [128, 144], BF16))
            c.b2a_sb = cdo(nc.sbuf_tensor("b2a_sb", [NEXP, D], BF16))
            c.cw_sb = cdo(nc.sbuf_tensor("cw_sb", [128, NTC * NEXP], F32))
            c.cwT_sb = cdo(nc.sbuf_tensor("cwT_sb", [NEXP, L], BF16))

            _phase_cde(c, P, cd)

    return nc, c


def _phase_a(c, P):
    nc = c.nc
    with ExitStack() as sc_a:
        ao = sc_a.enter_context
        c.x0T = x0T = ao(nc.sbuf_tensor("x0T", [128, 8, L], BF16))
        wt = ao(nc.sbuf_tensor("wt", [128, 8, 1024], BF16))
        x0b = [ao(nc.sbuf_tensor(f"x0b{i}", [128, D], BF16)) for i in range(2)]
        bvr_sb = ao(nc.sbuf_tensor("bvr_sb", [1, D], BF16))
        c.xh_sb = xh_sb = ao(nc.sbuf_tensor("xh_sb", [64, D], F32))
        abm_sb = ao(nc.sbuf_tensor("abm_sb", [128, NTC, 64], BF16))
        c.XhT = XhT = ao(nc.sbuf_tensor("XhT", [128, 8, 64], BF16))
        c.QhT = QhT = ao(nc.sbuf_tensor("QhT", [128, 8, 64], BF16))
        c.KhT = KhT = ao(nc.sbuf_tensor("KhT", [128, 8, 64], BF16))
        bq_sb = ao(nc.sbuf_tensor("bq_sb", [128, 8], F32))
        bks_sb = ao(nc.sbuf_tensor("bks_sb", [128, 8], F32))

        with ExitStack() as ps:
            po = ps.enter_context
            pk = [po(nc.psum_tensor(f"pA{i}", [128, 512], F32)) for i in range(2)]
            pb16 = [po(nc.psum_tensor(f"pAb{i}", [128, 1024], BF16)) for i in range(2)]
            pxh = [po(nc.psum_tensor(f"pXh{i}", [128, 512], F32)) for i in range(2)]
            plow = [po(nc.psum_tensor(f"plow{i}", [128, 512], F32)) for i in range(2)]

            S, PE, SC, VE = c.sLoad, c.sPE, c.sSC, c.sVE
            Wt, sX = c.sWt, [c.sX0, c.sX1]

            for dst, src in [
                (c.identf_sb[:, :], c.identf), (c.identb_sb[:, :], c.identb),
                (c.onesb_sb[:, :], c.onesb), (c.i8_sb[:, :], c.i8blk),
                (c.band_sb[:, :], c.band5k),
                (bq_sb[:, :], c.bq_rs), (bks_sb[:, :], c.bk_rs),
                (abm_sb[:, :, :], c.abm.rearrange("(tc p) n -> p tc n", p=128)),
            ]:
                P.run("sync", lambda e, d=dst, s=src: e.dma_start(out=d, in_=s), S, 16)
            ld_consts = S.n

            P.run("gpsimd", lambda e: e.dma_start(out=bvr_sb[:, :], in_=c.bvrow), Wt, 16)

            P.wait("vector", S, ld_consts)
            ve_bks = P.run("vector", lambda e: e.tensor_scalar_mul(
                bks_sb[:, :], bks_sb[:, :], SCALE), VE, 1)
            P.run("vector", lambda e: e.memset(c.Vaug[:, :, :, HD:HD + 1], 1.0), VE, 1)
            ve_init = P.run("vector", lambda e: e.memset(
                c.Vh_aug[0:64, :, HD:HD + 1], 32.0), VE, 1)

            ld_x0, pe_tile, sc_x0T = {}, {}, {}
            for tc in range(NTC):
                if tc >= 2:
                    P.wait("gpsimd", PE, pe_tile[tc - 2])
                ld_x0[tc] = P.run("gpsimd", lambda e, tc=tc: e.dma_start(
                    out=x0b[tc % 2][:, :], in_=c.x[128 * tc:128 * (tc + 1), :]),
                    sX[tc % 2], 16)

                P.wait("tensor", sX[tc % 2], ld_x0[tc])
                if tc == 0:
                    P.wait("tensor", S, ld_consts)
                for dd in range(2):
                    P.run("tensor", lambda e, tc=tc, dd=dd: e.matmul(
                        pxh[dd][0:64, :], abm_sb[:, tc, :],
                        x0b[tc % 2][:, 512 * dd:512 * (dd + 1)],
                        start=(tc == 0), stop=(tc == NTC - 1)))
                if tc >= 2:
                    P.wait("tensor", SC, sc_x0T[tc - 2])
                for dc in range(8):
                    pe_tile[tc] = P.run("tensor", lambda e, tc=tc, dc=dc: e.transpose(
                        pb16[tc % 2][:, 128 * dc:128 * (dc + 1)],
                        x0b[tc % 2][:, 128 * dc:128 * (dc + 1)],
                        c.identb_sb[:, :]), PE if dc == 7 else None, 1)

                P.wait("scalar", PE, pe_tile[tc])
                sc_x0T[tc] = P.run("scalar", lambda e, tc=tc: e.copy(
                    x0T[:, :, 128 * tc:128 * (tc + 1)],
                    pb16[tc % 2][:, :].rearrange("p (cc t) -> p cc t", cc=8)), SC, 1)

            ld_wq = P.run("gpsimd", lambda e: e.dma_start(
                out=wt[:, :, :], in_=c.wqT.rearrange("(cc p) d -> p cc d", p=128)), Wt, 16)

            P.wait("scalar", PE, pe_tile[NTC - 1])
            sc_xh = None
            for dd in range(2):
                sc_xh = P.run("scalar", lambda e, dd=dd: e.copy(
                    xh_sb[0:64, 512 * dd:512 * (dd + 1)], pxh[dd][0:64, :]), SC, 1)
            P.wait("tensor", SC, sc_xh)
            pe_xhT = None
            for dc in range(8):
                pe_xhT = P.run("tensor", lambda e, dc=dc: e.transpose(
                    pk[0][:, 64 * dc:64 * (dc + 1)],
                    xh_sb[0:64, 128 * dc:128 * (dc + 1)],
                    c.identf_sb[0:64, 0:64]), PE if dc == 7 else None, 1)
            P.wait("scalar", PE, pe_xhT)
            sc_xhT = P.run("scalar", lambda e: e.copy(
                XhT[:, :, :], pk[0][:, :].rearrange("p (cc n) -> p cc n", cc=8)), SC, 1)

            def proj_T(outT, bias_sb, scale, n_free, nt, w_ld, prev_sc):
                sc_evac = {}
                first = True
                for dc in range(8):
                    for tt in range(nt):
                        idx = dc * nt + tt
                        bank = pk[idx % 2]
                        if first:
                            P.wait("tensor", Wt, w_ld)
                            P.wait("tensor", SC, sc_xhT)
                            first = False
                        if idx >= 2:
                            P.wait("tensor", SC, sc_evac[idx - 2])
                        elif prev_sc:
                            P.wait("tensor", SC, prev_sc)
                        pe_g = None
                        for cc in range(8):
                            if n_free == 512:
                                def mk(e, dc=dc, tt=tt, cc=cc, bank=bank):
                                    return e.matmul(
                                        bank[:, 0:512],
                                        wt[:, cc, 128 * dc:128 * (dc + 1)],
                                        x0T[:, cc, 512 * tt:512 * (tt + 1)],
                                        start=(cc == 0), stop=(cc == 7))
                            else:
                                def mk(e, dc=dc, cc=cc, bank=bank):
                                    return e.matmul(
                                        bank[:, 0:64],
                                        wt[:, cc, 128 * dc:128 * (dc + 1)],
                                        XhT[:, cc, :],
                                        start=(cc == 0), stop=(cc == 7))
                            pe_g = P.run("tensor", mk, PE if cc == 7 else None, 1)
                        P.wait("scalar", PE, pe_g)
                        sc_evac[idx] = P.run("scalar",
                            lambda e, dc=dc, tt=tt, bank=bank: e.activation(
                                outT[:, dc, n_free * tt:n_free * (tt + 1)],
                                bank[:, 0:n_free], AF.Identity,
                                bias=bias_sb[:, dc:dc + 1], scale=scale), SC, 1)
                return sc_evac[8 * nt - 1]

            P.wait("scalar", VE, ve_bks)
            sc_q = proj_T(c.QT, bq_sb, 1.0, 512, 4, ld_wq, sc_xhT)
            sc_qh = proj_T(QhT, bq_sb, 1.0, 64, 1, ld_wq, sc_q)
            pe_after_q = PE.n

            P.wait("gpsimd", PE, pe_after_q)
            ld_wk = P.run("gpsimd", lambda e: e.dma_start(
                out=wt[:, :, :], in_=c.wkT.rearrange("(cc p) d -> p cc d", p=128)), Wt, 16)

            sc_k = proj_T(c.KsT, bks_sb, SCALE, 512, 4, ld_wk, sc_qh)
            sc_kh = proj_T(KhT, bks_sb, SCALE, 64, 1, ld_wk, sc_k)
            pe_after_k = PE.n

            P.wait("gpsimd", PE, pe_after_k)
            ld_wv = P.run("gpsimd", lambda e: e.dma_start(
                out=wt[:, :, :], in_=c.wvT.rearrange("(cc p) d -> p cc d", p=128)), Wt, 16)

            sc_v = {}
            idx = 0
            for tc in range(NTC):
                for dd in range(2):
                    bank = pk[idx % 2]
                    if idx == 0:
                        P.wait("tensor", Wt, ld_wv)
                        P.wait("tensor", VE, ve_init)
                    if idx >= 2:
                        P.wait("tensor", SC, sc_v[idx - 2])
                    elif idx < 2:
                        P.wait("tensor", SC, sc_kh)
                    for cc in range(8):
                        P.run("tensor", lambda e, tc=tc, dd=dd, cc=cc, bank=bank: e.matmul(
                            bank[:, :],
                            x0T[:, cc, 128 * tc:128 * (tc + 1)],
                            wt[:, cc, 512 * dd:512 * (dd + 1)],
                            start=(cc == 0), stop=False))
                    pe_g = P.run("tensor", lambda e, dd=dd, bank=bank: e.matmul(
                        bank[:, :], c.onesb_sb[0:1, 0:128],
                        bvr_sb[0:1, 512 * dd:512 * (dd + 1)],
                        start=False, stop=True), PE, 1)
                    P.wait("scalar", PE, pe_g)
                    sc_v[idx] = P.run("scalar", lambda e, tc=tc, dd=dd, bank=bank: e.copy(
                        c.Vaug[:, tc, 8 * dd:8 * (dd + 1), 0:HD],
                        bank[:, :].rearrange("p (h n) -> p h n", h=8)), SC, 1)
                    idx += 1

            sc_vh = {}
            for dd in range(2):
                bank = pk[dd % 2]
                P.wait("tensor", SC, sc_v[idx - 2 + dd])
                for cc in range(8):
                    P.run("tensor", lambda e, dd=dd, cc=cc, bank=bank: e.matmul(
                        bank[0:64, :], XhT[:, cc, :],
                        wt[:, cc, 512 * dd:512 * (dd + 1)],
                        start=(cc == 0), stop=False))
                pe_g = P.run("tensor", lambda e, dd=dd, bank=bank: e.matmul(
                    bank[0:64, :], c.onesb_sb[0:1, 0:64],
                    bvr_sb[0:1, 512 * dd:512 * (dd + 1)],
                    start=False, stop=True), PE, 1)
                P.wait("scalar", PE, pe_g)
                sc_vh[dd] = P.run("scalar", lambda e, dd=dd, bank=bank: e.activation(
                    c.Vh_aug[0:64, 8 * dd:8 * (dd + 1), 0:HD],
                    bank[0:64, :].rearrange("p (h n) -> p h n", h=8),
                    AF.Identity, scale=32.0), SC, 1)

            ve_low = {}
            for h in range(NH):
                b = 64 * (h % 2)
                hc = h // 2
                bank = plow[h % 2]
                if h == 0:
                    P.wait("tensor", SC, max(sc_qh, sc_kh))
                if h >= 2:
                    P.wait("tensor", VE, ve_low[h - 2])
                pe_l = P.run("tensor", lambda e, b=b, hc=hc, bank=bank: e.matmul(
                    bank[0:64, 0:64],
                    QhT[b:b + 64, hc, :], KhT[b:b + 64, hc, :],
                    start=True, stop=True), PE, 1)
                P.wait("vector", PE, pe_l)
                ve_rm = P.run("vector", lambda e, h=h, bank=bank: e.reduce_max(
                    c.rm10k[0:64, h:h + 1], bank[0:64, 0:64], axis=AX.X), VE, 1)
                P.wait("vector", VE, ve_rm)
                P.run("vector", lambda e, h=h, bank=bank: e.scalar_tensor_tensor(
                    c.norm_all[0:64, h, :], bank[0:64, 0:64],
                    c.rm10k[0:64, h:h + 1], c.band_sb[:, :],
                    ALU.subtract, ALU.add), VE, 1)
                ve_low[h] = P.run("vector", lambda e, h=h: e.tensor_scalar_add(
                    c.rm10k[0:64, h:h + 1], c.rm10k[0:64, h:h + 1], -10000.0), VE, 1)

            P.wait("sync", VE, ve_low[NH - 1])
            c.ld_norm_scr = P.run("sync", lambda e: e.dma_start(
                out=c.norm_scr.rearrange("h (q k) -> q h k", q=64),
                in_=c.norm_all[:, :, :]), S, 16)

            _tap_phase(c, P, "A")
            P.emit("phaseA")


def _phase_b(c, P, scope):
    nc = c.nc
    bo = scope.enter_context
    c.ebig_sb = bo(nc.sbuf_tensor("ebig_sb", [128, 2048], BF16))
    c.M01cT = bo(nc.sbuf_tensor("M01cT", [128, NH, 64], BF16))
    c.lowWT = bo(nc.sbuf_tensor("lowWT", [64, NH, 64], BF16))
    c.nfl = nfl = bo(nc.sbuf_tensor("nfl", [128, 512], F32))
    tmpf = bo(nc.sbuf_tensor("tmpf", [128, 512], F32))
    part = bo(nc.sbuf_tensor("part", [128, 1], F32))
    c.lo = lo = bo(nc.sbuf_tensor("lo", [128, 1], F32))
    hi = bo(nc.sbuf_tensor("hi", [128, 1], F32))
    mid = bo(nc.sbuf_tensor("mid", [128, 1], F32))
    bp = bo(nc.sbuf_tensor("bp", [128, 1], F32))
    bq = bo(nc.sbuf_tensor("bq", [128, 1], F32))
    bt = bo(nc.sbuf_tensor("bt", [128, 4], F32))
    c.m01fl = m01fl = bo(nc.sbuf_tensor("m01fl", [128, 512], F32))
    c.m01c_sb = m01c = bo(nc.sbuf_tensor("m01c", [64, NH, 64], F32))
    t64 = bo(nc.sbuf_tensor("t64", [64, 64], F32))
    lwf = bo(nc.sbuf_tensor("lwf", [64, 64], F32))
    c.PT = PT = [[bo(nc.sbuf_tensor(f"PT{i}_{j}", [128, 512], BF16)) for j in range(2)] for i in range(2)]
    c.attnT_sb = attnT = [bo(nc.sbuf_tensor(f"attnT{i}", [65, 1024], F32)) for i in range(2)]
    dden = bo(nc.sbuf_tensor("dden", [128, 2], F32))

    with ExitStack() as ps:
        po = ps.enter_context
        sb = [po(nc.psum_tensor(f"pS{i}", [128, 1024], F32)) for i in range(2)]
        sb2 = [[sb[ho][:, 0:512], sb[ho][:, 512:1024]] for ho in range(2)]
        av = [po(nc.psum_tensor(f"pV{i}", [128, 512], F32)) for i in range(4)]

        S, PE, SC, VE = c.sLoad, c.sPE, c.sSC, c.sVE

        P.wait("sync", c.sST, c.sST.n)
        P.wait("gpsimd", c.sST, c.sST.n)
        ld_ebig = P.run("sync", lambda e: e.dma_start(
            out=c.ebig_sb[:, :], in_=c.ebig), S, 16)

        P.wait("sync", S, max(c.ld_norm_scr, ld_ebig))
        ld_flat = P.run("sync", lambda e: e.dma_start(
            out=nfl[:, :], in_=c.norm_scr.rearrange("h (j f) -> (h j) f", j=8)), S, 16)

        P.wait("vector", S, ld_flat)
        P.run("vector", lambda e: e.memset(lo[:, :], -30.0))
        P.run("vector", lambda e: e.memset(hi[:, :], 1.0))
        P.run("vector", lambda e: e.memset(mid[:, :], -14.5), VE, 1)
        ve_upd = VE.n
        for it in range(BIS_IT):
            P.wait("vector", VE, ve_upd)
            P.run("vector", lambda e: e.tensor_scalar(
                tmpf[:, :], nfl[:, :], mid[:, 0:1], None, ALU.is_ge))
            ve_cmp = P.run("vector", lambda e: e.reduce_sum(
                part[:, :], tmpf[:, :], axis=AX.X), VE, 1)
            P.wait("tensor", VE, ve_cmp)
            if it > 0:
                P.wait("tensor", VE, ve_upd)
            pe_bis = P.run("tensor", lambda e: e.matmul(
                sb[0][:, 0:1], c.i8_sb[:, :], part[:, 0:1],
                start=True, stop=True), PE, 1)
            P.wait("vector", PE, pe_bis)
            P.run("vector", lambda e: e.tensor_scalar(
                bp[:, 0:1], sb[0][:, 0:1], 1024.0, None, ALU.is_ge))
            P.run("vector", lambda e: e.tensor_scalar(
                bq[:, 0:1], bp[:, 0:1], -1.0, 1.0, ALU.mult, ALU.add))
            P.run("vector", lambda e: e.tensor_mul(bt[:, 0:1], bp[:, 0:1], mid[:, 0:1]))
            P.run("vector", lambda e: e.tensor_mul(bt[:, 1:2], bq[:, 0:1], lo[:, 0:1]))
            P.run("vector", lambda e: e.tensor_mul(bt[:, 2:3], bq[:, 0:1], mid[:, 0:1]))
            P.run("vector", lambda e: e.tensor_mul(bt[:, 3:4], bp[:, 0:1], hi[:, 0:1]))
            P.run("vector", lambda e: e.tensor_add(lo[:, 0:1], bt[:, 0:1], bt[:, 1:2]))
            P.run("vector", lambda e: e.tensor_add(hi[:, 0:1], bt[:, 2:3], bt[:, 3:4]))
            P.run("vector", lambda e: e.tensor_add(mid[:, 0:1], lo[:, 0:1], hi[:, 0:1]))
            ve_upd = P.run("vector", lambda e: e.tensor_scalar_mul(
                mid[:, 0:1], mid[:, 0:1], 0.5), VE, 1)
        P.wait("vector", VE, ve_upd)
        ve_m01 = P.run("vector", lambda e: e.tensor_scalar(
            m01fl[:, :], nfl[:, :], lo[:, 0:1], 1.0, ALU.is_ge, ALU.subtract), VE, 1)

        P.wait("sync", VE, ve_m01)
        st_m = P.run("sync", lambda e: e.dma_start(
            out=c.m01_scr.rearrange("h (j f) -> (h j) f", j=8), in_=m01fl[:, :]), S, 16)
        P.wait("sync", S, st_m)
        ld_m01 = P.run("sync", lambda e: e.dma_start(
            out=m01c[:, :, :], in_=c.m01_scr.rearrange("h (q k) -> q h k", q=64)), S, 16)

        sc_mT = {}
        P.wait("vector", S, ld_m01)
        for h in range(NH):
            ve_a = P.run("vector", lambda e, h=h: e.scalar_tensor_tensor(
                t64[:, :], c.norm_all[0:64, h, :], c.rm10k[0:64, h:h + 1],
                c.band_sb[:, :], ALU.add, ALU.subtract), VE, 1)
            P.wait("vector", VE, ve_a)
            ve_b = P.run("vector", lambda e, h=h: e.scalar_tensor_tensor(
                t64[:, :], m01c[0:64, h, :], -10000.0, t64[:, :],
                ALU.mult, ALU.add), VE, 1)
            P.wait("scalar", VE, ve_b)
            sc_e = P.run("scalar", lambda e: e.activation(
                lwf[:, :], t64[:, :], AF.Exp), SC, 1)
            P.wait("tensor", SC, sc_e)
            P.run("tensor", lambda e: e.transpose(
                sb[0][0:64, 0:64], lwf[:, :], c.identf_sb[0:64, 0:64]))
            pe_t = P.run("tensor", lambda e, h=h: e.transpose(
                sb[1][0:64, 0:64], m01c[0:64, h, :], c.identf_sb[0:64, 0:64]), PE, 1)
            P.wait("scalar", PE, pe_t)
            P.run("scalar", lambda e, h=h: e.copy(
                c.lowWT[0:64, h, :], sb[0][0:64, 0:64]), SC, 1)
            sc_mT[h] = P.run("scalar", lambda e, h=h: e.copy(
                c.M01cT[0:64, h, :], sb[1][0:64, 0:64]), SC, 1)
            P.wait("vector", PE, pe_t)

        P.wait("sync", SC, sc_mT[NH - 1])
        ld_dup = P.run("sync", lambda e: e.dma_start(
            out=c.M01cT[64:128, :, :], in_=c.M01cT[0:64, :, :]), S, 16)

        sc_pt = {}
        pe_av = {}
        pe_sm = {}
        dv = {}
        tail_ae = {}

        def _emit_av(g, qp, kc):
            for ho in range(2):
                h = 2 * g + ho
                P.wait("tensor", SC, sc_pt[(g, qp, kc, ho)])
                pe_av[(g, qp, kc, ho)] = P.run("tensor",
                    lambda e, h=h, kc=kc, qp=qp, ho=ho: e.matmul(
                        av[2 * (qp % 2) + ho][0:65, :],
                        c.Vaug[:, kc, h, :],
                        PT[ho][kc % 2][:, :],
                        start=False, stop=(kc == NTC - 1)),
                    PE, 1)

        def _tail_piece(g, qp, i):
            at = attnT[qp % 2]
            if i == 0:
                P.wait("scalar", PE, pe_av[(g, qp, NTC - 1, 1)])
                for ho in range(2):
                    tail_ae[(g, qp)] = P.run("scalar",
                        lambda e, qp=qp, ho=ho, at=at: e.copy(
                            at[0:65, 512 * ho:512 * (ho + 1)],
                            av[2 * (qp % 2) + ho][0:65, :]), SC, 1)
                P.wait("tensor", SC, tail_ae[(g, qp)])
            ho = i // 4
            t = 4 * qp + (i % 4)
            h = 2 * g + ho
            bank = av[2 * (qp % 2) + i % 2]
            if i >= 2:
                P.wait("tensor", VE, dv[(g, qp, i - 2)])
            pe_tr = P.run("tensor", lambda e, i=i, bank=bank, at=at: e.transpose(
                bank[:, 0:65], at[0:65, 128 * i:128 * (i + 1)],
                c.identf_sb[0:65, 0:65]), PE, 1)
            P.wait("vector", PE, pe_tr)
            P.run("vector", lambda e, bank=bank: e.tensor_scalar_add(
                dden[:, 0:1], bank[:, 64:65], 1e-6))
            ve_rc = P.run("vector", lambda e: e.reciprocal(
                dden[:, 1:2], dden[:, 0:1]), VE, 1)
            P.wait("vector", VE, ve_rc)
            dv[(g, qp, i)] = P.run("vector",
                lambda e, t=t, h=h, bank=bank: e.tensor_scalar_mul(
                    c.ACC[:, 1024 * t + 64 * h:1024 * t + 64 * h + 64],
                    bank[:, 0:64], dden[:, 1:2]), VE, 1)

        pending = None
        for g in range(NH // 2):
            hc = g
            for qp in range(4):
                P.wait("tensor", SC, max(sc_mT[2 * g], sc_mT[2 * g + 1]))
                if g == 0 and qp == 0:
                    P.wait("tensor", S, ld_dup)
                prev_pass = (g, qp - 2) if qp >= 2 else (
                    (g - 1, qp + 2) if g > 0 else None)
                for ho in range(2):
                    h = 2 * g + ho
                    if prev_pass is not None:
                        P.wait("tensor", VE, dv[prev_pass + (6 + ho,)])
                    P.run("tensor", lambda e, h=h, qp=qp, ho=ho: e.matmul(
                        av[2 * (qp % 2) + ho][0:65, :],
                        c.Vh_aug[0:64, h, :],
                        c.lowWT[0:64, h, 16 * qp:16 * (qp + 1)]
                        .rearrange("p (f o) -> p f o", o=1).to_broadcast([64, 16, 32]),
                        start=True, stop=False))
                for kc in range(NTC):
                    key_prev = (g, qp, kc - 2) if kc >= 2 else (
                        (g, qp - 1, NTC - 2 + kc) if qp > 0 else
                        ((g - 1, 3, NTC - 2 + kc) if g > 0 else None))
                    if key_prev is not None:
                        P.wait("tensor", SC, max(sc_pt[key_prev + (0,)],
                                                 sc_pt[key_prev + (1,)]))
                    for ho in range(2):
                        b = 64 * ho
                        P.run("tensor", lambda e, b=b, hc=hc, kc=kc, qp=qp, ho=ho: e.matmul(
                            sb2[ho][kc % 2][:, :],
                            c.KsT[b:b + 64, hc, 128 * kc:128 * (kc + 1)],
                            c.QT[b:b + 64, hc, 512 * qp:512 * (qp + 1)],
                            start=True, stop=False))
                    for ho in range(2):
                        b = 64 * ho
                        h = 2 * g + ho
                        pe_sm[(g, qp, kc, ho)] = P.run("tensor",
                            lambda e, b=b, h=h, kc=kc, qp=qp, ho=ho: e.matmul(
                                sb2[ho][kc % 2][:, :],
                                c.ebig_sb[b:b + 64, 128 * kc:128 * (kc + 1)],
                                c.M01cT[b:b + 64, h, 16 * qp:16 * (qp + 1)]
                                .rearrange("p (f o) -> p f o", o=1).to_broadcast([64, 16, 32]),
                                start=False, stop=True), PE, 1)
                    for ho in range(2):
                        P.wait("scalar", PE, pe_sm[(g, qp, kc, ho)])
                        keyv = key_prev
                        if keyv is not None and (keyv + (ho,)) in pe_av:
                            P.wait("scalar", PE, pe_av[keyv + (ho,)])
                        sc_pt[(g, qp, kc, ho)] = P.run(
                            "scalar", lambda e, kc=kc, ho=ho: e.activation(
                                PT[ho][kc % 2][:, :], sb2[ho][kc % 2][:, :],
                                AF.Exp), SC, 1)
                    if kc >= 1:
                        _emit_av(g, qp, kc - 1)
                    if pending is not None and kc % 2 == 1:
                        _tail_piece(pending[0], pending[1], kc // 2)
                _emit_av(g, qp, NTC - 1)
                pending = (g, qp)
        for i in range(8):
            _tail_piece(NH // 2 - 1, 3, i)

        _tap_phase(c, P, "B")
        P.emit("phaseB")


def _phase_cde(c, P, scope):
    nc = c.nc

    with ExitStack() as ps:
        po = ps.enter_context
        ptr = [po(nc.psum_tensor(f"pT{i}", [128, 512], F32)) for i in range(4)]
        plg = po(nc.psum_tensor("plg", [128, 512], F32))
        pcw = po(nc.psum_tensor("pcw", [128, 512], F32))
        pb2 = [po(nc.psum_tensor(f"pB2_{i}", [128, 512], F32)) for i in range(2)]

        S, PE, SC, VE = c.sLoad, c.sPE, c.sSC, c.sVE
        G, sWp = c.sG, [c.sWp0, c.sWp1]

        def ln_stats(tc, col0):
            st = c.stats
            ve_m = P.run("vector", lambda e, tc=tc, col0=col0: e.reduce_sum(
                st[:, col0:col0 + 1], c.ACC[:, 1024 * tc:1024 * (tc + 1)],
                axis=AX.X), VE, 1)
            P.wait("scalar", VE, ve_m)
            sc_sq = P.run("scalar", lambda e, tc=tc, col0=col0: e.activation(
                c.sdump[:, :], c.ACC[:, 1024 * tc:1024 * (tc + 1)], AF.Square,
                accum_out=st[:, col0 + 1:col0 + 2]), SC, 1)
            P.wait("vector", SC, sc_sq)
            P.run("vector", lambda e, col0=col0: e.tensor_scalar_mul(
                st[:, col0 + 2:col0 + 3], st[:, col0:col0 + 1], 1.0 / 1024.0))
            P.run("vector", lambda e, col0=col0: e.tensor_mul(
                st[:, col0 + 3:col0 + 4], st[:, col0 + 2:col0 + 3], st[:, col0 + 2:col0 + 3]))
            P.run("vector", lambda e, col0=col0: e.scalar_tensor_tensor(
                st[:, col0 + 4:col0 + 5], st[:, col0 + 1:col0 + 2], 1.0 / 1024.0,
                st[:, col0 + 3:col0 + 4], ALU.mult, ALU.subtract))
            ve_v = P.run("vector", lambda e, col0=col0: e.tensor_scalar_add(
                st[:, col0 + 4:col0 + 5], st[:, col0 + 4:col0 + 5], 1e-5), VE, 1)
            P.wait("scalar", VE, ve_v)
            P.run("scalar", lambda e, col0=col0: e.activation(
                st[:, col0 + 5:col0 + 6], st[:, col0 + 4:col0 + 5], AF.Ln))
            sc_r = P.run("scalar", lambda e, col0=col0: e.activation(
                st[:, col0 + 6:col0 + 7], st[:, col0 + 5:col0 + 6], AF.Exp,
                scale=-0.5), SC, 1)
            P.wait("vector", SC, sc_r)
            ve_nm = P.run("vector", lambda e, col0=col0: e.scalar_tensor_tensor(
                st[:, col0 + 7:col0 + 8], st[:, col0 + 2:col0 + 3], -1.0,
                st[:, col0 + 6:col0 + 7], ALU.mult, ALU.mult), VE, 1)
            P.wait("vector", VE, ve_nm)
            return st[:, col0 + 6:col0 + 7], st[:, col0 + 7:col0 + 8]

        P.wait("sync", c.sST, c.sST.n)
        P.wait("gpsimd", c.sST, c.sST.n)
        P.run("gpsimd", lambda e: e.dma_start(
            out=c.wg_sb[:, 0:128].rearrange("p (cc e) -> p cc e", cc=8),
            in_=c.wg_aug[0:1024, :].rearrange("(cc p) e -> p cc e", p=128)), G, 16)
        P.run("gpsimd", lambda e: e.dma_start(
            out=c.wg_sb[0:2, 128:144], in_=c.wg_aug[1024:1026, :]), G, 16)
        c.ld_g = P.run("gpsimd", lambda e: e.dma_start(
            out=c.b2a_sb[:, :], in_=c.b2), G, 16)
        c.wld = {}
        for s in range(2):
            c.wld[(0, s, "w1")] = P.run("gpsimd", lambda e, s=s: e.dma_start(
                out=c.w1b[s][:, :, :],
                in_=c.w1[0, :, 1024 * s:1024 * (s + 1)]
                .rearrange("(cc p) hh -> p cc hh", p=128)), sWp[s], 16)
            c.wld[(0, s, "w2")] = P.run("gpsimd", lambda e, s=s: e.dma_start(
                out=c.w2b[s][:, :, :],
                in_=c.w2[0, 1024 * s:1024 * (s + 1), :]
                .rearrange("(cc p) dd -> p cc dd", p=128)), sWp[s], 16)
        c.ld_b1 = {0: P.run("sync", lambda e: e.dma_start(
            out=c.b1_sb[:, 0:16], in_=c.b1r[0, :, :]), c.sB1, 16)}

        ld2, ve_res, sc_x1T, pe_x1t, ve_b2, ve_cw = {}, {}, {}, {}, {}, {}
        for tc in range(NTC):
            col0 = 8 * (tc % 2)
            if tc >= 1:
                P.wait("sync", VE, ve_res[tc - 1])
            ld2[tc] = P.run("sync", lambda e, tc=tc: e.dma_start(
                out=c.x0t2[:, :], in_=c.x[128 * tc:128 * (tc + 1), :]), S, 16)
            P.wait("vector", S, ld2[tc])
            ve_res[tc] = P.run("vector", lambda e, tc=tc: e.tensor_add(
                c.ACC[:, 1024 * tc:1024 * (tc + 1)],
                c.ACC[:, 1024 * tc:1024 * (tc + 1)], c.x0t2[:, :]), VE, 1)
            rstd, negmr = ln_stats(tc, col0)
            ve_x1 = P.run("vector", lambda e, tc=tc, rstd=rstd, negmr=negmr: e.tensor_scalar(
                c.ACC[:, 1024 * tc:1024 * (tc + 1)],
                c.ACC[:, 1024 * tc:1024 * (tc + 1)],
                rstd, negmr, ALU.mult, ALU.add), VE, 1)

            P.wait("tensor", VE, ve_x1)
            if tc >= 2:
                P.wait("tensor", SC, sc_x1T[tc - 2])
            for dc in range(8):
                bank = ptr[2 * (tc % 2) + dc // 4]
                pe_x1t[tc] = P.run("tensor", lambda e, tc=tc, dc=dc, bank=bank: e.transpose(
                    bank[:, 128 * (dc % 4):128 * (dc % 4 + 1)],
                    c.ACC[:, 1024 * tc + 128 * dc:1024 * tc + 128 * (dc + 1)],
                    c.identf_sb[:, :]), PE if dc == 7 else None, 1)
            P.wait("scalar", PE, pe_x1t[tc])
            for half in range(2):
                bank = ptr[2 * (tc % 2) + half]
                sc_x1T[tc] = P.run("scalar", lambda e, tc=tc, half=half, bank=bank: e.copy(
                    c.x1T[:, 4 * half:4 * (half + 1), 128 * tc:128 * (tc + 1)],
                    bank[:, :].rearrange("p (cc t) -> p cc t", cc=4)), SC, 1)

            P.wait("tensor", SC, sc_x1T[tc])
            if tc == 0:
                P.wait("tensor", G, c.ld_g)
            for cc in range(8):
                P.run("tensor", lambda e, tc=tc, cc=cc: e.matmul(
                    plg[:, 0:16],
                    c.x1T[:, cc, 128 * tc:128 * (tc + 1)],
                    c.wg_sb[:, 16 * cc:16 * (cc + 1)],
                    start=(cc == 0), stop=False))
            pe_lg = P.run("tensor", lambda e: e.matmul(
                plg[:, 0:16], c.onesb_sb[0:2, 0:128], c.wg_sb[0:2, 128:144],
                start=False, stop=True), PE, 1)
            P.wait("scalar", PE, pe_lg)
            sc_lg = P.run("scalar", lambda e, tc=tc: e.copy(
                c.lg_sb[:, 16 * (tc % 2):16 * (tc % 2) + 16], plg[:, 0:16]), SC, 1)

            P.wait("vector", SC, sc_lg)
            lg = c.lg_sb[:, 16 * (tc % 2):16 * (tc % 2) + 16]
            cw = c.cw_sb[:, 16 * tc:16 * (tc + 1)]
            t16 = c.t16
            ve_mx = P.run("vector", lambda e, lg=lg: e.max(
                out=c.top8[:, :], in_=lg), VE, 1)
            ve_d = P.run("vector", lambda e: e.tensor_sub(
                c.stats[:, 16:17], c.top8[:, 1:2], c.top8[:, 0:1]), VE, 1)
            P.wait("scalar", VE, ve_d)
            sc_e2 = P.run("scalar", lambda e: e.activation(
                c.stats[:, 17:18], c.stats[:, 16:17], AF.Exp), SC, 1)
            P.wait("vector", SC, sc_e2)
            P.run("vector", lambda e: e.tensor_scalar_add(
                c.stats[:, 18:19], c.stats[:, 17:18], 1.0))
            ve_rc = P.run("vector", lambda e: e.reciprocal(
                c.stats[:, 19:20], c.stats[:, 18:19]), VE, 1)
            ve_w2 = P.run("vector", lambda e: e.tensor_mul(
                c.stats[:, 20:21], c.stats[:, 17:18], c.stats[:, 19:20]), VE, 1)
            P.wait("vector", VE, ve_mx)
            P.run("vector", lambda e, lg=lg: e.tensor_scalar(
                t16[:, 0:16], lg, c.top8[:, 0:1], None, ALU.is_equal))
            P.run("vector", lambda e, lg=lg: e.scalar_tensor_tensor(
                t16[:, 16:32], t16[:, 0:16], -1e30, lg, ALU.mult, ALU.add))
            P.run("vector", lambda e: e.tensor_scalar(
                t16[:, 32:48], t16[:, 16:32], c.top8[:, 1:2], None, ALU.is_equal))
            P.wait("vector", VE, ve_w2)
            P.run("vector", lambda e, cw=cw: e.tensor_scalar(
                cw, t16[:, 0:16], c.stats[:, 19:20], None, ALU.mult))
            ve_cw[tc] = P.run("vector", lambda e, cw=cw: e.scalar_tensor_tensor(
                cw, t16[:, 32:48], c.stats[:, 20:21], cw, ALU.mult, ALU.add), VE, 1)

            P.wait("tensor", VE, ve_cw[tc])
            pe_cw = P.run("tensor", lambda e, cw=cw: e.transpose(
                pcw[0:16, 0:128], cw, c.identf_sb[:, :]), PE, 1)
            P.wait("scalar", PE, pe_cw)
            sc_cwT = P.run("scalar", lambda e, tc=tc: e.copy(
                c.cwT_sb[0:16, 128 * tc:128 * (tc + 1)], pcw[0:16, 0:128]), SC, 1)
            P.wait("tensor", SC, sc_cwT)
            if tc == 0:
                P.wait("tensor", G, c.ld_g)
            if tc >= 1:
                P.wait("tensor", VE, ve_b2[tc - 1])
            pe_b2 = None
            for dd in range(2):
                pe_b2 = P.run("tensor", lambda e, tc=tc, dd=dd: e.matmul(
                    pb2[dd][:, :], c.cwT_sb[0:16, 128 * tc:128 * (tc + 1)],
                    c.b2a_sb[0:16, 512 * dd:512 * (dd + 1)],
                    start=True, stop=True), PE if dd == 1 else None, 1)
            P.wait("vector", PE, pe_b2)
            for dd in range(2):
                ve_b2[tc] = P.run("vector", lambda e, tc=tc, dd=dd: e.tensor_add(
                    c.ACC[:, 1024 * tc + 512 * dd:1024 * tc + 512 * (dd + 1)],
                    c.ACC[:, 1024 * tc + 512 * dd:1024 * tc + 512 * (dd + 1)],
                    pb2[dd][:, :]), VE if dd == 1 else None, 1)

        _tap_phase(c, P, "C")
        P.emit("phaseC")

        phb = [plg, pcw]
        ybk = ptr

        sc_relu, pe_h, pe_y, pe_yh, ve_y = {}, {}, {}, {}, {}

        def emit_y(e_i, s, tc, Hs, par, first):
            k = (e_i, s, tc, Hs)
            P.wait("tensor", SC, sc_relu[k])
            if Hs == 0:
                if tc > 0:
                    P.wait("tensor", VE, ve_y[(e_i, s, tc - 1)])
                elif not first:
                    ep, sp = divmod(2 * e_i + s - 1, 2)
                    P.wait("tensor", VE, ve_y[(ep, sp, 7)])
            for tsub in range(2):
                for dd in range(2):
                    m = P.run("tensor",
                        lambda e, par=par, Hs=Hs, tsub=tsub, dd=dd: e.matmul(
                            ybk[2 * tsub + dd][:, :],
                            c.ht_sb[Hs % 3][:, 128 * tsub:128 * (tsub + 1)],
                            c.w2b[par][:, Hs, 512 * dd:512 * (dd + 1)],
                            start=(Hs == 0), stop=(Hs == 7)),
                        PE if (tsub == 1 and dd == 1) else None, 1)
            pe_yh[k] = m
            pe_y[(e_i, s, tc)] = m

        def _hk_prev(e_i, s, tc, Hs, d, first):
            idx = (2 * e_i + s) * 64 + tc * 8 + Hs - d
            if idx < 0:
                return None
            es, r = divmod(idx, 64)
            ep, sp = divmod(es, 2)
            return (ep, sp, r // 8, r % 8)

        for e_i in range(NEXP):
            for s in range(2):
                par = (2 * e_i + s) % 2
                first = (e_i == 0 and s == 0)
                for tc in range(8):
                    for Hs in range(8):
                        Hglob = 8 * s + Hs
                        hb = phb[Hs % 2]
                        k = (e_i, s, tc, Hs)
                        if tc == 0 and Hs == 0:
                            P.wait("tensor", sWp[par], c.wld[(e_i, s, "w2")])
                            P.wait("tensor", c.sB1, c.ld_b1[e_i])
                        kprev = _hk_prev(e_i, s, tc, Hs, 2, e_i == 0 and s == 0)
                        if kprev is not None and kprev in sc_relu:
                            P.wait("tensor", SC, sc_relu[kprev])
                        for cc in range(8):
                            pe_h[k] = P.run("tensor",
                                lambda e, par=par, Hs=Hs, tc=tc, cc=cc, hb=hb: e.matmul(
                                    hb[:, 0:256],
                                    c.w1b[par][:, cc, 128 * Hs:128 * (Hs + 1)],
                                    c.x1T[:, cc, 256 * tc:256 * (tc + 1)],
                                    start=(cc == 0), stop=(cc == 7)),
                                PE if cc == 7 else None, 1)
                        P.wait("scalar", PE, pe_h[k])
                        kht = _hk_prev(e_i, s, tc, Hs, 3, e_i == 0 and s == 0)
                        if kht is not None and kht in pe_yh:
                            P.wait("scalar", PE, pe_yh[kht])
                        sc_relu[k] = P.run("scalar",
                            lambda e, e_i=e_i, Hglob=Hglob, Hs=Hs, hb=hb: e.activation(
                                c.ht_sb[Hs % 3][:, :], hb[:, 0:256], AF.Relu,
                                bias=c.b1_sb[:, 16 * (e_i % 2) + Hglob:
                                             16 * (e_i % 2) + Hglob + 1]), SC, 1)
                        if Hs >= 2:
                            emit_y(e_i, s, tc, Hs - 2, par, first)
                    emit_y(e_i, s, tc, 6, par, first)
                    emit_y(e_i, s, tc, 7, par, first)
                    P.wait("vector", PE, pe_y[(e_i, s, tc)])
                    for tsub in range(2):
                        tcg = 2 * tc + tsub
                        for dd in range(2):
                            ve_y[(e_i, s, tc)] = P.run("vector",
                                lambda e, e_i=e_i, tcg=tcg, tsub=tsub, dd=dd: e.scalar_tensor_tensor(
                                    c.ACC[:, 1024 * tcg + 512 * dd:1024 * tcg + 512 * (dd + 1)],
                                    ybk[2 * tsub + dd][:, :],
                                    c.cw_sb[:, 16 * tcg + e_i:16 * tcg + e_i + 1],
                                    c.ACC[:, 1024 * tcg + 512 * dd:1024 * tcg + 512 * (dd + 1)],
                                    ALU.mult, ALU.add),
                                VE if (tsub == 1 and dd == 1) else None, 1)

                nxt = 2 * e_i + s + 2
                if nxt < 2 * NEXP:
                    en, sn = divmod(nxt, 2)
                    P.wait("gpsimd", PE, pe_y[(e_i, s, 7)])
                    c.wld[(en, sn, "w1")] = P.run("gpsimd",
                        lambda e, en=en, sn=sn, par=par: e.dma_start(
                            out=c.w1b[par][:, :, :],
                            in_=c.w1[en, :, 1024 * sn:1024 * (sn + 1)]
                            .rearrange("(cc p) hh -> p cc hh", p=128)), sWp[par], 16)
                    c.wld[(en, sn, "w2")] = P.run("gpsimd",
                        lambda e, en=en, sn=sn, par=par: e.dma_start(
                            out=c.w2b[par][:, :, :],
                            in_=c.w2[en, 1024 * sn:1024 * (sn + 1), :]
                            .rearrange("(cc p) dd -> p cc dd", p=128)), sWp[par], 16)
                    if sn == 0:
                        if en >= 2:
                            P.wait("sync", SC, sc_relu[(en - 2, 1, 7, 7)])
                        P.wait("sync", c.sB1, c.ld_b1[en - 1])
                        c.ld_b1[en] = P.run("sync", lambda e, en=en: e.dma_start(
                            out=c.b1_sb[:, 16 * (en % 2):16 * (en % 2) + 16],
                            in_=c.b1r[en, :, :]), c.sB1, 16)

        _tap_phase(c, P, "D")
        P.emit("phaseD")

        for tc in range(NTC):
            col0 = 8 * (tc % 2)
            rstd, negmr = ln_stats(tc, col0)
            ve_o = P.run("vector", lambda e, tc=tc, rstd=rstd, negmr=negmr: e.tensor_scalar(
                c.ACC[:, 1024 * tc:1024 * (tc + 1)],
                c.ACC[:, 1024 * tc:1024 * (tc + 1)],
                rstd, negmr, ALU.mult, ALU.add), VE, 1)
            P.wait("sync", VE, ve_o)
            P.run("sync", lambda e, tc=tc: e.dma_start(
                out=c.y[128 * tc:128 * (tc + 1), :],
                in_=c.ACC[:, 1024 * tc:1024 * (tc + 1)]), c.sST, 16)
        _tap_phase(c, P, "E")
        P.wait("sync", c.sST, c.sST.n)
        P.emit("phaseE")


def kernel(**inputs):
    X = np.asarray(inputs["L0_attn_X"], np.float32)
    Wq = np.asarray(inputs["L0_attn_Wq"], np.float32)
    bq = np.asarray(inputs["L0_attn_bq"], np.float32)
    Wk = np.asarray(inputs["L0_attn_Wk"], np.float32)
    bk = np.asarray(inputs["L0_attn_bk"], np.float32)
    Wv = np.asarray(inputs["L0_attn_Wv"], np.float32)
    bv = np.asarray(inputs["L0_attn_bv"], np.float32)
    Wg = np.asarray(inputs["L0_ffn_Wg"], np.float32)
    bg = np.asarray(inputs["L0_ffn_bg"], np.float32)
    bias = np.asarray(inputs["L0_ffn_bias"], np.float32)
    W1 = np.asarray(inputs["L0_ffn_W1"], np.float32)
    b1 = np.asarray(inputs["L0_ffn_b1"], np.float32)
    W2 = np.asarray(inputs["L0_ffn_W2"], np.float32)
    b2 = np.asarray(inputs["L0_ffn_b2"], np.float32)

    cst = _consts()
    common = dict(
        wqT=np.ascontiguousarray(Wq.T),
        wkT=np.ascontiguousarray(Wk.T),
        wvT=np.ascontiguousarray(Wv.T),
        bq_rs=np.ascontiguousarray(bq.reshape(8, 128).T),
        bk_rs=np.ascontiguousarray(bk.reshape(8, 128).T),
        bvrow=np.ascontiguousarray(bv.reshape(1, D)),
        wg_aug=np.ascontiguousarray(
            np.concatenate([Wg, bg[None, :], bias[None, :]], 0)),
        w1=W1, w2=W2, b2=b2,
        b1r=np.ascontiguousarray(b1.reshape(NEXP, 16, 128).transpose(0, 2, 1)),
        **cst,
    )
    nc, _ = build_program()
    in_maps = [dict(common, x=np.ascontiguousarray(X[b])) for b in range(8)]
    res = run_bass_kernel_spmd(nc, in_maps, core_ids=list(range(8)))
    return np.stack([r["y"] for r in res.results], 0)


if __name__ == "__main__":
    import reference as ref
    ins = {k: np.asarray(v) for k, v in ref.setup_inputs(0).items()}
    out = kernel(**ins)
    print("out", out.shape, out.dtype)
